# version 02
# baseline (speedup 1.0000x reference)
import numpy as np
import ml_dtypes
import concourse.bass as bass
import concourse.mybir as mybir
from concourse.bass_utils import run_bass_kernel_spmd

F32 = mybir.dt.float32
BF16 = mybir.dt.bfloat16
AF = mybir.ActivationFunctionType
ALU = mybir.AluOpType
AX = mybir.AxisListType

D = 2048
T = 1024
NC = 16
NH = 16
HD = 128
S = 4096
NE = 16
NPE = 18
FMOE = 1024
DFF = 8192
EPS_LN = 1e-5
INV_SQRT_HD = 1.0 / float(np.sqrt(128.0))

ALL_PHASES = ("ln", "moe", "ln", "attn0", "ln", "moe",
              "prep1", "ln", "ffn1", "ln", "attn1", "ln", "ffn1")

GROUPS = [ALL_PHASES]



def _blk_lhsT(WT):
    K, M = WT.shape
    a = WT.reshape(K // 128, 128, M // 128, 128)
    return np.ascontiguousarray(a.transpose(2, 1, 0, 3).reshape(M // 128, 128, K))


def _prep(inp):
    g = {}
    f32 = lambda a: np.ascontiguousarray(np.asarray(a), dtype=np.float32)

    x = f32(inp["L0_attn_x"])
    g["x_cores"] = [np.ascontiguousarray(x[c // 4, (c % 4) * T:(c % 4 + 1) * T, :].T)
                    for c in range(8)]

    for nm in ("Wq", "Wk", "Wv", "Wo"):
        W = f32(inp[f"L0_attn_{nm}"])
        g[f"l0_{nm.lower()}"] = _blk_lhsT(np.ascontiguousarray(W.T))
    for nm in ("bq", "bk", "bv", "bo"):
        g[f"l0_{nm}"] = f32(inp[f"L0_attn_{nm}"]).reshape(1, D)

    gw = f32(inp["L0_ffn_gate_weight"])
    g["gwt"] = np.ascontiguousarray(gw.T.reshape(NC, 128, NE))
    g["e_bias"] = f32(inp["L0_ffn_e_score_correction_bias"]).reshape(1, NE)
    gate_w = f32(inp["L0_ffn_gate_w"])
    up_w = f32(inp["L0_ffn_up_w"])
    down_w = f32(inp["L0_ffn_down_w"])
    shg = f32(inp["L0_ffn_sh_gate_w"])
    shu = f32(inp["L0_ffn_sh_up_w"])
    shd = f32(inp["L0_ffn_sh_down_w"])
    gu_units = np.empty((NPE, 8, 128, 4096), np.float32)
    for e in range(NPE):
        if e < NE:
            gmat, umat = gate_w[e], up_w[e]
        else:
            lo = (e - NE) * FMOE
            gmat, umat = shg[lo:lo + FMOE], shu[lo:lo + FMOE]
        gb = _blk_lhsT(np.ascontiguousarray(gmat.T))
        ub = _blk_lhsT(np.ascontiguousarray(umat.T))
        gu_units[e, :, :, :2048] = gb
        gu_units[e, :, :, 2048:] = ub
    g["l0_gu"] = gu_units.reshape(NPE * 8, 128, 4096)

    groups = [[0, 1, 2, 3], [4, 5, 6, 7], [8, 9, 10, 11], [12, 13, 14, 15], [16, 17]]
    dmats = [down_w[e].T if e < NE else
             (shd[:, (e - NE) * FMOE:(e - NE + 1) * FMOE]).T for e in range(NPE)]
    dblk = [_blk_lhsT(np.ascontiguousarray(m)) for m in dmats]
    full = np.empty((4 * NC, 128, 4096), np.float32)
    for gi in range(4):
        for hc in range(NC):
            full[gi * NC + hc] = np.concatenate(
                [dblk[e][hc] for e in groups[gi]], axis=1)
    g["l0_down"] = full
    shared = np.empty((NC, 128, 2048), np.float32)
    for hc in range(NC):
        shared[hc] = np.concatenate([dblk[16][hc], dblk[17][hc]], axis=1)
    g["l0_downsh"] = shared

    for nm in ("Wq", "Wk", "Wv", "Wo"):
        W = f32(inp[f"L1_attn_{nm}"])
        g[f"l1_{nm.lower()}"] = _blk_lhsT(np.ascontiguousarray(W.T))
    for nm in ("bq", "bk", "bv", "bo"):
        g[f"l1_{nm}"] = f32(inp[f"L1_attn_{nm}"]).reshape(1, D)

    w1 = f32(inp["L1_ffn_w1"])
    w2 = f32(inp["L1_ffn_w2"])
    g["l1_w1"] = _blk_lhsT(np.ascontiguousarray(w1.T))
    g["l1_w2"] = _blk_lhsT(np.ascontiguousarray(w2.T))
    b1 = f32(inp["L1_ffn_b1"])
    g["b1col"] = np.ascontiguousarray(b1.reshape(DFF // 128, 128).T)
    b2 = f32(inp["L1_ffn_b2"])
    g["b2col"] = np.ascontiguousarray(b2.reshape(NC, 128).T)

    g["ones_row"] = np.ones((1, 512), np.float32)
    g["onescol"] = np.ones((128, 1), np.float32)
    g["ident"] = np.eye(128, dtype=np.float32)
    sel = np.zeros((NE, NE, 128), np.float32)
    for e in range(NE):
        sel[e, e, :] = 1.0
    g["sel"] = sel
    pos = np.arange(S, dtype=np.float64)
    decay = np.exp(-pos / float(S)).astype(np.float32)
    g["decay_pp"] = [np.ascontiguousarray(
        decay[(c % 4) * T:(c % 4 + 1) * T].reshape(8, 128).T) for c in range(8)]
    g["decay_row"] = [np.ascontiguousarray(
        decay[(c % 4) * T:(c % 4 + 1) * T].reshape(1, T)) for c in range(8)]
    return g



ENGINES = ("tensor", "vector", "scalar", "gpsimd", "sync")


class Sem:
    __slots__ = ("h", "v")

    def __init__(self, nc, name):
        self.h = nc.alloc_semaphore(name)
        self.v = 0


class RecInst:
    __slots__ = ("item",)

    def __init__(self, item):
        self.item = item

    def inc(self, sem, n=1):
        sem.v += n
        self.item[3].append((sem.h, n))
        return self


class Eng:

    def __init__(self, rec, name):
        self.rec = rec
        self.name = name

    def __getattr__(self, meth):
        def call(*a, **k):
            item = [meth, a, k, []]
            self.rec.q[self.name].append(item)
            self.rec.last[self.name] = item
            return RecInst(item)
        return call

    def wait(self, sem):
        if sem.v > 0:
            self.rec.q[self.name].append(["wait_ge", (sem.h, sem.v), {}, []])

    def wait_val(self, sem, v):
        self.rec.q[self.name].append(["wait_ge", (sem.h, v), {}, []])


class Rec:
    def __init__(self, nc):
        self.nc = nc
        self.q = {e: [] for e in ENGINES}
        self.last = {e: None for e in ENGINES}
        self.T = Eng(self, "tensor")
        self.V = Eng(self, "vector")
        self.S = Eng(self, "scalar")
        self.G = Eng(self, "gpsimd")
        self.Y = Eng(self, "sync")
        self._sems = {}
        self.bar = Sem(nc, "bar")
        self._bar_last = {e: None for e in ENGINES}

    def sem(self, name):
        s = self._sems.get(name)
        if s is None:
            s = self._sems[name] = Sem(self.nc, name)
        return s

    def rlast(self, eng):
        return RecInst(self.last[eng])

    def barrier(self, engines=("tensor", "vector", "scalar")):
        n = 0
        for e in engines:
            it = self.last[e]
            if it is not None and it is not self._bar_last[e]:
                RecInst(it).inc(self.bar, 1)
                self._bar_last[e] = it
                n += 1
        if n:
            for e in engines:
                self.q[e].append(["wait_ge", (self.bar.h, self.bar.v), {}, []])

    def replay(self):
        with self.nc.Block() as blk:
            for name in ENGINES:
                qq = self.q[name]

                def run(eng, qq=qq, name=name):
                    for meth, a, k, incs in qq:
                        inst = getattr(eng, meth)(*a, **k)
                        if not incs:
                            continue
                        if meth in ("dma_start", "collective_compute"):
                            assert len(incs) == 1, (meth, incs)
                            inst.then_inc(*incs[0])
                            continue
                        inst.then_inc(*incs[0])
                        for (h, n) in incs[1:]:
                            eng.drain().then_inc(h, n)
                getattr(blk, name)(run)


class BufUse:

    __slots__ = ("ap", "loaded", "wait_v", "freed", "slots_span", "stream")

    def __init__(self, ap, loaded, wait_v, freed, slots_span, stream):
        self.ap = ap
        self.loaded = loaded
        self.wait_v = wait_v
        self.freed = freed
        self.slots_span = slots_span
        self.stream = stream

    def acquire(self, eng):
        eng.wait_val(self.loaded, self.wait_v)

    def release(self, inst):
        inst.inc(self.freed, 1)
        mark = (self.freed, self.freed.v)
        for s in self.slots_span:
            s.release_at = mark


class Slot:
    def __init__(self, rec, idx):
        self.idx = idx
        self.loaded = {"gpsimd": rec.sem(f"wb_ldg{idx}"),
                       "sync": rec.sem(f"wb_ldy{idx}")}
        self.freed = rec.sem(f"wb_fr{idx}")
        self.release_at = None


class Stream:

    def __init__(self, rec, wbuf, slots, ring, slot_cols=2048):
        self.rec = rec
        self.wbuf = wbuf
        self.slots = slots
        self.ring = ring
        self.slot_cols = slot_cols
        self.i = 0

    def load(self, src_ap, ncols, queue="gpsimd", ndma=1):
        first, span = self.ring[self.i % len(self.ring)]
        self.i += 1
        sl = self.slots[first]
        eng = self.rec.G if queue == "gpsimd" else self.rec.Y
        spanned = self.slots[first:first + span]
        for s in spanned:
            r = s.release_at
            if r is not None:
                eng.wait_val(r[0], r[1])
        dst = self.wbuf[:, first * self.slot_cols: first * self.slot_cols + ncols]
        ld = sl.loaded[queue]
        eng.dma_start(out=dst, in_=src_ap).inc(ld, 16)
        return BufUse(dst, ld, ld.v, sl.freed, spanned, self)



class K:
    pass


def _alias(nc, base_handle, name, shape, dtype, byte_off):
    addr = nc.lookup_mloc(base_handle).addr + byte_off
    return nc.alloc_sbuf_tensor_at(name, shape, dtype, offset=addr)


def build_program(phases, dbg=None):
    nc = bass.Bass("TRN2", target_bir_lowering=False)
    k = K()
    k.nc = nc
    k.phases = phases
    r = k.r = Rec(nc)
    k.dbg = dbg

    k.d_x = nc.dram_tensor("x", [D, T], F32, kind="ExternalInput").ap()
    k.d_y = nc.dram_tensor("y", [D, T], F32, kind="ExternalOutput").ap()
    k.innames = []

    def din(name, shape):
        k.innames.append(name)
        return nc.dram_tensor(name, list(shape), F32, kind="ExternalInput").ap()

    k.d_ones = din("ones_row", (1, 512))
    k.d_onescol = din("onescol", (128, 1))
    k.d_ident = din("ident", (128, 128))
    need = set(phases)
    if "moe" in need:
        k.d_gwt = din("gwt", (NC, 128, NE))
        k.d_ebias = din("e_bias", (1, NE))
        k.d_gu = din("l0_gu", (NPE * 8, 128, 4096))
        k.d_down = din("l0_down", (4 * NC, 128, 4096))
        k.d_downsh = din("l0_downsh", (NC, 128, 2048))
        k.d_sel = din("sel", (NE, NE, 128))
    if "attn0" in need:
        for nm in ("wq", "wk", "wv", "wo"):
            setattr(k, f"d_l0_{nm}", din(f"l0_{nm}", (NC, 128, D)))
        for nm in ("bq", "bk", "bv", "bo"):
            setattr(k, f"d_l0_{nm}", din(f"l0_{nm}", (1, D)))
        k.d_l0k_in = nc.dram_tensor("l0k_in", [2049, T], BF16, kind="Internal")
        k.d_l0k_out = nc.dram_tensor("l0k_out", [4 * 2049, T], BF16, kind="Internal")
        k.d_l0v_in = nc.dram_tensor("l0v_in", [T, D], BF16, kind="Internal")
        k.d_l0v_out = nc.dram_tensor("l0v_out", [S, D], BF16, kind="Internal")
    if "attn1" in need:
        for nm in ("wq", "wk", "wv", "wo"):
            setattr(k, f"d_l1_{nm}", din(f"l1_{nm}", (NC, 128, D)))
        for nm in ("bq", "bk", "bv", "bo"):
            setattr(k, f"d_l1_{nm}", din(f"l1_{nm}", (1, D)))
        k.d_decay_pp = din("decay_pp", (128, 8))
        k.d_decay_row = din("decay_row", (1, T))
        k.d_l1cc_in = nc.dram_tensor("l1cc_in", [128, 2064], F32, kind="Internal")
        k.d_l1cc_out = nc.dram_tensor("l1cc_out", [128, 2064], F32, kind="Internal")
    if "prep1" in need or "ffn1" in need:
        k.d_w1 = din("l1_w1", (DFF // 128, 128, D))
        k.d_w2 = din("l1_w2", (NC, 128, DFF))
        k.d_b1col = din("b1col", (128, 64))
        k.d_b2col = din("b2col", (128, NC))
        k.d_w1q = nc.dram_tensor("w1q", [DFF // 128, 128, D], BF16, kind="Internal")
        k.d_w2q = nc.dram_tensor("w2q", [NC, 128, DFF], BF16, kind="Internal")

    a = nc.alloc_sbuf_tensor
    k.X = a("Xr", [128, 16384], F32)[:, :]
    k.LNX = a("LNXr", [128, 16384], BF16)[:, :]
    k.ACTh = a("ACTr", [128, 32768], BF16)
    k.ACT = k.ACTh[:, :]
    k.WBUFh = a("WBUFr", [128, 12288], BF16)
    k.WBUF = k.WBUFh[:, :]
    k.SMFh = a("SMFr", [128, 2048], F32)
    k.SMF = k.SMFh[:, :]
    k.SMBh = a("SMBr", [128, 4096], BF16)
    k.SMB = k.SMBh[:, :]
    k.SMT = a("SMTr", [128, 360], F32)[:, :]
    k.ONES_BF = a("ONESBF", [1, 512], BF16)[:, :]
    k.ONES_F = a("ONESF", [1, 128], F32)[:, :]
    k.IDENT = a("IDENTr", [128, 128], F32)[:, :]
    k.IDENTB = a("IDENTBr", [128, 128], BF16)[:, :]
    k.ONESCB = a("ONESCBr", [128, 18], BF16)[:, :]
    k.MAXB = a("MAXBr", [1, 64], BF16)[:, :]
    k.ROWSF = a("ROWSFr", [1, 1024], F32)[:, :]

    k.ACT32 = _alias(nc, k.ACTh, "ACT32", [128, 16384], F32, 0)[:, :]
    k.KVS = _alias(nc, k.ACTh, "KVS", [128, 2048], F32, 32768)[:, :]
    k.BROW = _alias(nc, k.SMFh, "BROW", [1, 2048], BF16, 0)[:, :]
    k.BROW2 = _alias(nc, k.SMFh, "BROW2", [1, 2048], BF16, 4096)[:, :]
    k.WEM = _alias(nc, k.SMFh, "WEM", [16, 1024], F32, 4096)[:, :]
    k.ZROW = _alias(nc, k.SMFh, "ZROW", [1, 1024], F32, 4096)[:, :]
    k.SIL = _alias(nc, k.SMFh, "SIL", [128, 1024], F32, 0)[:, :]
    k.MROW = _alias(nc, k.SMBh, "MROW", [1, 1024], F32, 0)[:, :]
    k.RROW = _alias(nc, k.SMBh, "RROW", [1, 1024], F32, 4096)[:, :]

    k.PA = nc.alloc_psum_tensor("PA", [128, 1024], F32)[:, :]
    k.PB = nc.alloc_psum_tensor("PB", [128, 1024], F32)[:, :]
    k.PC = nc.alloc_psum_tensor("PC", [128, 1024], F32)[:, :]
    k.PD = nc.alloc_psum_tensor("PD", [128, 1024], F32)[:, :]
    for nm in ("pa", "pb", "pc", "pd"):
        setattr(k, f"s_{nm}p", r.sem(f"{nm}_p"))
        setattr(k, f"s_{nm}c", r.sem(f"{nm}_c"))

    k.slots = [Slot(r, i) for i in range(6)]
    k.ws = Stream(r, k.WBUF, k.slots, [(0, 2), (2, 2), (4, 2)])
    k.WB32 = _alias(nc, k.WBUFh, "WB32", [128, 6144], F32, 0)[:, :]
    k.ws32 = Stream(r, k.WB32, k.slots, [(0, 2), (2, 2), (4, 2)], slot_cols=1024)

    k.s_ld = r.sem("ld_const")
    k.s_ldg = r.sem("ld_constg")
    k.s_x = r.sem("x_io")
    k.vch = r.sem("vchain")
    k.sch = r.sem("schain")

    r.Y.dma_start(out=k.X.rearrange("p (c t) -> p c t", c=NC),
                  in_=k.d_x.rearrange("(c p) t -> p c t", p=128)).inc(k.s_x, 16)
    r.Y.dma_start(out=k.ONES_F, in_=k.d_ones[:, 0:128]).inc(k.s_ld, 16)
    r.Y.dma_start(out=k.IDENT, in_=k.d_ident).inc(k.s_ld, 16)
    r.G.dma_start(out=k.ONES_BF, in_=k.d_ones).inc(k.s_ldg, 16)
    r.G.dma_start(out=k.IDENTB, in_=k.d_ident).inc(k.s_ldg, 16)
    r.G.dma_start(out=k.ONESCB[:, 0:1], in_=k.d_onescol).inc(k.s_ldg, 16)
    if "prep1" in need or "ffn1" in need:
        r.Y.dma_start(out=k.SMT[:, 136:200], in_=k.d_b1col).inc(k.s_ld, 16)
        r.Y.dma_start(out=k.SMT[:, 200:216], in_=k.d_b2col).inc(k.s_ld, 16)
    if "attn1" in need:
        r.Y.dma_start(out=k.SMT[:, 216:224], in_=k.d_decay_pp).inc(k.s_ld, 16)

    for e in (r.T, r.V, r.S):
        e.wait(k.s_x)
        e.wait(k.s_ld)
        e.wait(k.s_ldg)
    r.G.wait(k.s_x)
    r.G.wait(k.s_ld)

    n_ln = 0
    for ph in phases:
        if ph == "ln":
            ph_ln(k)
            n_ln += 1
        elif ph == "moe":
            ph_moe(k)
        elif ph == "attn0":
            ph_attn0(k)
        elif ph == "prep1":
            ph_prep1(k)
        elif ph == "ffn1":
            ph_ffn1(k)
        elif ph == "attn1":
            ph_attn1(k)
        r.barrier()

    st = r.sem("store")
    r.Y.wait_val(r.bar, r.bar.v)
    r.Y.dma_start(out=k.d_y.rearrange("(c p) t -> p c t", p=128),
                  in_=k.X.rearrange("p (c t) -> p c t", c=NC)).inc(st, 16)
    nst = 16
    if dbg is not None:
        which, shape = dbg
        k.d_dbg = nc.dram_tensor("dbgout", list(shape), BF16 if which in ("LNX", "ACT", "WBUF", "SMB") else F32, kind="ExternalOutput").ap()
        src = getattr(k, which)
        r.Y.dma_start(out=k.d_dbg, in_=src[:shape[0], :shape[1]]).inc(st, 16)
        nst += 16
    r.Y.wait_val(st, nst)

    r.replay()
    return k



def ph_ln(k):
    r = k.r
    XB = k.ACT
    XSQ = lambda c: k.ACT[:, 16384 + c * 1024: 16384 + (c + 1) * 1024]
    sv = r.sem("ln_cast")
    for c in range(NC):
        xc = k.X[:, c * 1024:(c + 1) * 1024]
        r.V.tensor_copy(XB[:, c * 1024:(c + 1) * 1024], xc).inc(sv, 1)
        r.S.activation(XSQ(c), xc, AF.Square).inc(sv, 1)
    r.T.wait(k.s_pac)
    r.T.wait(k.s_pbc)
    r.T.wait(sv)
    oc = k.ONESCB[:, 0:1]
    for c in range(NC):
        for n in range(2):
            st = (c == 0)
            sp = (c == NC - 1)
            r.T.matmul(k.PA[0:1, n * 512:(n + 1) * 512], oc,
                       XB[:, c * 1024 + n * 512: c * 1024 + (n + 1) * 512],
                       start=st, stop=sp)
            i2 = r.T.matmul(k.PB[0:1, n * 512:(n + 1) * 512], oc,
                            XSQ(c)[:, n * 512:(n + 1) * 512], start=st, stop=sp)
    i2.inc(k.s_pap, 1).inc(k.s_pbp, 1)
    r.V.wait(k.s_pap)
    r.V.tensor_scalar(k.MROW, k.PA[0:1, :], 1.0 / D, None, ALU.mult).inc(k.s_pac, 1).inc(k.vch, 1)
    r.V.wait(k.vch)
    r.V.tensor_tensor(k.ROWSF, k.MROW, k.MROW, ALU.mult).inc(k.vch, 1)
    r.V.wait(k.s_pbp)
    r.V.wait(k.vch)
    r.V.scalar_tensor_tensor(k.RROW, k.PB[0:1, :], 1.0 / D, k.ROWSF,
                             ALU.mult, ALU.subtract).inc(k.s_pbc, 1).inc(k.vch, 1)
    sr = r.sem("ln_rstd")
    r.V.wait(k.vch)
    r.V.tensor_scalar(k.ROWSF, k.RROW, EPS_LN, None, ALU.add).inc(k.vch, 1)
    r.S.wait(k.vch)
    r.S.activation(k.RROW, k.ROWSF, AF.Ln).inc(k.sch, 1)
    r.S.wait(k.sch)
    r.S.activation(k.ROWSF, k.RROW, AF.Exp, scale=-0.5).inc(sr, 1)
    r.T.wait(k.s_pcc)
    r.T.wait(k.s_pdc)
    r.T.wait(sr)
    for n in range(2):
        r.T.matmul(k.PC[:, n * 512:(n + 1) * 512], k.ONES_F,
                   k.MROW[:, n * 512:(n + 1) * 512], start=True, stop=True
                   ).inc(k.s_pcp, 1)
        r.T.matmul(k.PD[:, n * 512:(n + 1) * 512], k.ONES_F,
                   k.ROWSF[:, n * 512:(n + 1) * 512], start=True, stop=True
                   ).inc(k.s_pdp, 1)
    MB = k.SMF[:, 0:1024]
    RB = k.SMF[:, 1024:2048]
    r.V.wait(k.s_pcp)
    r.V.tensor_copy(MB, k.PC).inc(k.s_pcc, 1).inc(k.vch, 1)
    r.V.wait(k.s_pdp)
    r.V.tensor_copy(RB, k.PD).inc(k.s_pdc, 1).inc(k.vch, 1)
    r.V.wait(k.vch)
    sdone = r.sem("ln_done")
    for c in range(NC):
        sl = slice(c * 1024, (c + 1) * 1024)
        tmp = k.ACT32[:, sl]
        r.V.tensor_tensor(tmp, k.X[:, sl], MB, ALU.subtract).inc(k.vch, 1)
        r.V.wait(k.vch)
        r.V.tensor_tensor(k.LNX[:, sl], tmp, RB, ALU.mult).inc(sdone, 1)



MOE_GROUPS = ([0, 1, 2, 3], [4, 5, 6, 7], [8, 9, 10, 11], [12, 13, 14, 15], [16, 17])
BIGM = 1.0e4


def ph_moe(k):
    r = k.r
    GWT = k.SMB[:, 2048:2304]
    SELF32 = _alias(k.nc, k.SMBh, f"SELF32_{r.bar.v}_{k.ws.i}", [16, 128], F32, 6656)[:, :]
    TMPB = k.SMB[:, 2048:3072]
    EB = k.SMT[:, 118:134]
    EBROW = k.SMT[0:1, 272:288]
    s_gwt = r.sem("moe_gwt")
    s_ebr = r.sem("moe_ebr")
    s_rt = r.sem("moe_rt")
    s_selfr = r.sem("moe_selfr")
    s_self = r.sem("moe_self")
    s_bcw = r.sem("moe_bcw")
    s_act = r.sem("moe_act")
    s_scfree = r.sem("moe_scfree")
    s_silfree = r.sem("moe_silfree")
    s_actfree = r.sem("moe_actfree")

    r.G.wait_val(r.bar, r.bar.v)
    r.G.dma_start(out=GWT.rearrange("p (c e) -> p c e", c=NC),
                  in_=k.d_gwt.rearrange("c p e -> p c e"),
                  allow_slow_non_contiguous=True).inc(s_gwt, 16)
    r.Y.wait_val(r.bar, r.bar.v)
    r.Y.dma_start(out=EBROW, in_=k.d_ebias).inc(s_ebr, 16)

    r.T.wait(s_gwt)
    r.T.wait(s_ebr)
    r.T.wait(k.s_pcc)
    r.T.matmul(k.PC[:, 0:16], k.ONES_F, EBROW, start=True, stop=True).inc(k.s_pcp, 1)
    r.V.wait(k.s_pcp)
    r.V.tensor_copy(EB, k.PC[:, 0:16]).inc(k.s_pcc, 1).inc(s_rt, 1)

    SC = k.SMT[:, 0:16]
    S4 = k.SMT[:, 16:32]
    PAIR = k.SMT[:, 32:56]
    GS = k.SMT[:, 56:60]
    MIN6 = k.SMT[:, 60:66]
    TH = k.SMT[:, 66:67]
    M0 = k.SMT[:, 68:84]
    MA = k.SMT[:, 84:100]
    MBv = k.SMT[:, 100:116]
    WR = k.SMT[:, 116:117]
    WRT = k.SMT[:, 134:135]
    SU = k.SMT[:, 135:136]
    pairs = [(0, 1), (0, 2), (0, 3), (1, 2), (1, 3), (2, 3)]
    for tt in range(8):
        P = (k.PA if tt % 2 == 0 else k.PB)[:, 0:16]
        sp = k.s_pap if tt % 2 == 0 else k.s_pbp
        sc_ = k.s_pac if tt % 2 == 0 else k.s_pbc
        r.T.wait(sc_)
        for c in range(NC):
            r.T.matmul(P, k.LNX[:, c * 1024 + tt * 128: c * 1024 + (tt + 1) * 128],
                       GWT[:, c * 16:(c + 1) * 16],
                       start=(c == 0), stop=(c == NC - 1))
        r.rlast("tensor").inc(sp, 1)
        r.S.wait(sp)
        r.S.wait(s_scfree)
        r.S.activation(SC, P, AF.Sigmoid).inc(sc_, 1).inc(s_rt, 1)
        r.V.wait(s_rt)
        r.V.tensor_tensor(S4, SC, EB, ALU.add).inc(k.vch, 1)
        r.V.wait(k.vch)
        for pi, (i, j) in enumerate(pairs):
            r.V.tensor_tensor(PAIR[:, pi * 4:(pi + 1) * 4],
                              S4[:, i:16:4], S4[:, j:16:4], ALU.add).inc(k.vch, 1)
        r.V.wait(k.vch)
        r.V.tensor_tensor(GS, PAIR[:, 0:4], PAIR[:, 4:8], ALU.max).inc(k.vch, 1)
        for pi in range(2, 6):
            r.V.wait(k.vch)
            r.V.tensor_tensor(GS, GS, PAIR[:, pi * 4:(pi + 1) * 4], ALU.max).inc(k.vch, 1)
        gpairs = [(0, 1), (0, 2), (0, 3), (1, 2), (1, 3), (2, 3)]
        r.V.wait(k.vch)
        for pi, (i, j) in enumerate(gpairs):
            r.V.tensor_tensor(MIN6[:, pi:pi + 1], GS[:, i:i + 1], GS[:, j:j + 1],
                              ALU.min).inc(k.vch, 1)
        r.V.wait(k.vch)
        r.V.reduce_max(TH, MIN6, axis=AX.X).inc(k.vch, 1)
        r.V.wait(k.vch)
        GM = MIN6[:, 0:4]
        r.V.tensor_scalar(GM, GS, TH, None, ALU.is_ge).inc(k.vch, 1)
        r.V.wait(k.vch)
        for i in range(4):
            r.V.tensor_copy(M0[:, i:16:4], GM).inc(k.vch, 1)
        r.V.wait(k.vch)
        r.V.tensor_scalar(MA, S4, BIGM, None, ALU.add).inc(k.vch, 1)
        r.V.wait(k.vch)
        r.V.tensor_tensor(MA, MA, M0, ALU.mult).inc(k.vch, 1)
        r.V.wait(k.vch)
        r.V.tensor_scalar(MA, MA, BIGM, None, ALU.subtract).inc(k.vch, 1)
        r.V.wait(k.vch)
        r.V.tensor_copy(MBv, MA).inc(k.vch, 1)
        for rnd in range(3):
            r.V.wait(k.vch)
            r.V.reduce_max(TH, MA, axis=AX.X).inc(k.vch, 1)
            r.V.wait(k.vch)
            E = PAIR[:, 0:16]
            r.V.tensor_scalar(E, MA, TH, None, ALU.is_equal).inc(k.vch, 1)
            r.V.wait(k.vch)
            r.V.scalar_tensor_tensor(MA, E, -2.0 * BIGM, MA, ALU.mult, ALU.add
                                     ).inc(k.vch, 1)
        r.V.wait(k.vch)
        r.V.reduce_max(TH, MA, axis=AX.X).inc(k.vch, 1)
        r.V.wait(k.vch)
        SEL = PAIR[:, 0:16]
        r.V.tensor_scalar(SEL, MBv, TH, None, ALU.is_ge).inc(k.vch, 1)
        r.V.wait(k.vch)
        WSEL = M0
        r.V.tensor_tensor(WSEL, SEL, SC, ALU.mult).inc(k.vch, 1).inc(s_scfree, 1)
        r.V.wait(k.vch)
        r.V.reduce_sum(SU, WSEL, axis=AX.X).inc(k.vch, 1)
        r.V.wait(k.vch)
        r.V.tensor_scalar(SU, SU, 1e-20, None, ALU.add).inc(k.vch, 1)
        r.V.wait(k.vch)
        r.V.reciprocal(WRT, SU).inc(k.vch, 1)
        r.V.wait(k.vch)
        r.V.tensor_scalar(MA, WSEL, WRT, 1.25, ALU.mult, ALU.mult).inc(k.vch, 1)
        r.V.wait(k.vch)
        r.T.wait(k.s_pcc)
        r.T.wait(k.vch)
        r.T.transpose(k.PC[0:16, 0:128], MA, k.IDENT).inc(k.s_pcp, 1)
        r.V.wait(k.s_pcp)
        r.V.tensor_copy(k.WEM[:, tt * 128:(tt + 1) * 128], k.PC[0:16, 0:128]
                        ).inc(k.s_pcc, 1).inc(s_rt, 1)

    for g, grp in enumerate(MOE_GROUPS):
        for ei, e in enumerate(grp):
            shared = e >= NE
            bslot = k.SMB[:, (ei % 2) * 1024:(ei % 2) * 1024 + 1024]
            if not shared:
                r.Y.wait(s_selfr)
                r.Y.dma_start(out=SELF32, in_=k.d_sel[e]).inc(s_self, 16)
                r.T.wait(s_self)
                r.T.wait(k.s_pcc)
                r.T.wait(s_rt)
                for n in range(2):
                    i = r.T.matmul(k.PC[:, n * 512:(n + 1) * 512], SELF32,
                                   k.WEM[:, n * 512:(n + 1) * 512],
                                   start=True, stop=True)
                i.inc(k.s_pcp, 1).inc(s_selfr, 1)
                r.V.wait(k.s_pcp)
                r.V.wait(s_act)
                r.V.tensor_copy(bslot, k.PC).inc(k.s_pcc, 1).inc(s_bcw, 1)
            else:
                r.V.wait(s_act)
                r.V.memset(bslot, 0.5).inc(s_bcw, 1)
            for fc in range(8):
                bu = k.ws.load(k.d_gu[e * 8 + fc], 4096)
                PG, sgp, sgc = (k.PA, k.s_pap, k.s_pac) if fc % 2 == 0 else \
                               (k.PC, k.s_pcp, k.s_pcc)
                PU, sup, suc = (k.PB, k.s_pbp, k.s_pbc) if fc % 2 == 0 else \
                               (k.PD, k.s_pdp, k.s_pdc)
                bu.acquire(r.T)
                r.T.wait(sgc)
                for c in range(NC):
                    for n in range(2):
                        r.T.matmul(PG[:, n * 512:(n + 1) * 512],
                                   bu.ap[:, c * 128:(c + 1) * 128],
                                   k.LNX[:, c * 1024 + n * 512: c * 1024 + n * 512 + 512],
                                   start=(c == 0), stop=(c == NC - 1))
                r.rlast("tensor").inc(sgp, 1)
                r.T.wait(suc)
                for c in range(NC):
                    for n in range(2):
                        r.T.matmul(PU[:, n * 512:(n + 1) * 512],
                                   bu.ap[:, 2048 + c * 128: 2048 + (c + 1) * 128],
                                   k.LNX[:, c * 1024 + n * 512: c * 1024 + n * 512 + 512],
                                   start=(c == 0), stop=(c == NC - 1))
                i = r.rlast("tensor").inc(sup, 1)
                bu.release(i)
                r.S.wait(sgp)
                r.S.wait(s_silfree)
                r.S.activation(k.SIL, PG, AF.Silu).inc(sgc, 1).inc(k.sch, 1)
                r.V.wait(k.sch)
                r.V.wait(sup)
                r.V.wait(s_act)
                r.V.tensor_tensor(TMPB, k.SIL, PU, ALU.mult
                                  ).inc(suc, 1).inc(k.vch, 1).inc(s_silfree, 1)
                r.V.wait(k.vch)
                r.V.wait(s_bcw)
                aslot = k.ACT[:, ei * 8192 + fc * 1024: ei * 8192 + (fc + 1) * 1024]
                r.V.tensor_tensor(aslot, TMPB, bslot, ALU.mult).inc(s_act, 1)
        nacts = len(grp) * 8
        for hc in range(NC):
            if len(grp) == 4:
                bu = k.ws.load(k.d_down[g * NC + hc], 4096)
            else:
                bu = k.ws.load(k.d_downsh[hc], 2048)
            PDn, sdp, sdc = (k.PA, k.s_pap, k.s_pac) if hc % 2 == 0 else \
                            (k.PC, k.s_pcp, k.s_pcc)
            bu.acquire(r.T)
            r.T.wait(sdc)
            if hc == 0:
                r.T.wait(s_act)
            nmm = 0
            for ei, e in enumerate(grp):
                for fc in range(8):
                    for n in range(2):
                        r.T.matmul(PDn[:, n * 512:(n + 1) * 512],
                                   bu.ap[:, (ei * 8 + fc) * 128:(ei * 8 + fc + 1) * 128],
                                   k.ACT[:, ei * 8192 + fc * 1024 + n * 512:
                                         ei * 8192 + fc * 1024 + n * 512 + 512],
                                   start=(ei == 0 and fc == 0),
                                   stop=(ei == len(grp) - 1 and fc == 7))
            i = r.rlast("tensor").inc(sdp, 1)
            bu.release(i)
            if hc == NC - 1:
                i.inc(s_actfree, 1)
            r.V.wait(sdp)
            xsl = k.X[:, hc * 1024:(hc + 1) * 1024]
            r.V.tensor_tensor(xsl, xsl, PDn, ALU.add).inc(sdc, 1).inc(k.vch, 1)
            r.V.wait(k.vch)



def ph_attn0(k):
    r = k.r
    s_br = r.sem("a0_brow")
    s_proj = r.sem("a0_proj")
    s_kstage = r.sem("a0_kstage")
    s_vstage = r.sem("a0_vstage")
    s_lmax = r.sem("a0_lmax")
    s_lmd = r.sem("a0_lmd")
    s_ksq = r.sem("a0_ksq")
    s_ccK = r.sem("a0_ccK")
    s_ccV = r.sem("a0_ccV")
    s_maxk = r.sem("a0_maxk")
    s_exp = r.sem("a0_exp")
    s_pvd = r.sem("a0_pvd")
    s_po = r.sem("a0_po")
    s_zr = r.sem("a0_zr")
    s_zb = r.sem("a0_zb")
    s_evac = r.sem("a0_evac")
    s_vstg = r.sem("a0_vstg")
    KTMP = k.SMB[:, 2048:3072]
    KSQ = k.SMB[:, 3072:4096]
    VTMPF = k.ACT32[:, 8192:9216]
    VTT = [k.SMB[:, 1024:1152], k.SMB[:, 1152:1280]]
    ZB = k.SMB[:, 2048:2560]

    r.G.wait_val(r.bar, r.bar.v)
    r.G.dma_start(out=k.BROW, in_=k.d_l0_bq).inc(s_br, 16)
    r.G.dma_start(out=k.BROW2, in_=k.d_l0_bk).inc(s_br, 16)

    def proj_fm(dunits, brow, h, ping, out_cb):
        bu = k.ws.load(dunits[h], 2048)
        P, sp, sc = (k.PA, k.s_pap, k.s_pac) if ping else (k.PB, k.s_pbp, k.s_pbc)
        bu.acquire(r.T)
        r.T.wait(sc)
        r.T.wait(s_br)
        for c in range(NC):
            for n in range(2):
                r.T.matmul(P[:, n * 512:(n + 1) * 512],
                           bu.ap[:, c * 128:(c + 1) * 128],
                           k.LNX[:, c * 1024 + n * 512: c * 1024 + n * 512 + 512],
                           start=(c == 0), stop=False)
        for n in range(2):
            r.T.matmul(P[:, n * 512:(n + 1) * 512],
                       brow[0:1, h * 128:(h + 1) * 128],
                       k.ONES_BF, start=False, stop=True)
        i = r.rlast("tensor").inc(sp, 1).inc(s_proj, 1)
        bu.release(i)
        out_cb(P, sp, sc)

    for h in range(NH):
        def evq(P, sp, sc, h=h):
            r.V.wait(sp)
            r.V.tensor_copy(k.ACT[:, h * 1024:(h + 1) * 1024], P).inc(sc, 1)
        proj_fm(k.d_l0_wq, k.BROW, h, h % 2 == 0, evq)

    for h in range(NH):
        def evk(P, sp, sc, h=h):
            r.V.wait(sp)
            r.V.wait(s_kstage)
            r.V.wait(s_ksq)
            r.V.tensor_copy(KTMP, P).inc(sc, 1).inc(k.vch, 1)
            r.S.wait(k.vch)
            r.S.wait(s_ksq)
            r.S.activation(KSQ, KTMP, AF.Square).inc(s_ksq, 1)
            r.Y.wait(k.vch)
            r.Y.dma_start(out=k.d_l0k_in.ap()[h * 128:(h + 1) * 128, :], in_=KTMP
                          ).inc(s_kstage, 16)
            r.T.wait(k.s_pcc)
            r.T.wait(s_ksq)
            for n in range(2):
                r.T.matmul(k.PC[0:1, n * 512:(n + 1) * 512], k.ONESCB[:, 0:1],
                           KSQ[:, n * 512:(n + 1) * 512], start=True, stop=True)
            r.rlast("tensor").inc(k.s_pcp, 1).inc(s_ksq, 1)
            r.V.wait(k.s_pcp)
            r.V.reduce_max(k.SMT[0:1, 224 + h:225 + h], k.PC[0:1, 0:1024],
                           axis=AX.X).inc(k.s_pcc, 1).inc(s_lmax, 1)
        proj_fm(k.d_l0_wk, k.BROW2, h, h % 2 == 0, evk)

    r.V.wait(s_lmax)
    r.V.memset(k.ROWSF, 0.0).inc(k.vch, 1)
    r.V.wait(k.vch)
    r.V.tensor_copy(k.ROWSF[0:1, 0:16], k.SMT[0:1, 224:240]).inc(s_lmax, 1)
    r.G.wait_val(s_lmax, s_lmax.v)
    r.G.dma_start(out=k.d_l0k_in.ap()[2048:2049, :], in_=k.ROWSF).inc(s_lmd, 16)

    r.G.wait_val(s_proj, s_proj.v)
    r.G.dma_start(out=k.BROW, in_=k.d_l0_bv).inc(s_br, 16)

    ptile = [r.sem("a0_pt0"), r.sem("a0_pt1")]
    ptilec = [r.sem("a0_pt0c"), r.sem("a0_pt1c")]
    vstg = [r.sem("a0_vs0"), r.sem("a0_vs1")]
    for dv in range(NH):
        def evv(P, sp, sc, dv=dv):
            r.V.wait(sp)
            r.V.wait(s_vstg)
            r.V.tensor_copy(VTMPF, P).inc(sc, 1).inc(k.vch, 1)
            r.T.wait(k.vch)
            for tt in range(8):
                sl = tt % 2
                r.T.wait(ptilec[sl])
                i = r.T.transpose(k.PD[:, sl * 512:sl * 512 + 128],
                                  VTMPF[:, tt * 128:(tt + 1) * 128], k.IDENT
                                  ).inc(ptile[sl], 1)
                if tt >= 6:
                    i.inc(s_vstg, 1)
                r.V.wait(ptile[sl])
                r.V.wait(vstg[sl])
                r.V.tensor_copy(VTT[sl], k.PD[:, sl * 512:sl * 512 + 128]
                                ).inc(ptilec[sl], 1).inc(k.vch, 1)
                r.Y.wait(k.vch)
                r.Y.dma_start(out=k.d_l0v_in.ap()[tt * 128:(tt + 1) * 128,
                                                  dv * 128:(dv + 1) * 128],
                              in_=VTT[sl]).inc(vstg[sl], 16)
        proj_fm(k.d_l0_wv, k.BROW, dv, dv % 2 == 0, evv)

    r.G.wait_val(s_kstage, s_kstage.v)
    r.G.wait(s_lmd)
    r.G.collective_compute("AllGather", ALU.bypass,
                           replica_groups=[[0, 1, 2, 3], [4, 5, 6, 7]],
                           ins=[k.d_l0k_in.ap()], outs=[k.d_l0k_out.ap()]
                           ).inc(s_ccK, 1)
    r.G.wait(vstg[0])
    r.G.wait(vstg[1])
    r.G.collective_compute("AllGather", ALU.bypass,
                           replica_groups=[[0, 1, 2, 3], [4, 5, 6, 7]],
                           ins=[k.d_l0v_in.ap()], outs=[k.d_l0v_out.ap()]
                           ).inc(s_ccV, 1)

    r.Y.wait(s_ccK)
    for rr in range(4):
        r.Y.dma_start(out=k.MAXB[0:1, rr * 16:(rr + 1) * 16],
                      in_=k.d_l0k_out.ap()[rr * 2049 + 2048: rr * 2049 + 2049, 0:16]
                      ).inc(s_maxk, 16)
    r.V.wait(s_maxk)
    r.V.wait(s_lmd)
    GAT = k.ROWSF[0:1, 0:64]
    r.V.tensor_copy(GAT, k.MAXB).inc(k.vch, 1)
    mk1, mk2 = k.ROWSF[0:1, 64:80], k.ROWSF[0:1, 80:96]
    r.V.wait(k.vch)
    r.V.tensor_tensor(mk1, k.ROWSF[0:1, 0:16], k.ROWSF[0:1, 16:32], ALU.max
                      ).inc(k.vch, 1)
    r.V.wait(k.vch)
    r.V.tensor_tensor(mk2, mk1, k.ROWSF[0:1, 32:48], ALU.max).inc(k.vch, 1)
    r.V.wait(k.vch)
    r.V.tensor_tensor(mk1, mk2, k.ROWSF[0:1, 48:64], ALU.max).inc(k.vch, 1)

    QSQ = k.SMB[:, 0:1024]
    for h in range(NH):
        r.S.wait(s_vstg)
        r.S.wait(k.s_pcc)
        r.S.activation(QSQ, k.ACT[:, h * 1024:(h + 1) * 1024], AF.Square
                       ).inc(k.sch, 1)
        r.T.wait(k.sch)
        r.T.wait(k.s_pcc)
        for n in range(2):
            r.T.matmul(k.PC[0:1, n * 512:(n + 1) * 512], k.ONESCB[:, 0:1],
                       QSQ[:, n * 512:(n + 1) * 512], start=True, stop=True)
        r.rlast("tensor").inc(k.s_pcp, 1)
        r.V.wait(k.s_pcp)
        r.V.wait(s_lmd)
        r.V.reduce_max(k.ROWSF[0:1, 160 + h:161 + h], k.PC[0:1, 0:1024], axis=AX.X
                       ).inc(k.s_pcc, 1).inc(k.vch, 1)
    r.S.wait(k.vch)
    r.S.activation(k.ROWSF[0:1, 96:112], k.ROWSF[0:1, 160:176], AF.Sqrt).inc(k.sch, 1)
    r.S.activation(k.ROWSF[0:1, 112:128], mk1, AF.Sqrt).inc(k.sch, 1)
    r.V.wait(k.sch)
    r.V.tensor_tensor(k.ROWSF[0:1, 128:144], k.ROWSF[0:1, 96:112],
                      k.ROWSF[0:1, 112:128], ALU.mult).inc(k.vch, 1)
    r.V.wait(k.vch)
    r.V.tensor_scalar(k.ROWSF[0:1, 144:160], k.ROWSF[0:1, 128:144],
                      -INV_SQRT_HD, None, ALU.mult).inc(k.vch, 1)
    NEGMB = k.SMT[:, 102:118]
    r.T.wait(k.vch)
    r.T.wait(k.s_pcc)
    r.T.matmul(k.PC[:, 0:16], k.ONES_F, k.ROWSF[0:1, 144:160], start=True, stop=True
               ).inc(k.s_pcp, 1)
    r.V.wait(k.s_pcp)
    r.V.tensor_copy(NEGMB, k.PC[:, 0:16]).inc(k.s_pcc, 1).inc(k.vch, 1)

    r.Y.wait(s_ccK)
    r.Y.wait(s_ccV)
    PBUF = [k.SMB[:, 0:1024], k.SMB[:, 1024:2048]]
    pv_done = [None, None]
    nm_done_v = k.vch.v
    for h in range(NH):
        ksrc = []
        for rr in range(4):
            ksrc.append((None, k.d_l0k_out.ap()[rr * 2049 + h * 128:
                                                rr * 2049 + (h + 1) * 128, :]))
        first, span = k.ws.ring[k.ws.i % 3]
        buK = None
        sl0 = k.ws.slots[first]
        for s_ in k.ws.slots[first:first + span]:
            if s_.release_at is not None:
                r.Y.wait_val(s_.release_at[0], s_.release_at[1])
        ldy = sl0.loaded["sync"]
        for rr in range(4):
            r.Y.dma_start(out=k.WBUF[:, first * 2048 + rr * 1024:
                                     first * 2048 + (rr + 1) * 1024],
                          in_=ksrc[rr][1]).inc(ldy, 16)
        buK = BufUse(k.WBUF[:, first * 2048:first * 2048 + 4096], ldy,
                     ldy.v, sl0.freed, k.ws.slots[first:first + span], k.ws)
        k.ws.i += 1
        buV = k.ws.load(
            k.d_l0v_out.ap()[:, h * 128:(h + 1) * 128].rearrange(
                "(kt p) j -> p kt j", p=128), 4096, queue="sync")
        for qb in range(2):
            Q = k.ACT[:, h * 1024 + qb * 512: h * 1024 + qb * 512 + 512]
            first_pv = True
            for kg in range(16):
                sl = kg % 2
                Ps, sps, spc = (k.PA, k.s_pap, k.s_pac) if sl == 0 else \
                               (k.PB, k.s_pbp, k.s_pbc)
                if kg == 0 and qb == 0:
                    buK.acquire(r.T)
                    buV.acquire(r.T)
                r.T.wait(spc)
                for kt2 in range(2):
                    kt = kg * 2 + kt2
                    r.T.matmul(Ps[:, kt2 * 512:(kt2 + 1) * 512],
                               buK.ap[:, kt * 128:(kt + 1) * 128], Q,
                               start=True, stop=True)
                r.rlast("tensor").inc(sps, 1)
                r.S.wait(sps)
                if h == 0 and qb == 0 and kg == 0:
                    r.S.wait_val(k.vch, nm_done_v)
                if pv_done[sl] is not None:
                    r.S.wait_val(s_pvd, pv_done[sl])
                r.S.activation(PBUF[sl], Ps, AF.Exp, bias=NEGMB[:, h:h + 1],
                               scale=INV_SQRT_HD).inc(spc, 1).inc(s_exp, 1)
                r.T.wait(s_exp)
                if first_pv:
                    r.T.wait(k.s_pcc)
                    first_pv = False
                for kt2 in range(2):
                    kt = kg * 2 + kt2
                    r.T.matmul(k.PC[:, 0:512], buV.ap[:, kt * 128:(kt + 1) * 128],
                               PBUF[sl][:, kt2 * 512:(kt2 + 1) * 512],
                               start=(kg == 0 and kt2 == 0),
                               stop=(kg == 15 and kt2 == 1))
                    r.T.matmul(k.PC[0:1, 512:1024], k.ONESCB[:, 0:1],
                               PBUF[sl][:, kt2 * 512:(kt2 + 1) * 512],
                               start=(kg == 0 and kt2 == 0),
                               stop=(kg == 15 and kt2 == 1))
                r.rlast("tensor").inc(s_pvd, 1)
                pv_done[sl] = s_pvd.v
            r.rlast("tensor").inc(k.s_pcp, 1)
            if qb == 1:
                i = r.rlast("tensor")
                buK.release(i)
                buV.release(i)
            r.V.wait(k.s_pcp)
            r.V.reciprocal(k.ZROW[0:1, 0:512], k.PC[0:1, 512:1024]).inc(s_zr, 1)
            r.T.wait(s_zr)
            r.T.matmul(k.PC[:, 512:1024], k.ONES_F, k.ZROW[0:1, 0:512],
                       start=True, stop=True).inc(s_zb, 1)
            r.V.wait(s_zb)
            r.V.tensor_copy(ZB, k.PC[:, 512:1024]).inc(k.vch, 1)
            r.V.wait(k.vch)
            r.V.tensor_tensor(k.LNX[:, h * 1024 + qb * 512:
                                    h * 1024 + qb * 512 + 512],
                              k.PC[:, 0:512], ZB, ALU.mult
                              ).inc(k.s_pcc, 1).inc(s_evac, 1)

    r.G.wait_val(s_proj, s_proj.v)
    r.G.dma_start(out=k.BROW, in_=k.d_l0_bo).inc(s_br, 16)
    for m in range(NH):
        def evo(P, sp, sc, m=m):
            r.V.wait(sp)
            xsl = k.X[:, m * 1024:(m + 1) * 1024]
            r.V.tensor_tensor(xsl, xsl, P, ALU.add).inc(sc, 1)
        if m == 0:
            r.T.wait(s_evac)
        proj_wo(k, k.d_l0_wo, k.BROW, m, m % 2 == 0, evo, s_br, s_proj)


def proj_wo(k, dunits, brow, h, ping, out_cb, s_br, s_proj):
    r = k.r
    bu = k.ws.load(dunits[h], 2048)
    P, sp, sc = (k.PA, k.s_pap, k.s_pac) if ping else (k.PB, k.s_pbp, k.s_pbc)
    bu.acquire(r.T)
    r.T.wait(sc)
    r.T.wait(s_br)
    for c in range(NC):
        for n in range(2):
            r.T.matmul(P[:, n * 512:(n + 1) * 512],
                       bu.ap[:, c * 128:(c + 1) * 128],
                       k.LNX[:, c * 1024 + n * 512: c * 1024 + n * 512 + 512],
                       start=(c == 0), stop=False)
    for n in range(2):
        r.T.matmul(P[:, n * 512:(n + 1) * 512],
                   brow[0:1, h * 128:(h + 1) * 128],
                   k.ONES_BF, start=False, stop=True)
    i = k.r.rlast("tensor").inc(sp, 1).inc(s_proj, 1)
    bu.release(i)
    out_cb(P, sp, sc)



def ph_prep1(k):
    r = k.r
    if getattr(k, "_prep_done", False):
        return
    k._prep_done = True
    s_q = r.sem("p1_q")
    s_sgfree = r.sem("p1_sgfree")
    A0 = k.SMT[:, 224:225]
    TMPC = k.SMT[:, 227:228]
    ONEC = k.SMT[:, 226:227]
    THR = k.SMT[:, 229:230]
    AB = k.SMF
    SG = k.SMB[:, 0:2048]
    MK = k.SMB[:, 2048:4096]
    WQ = [k.ACT[:, 0:2048], k.ACT[:, 2048:4096]]
    s_wq = [r.sem("p1_wq0"), r.sem("p1_wq1")]

    def units():
        for u in range(DFF // 128):
            yield k.d_w1[u], ("w1", u)
        for m in range(NC):
            for q in range(4):
                yield k.d_w2[m][:, q * 2048:(q + 1) * 2048], ("w2", m, q)

    A1 = k.SMT[:, 225:226]
    THR2 = k.SMT[:, 230:231]
    r.G.wait_val(r.bar, r.bar.v)
    s_onec = r.sem("p1_onec")
    r.Y.wait_val(r.bar, r.bar.v)
    r.Y.dma_start(out=ONEC, in_=k.d_onescol).inc(s_onec, 16)
    r.V.memset(A0, 0.0).inc(k.vch, 1)
    r.V.memset(A1, 0.0).inc(k.vch, 1)
    for i, (src, tag) in enumerate(units()):
        bu = k.ws32.load(src, 2048, queue="sync")
        bu.acquire(r.V)
        r.V.tensor_reduce(TMPC, bu.ap, AX.X, ALU.add, apply_absolute_value=True
                          ).inc(k.vch, 1)
        bu.release(r.rlast("vector"))
        r.V.wait(k.vch)
        acc = A0 if tag[0] == "w1" else A1
        r.V.tensor_tensor(acc, acc, TMPC, ALU.add).inc(k.vch, 1)
        r.V.wait(k.vch)
    for acc, thr in ((A0, THR), (A1, THR2)):
        r.T.wait(k.vch)
        r.T.wait(s_onec)
        r.T.wait(k.s_pcc)
        r.T.matmul(k.PC[0:1, 0:1], acc, ONEC, start=True, stop=True).inc(k.s_pcp, 1)
        r.V.wait(k.s_pcp)
        r.V.tensor_scalar(k.SMT[0:1, 228:229], k.PC[0:1, 0:1], 0.7 / float(DFF * D),
                          None, ALU.mult).inc(k.s_pcc, 1).inc(k.vch, 1)
        r.T.wait(k.vch)
        r.T.wait(k.s_pcc)
        r.T.matmul(k.PC[:, 0:1], k.ONES_F, k.SMT[0:1, 228:229], start=True, stop=True
                   ).inc(k.s_pcp, 1)
        r.V.wait(k.s_pcp)
        r.V.tensor_copy(thr, k.PC[:, 0:1]).inc(k.s_pcc, 1).inc(k.vch, 1)

    for i, (src, tag) in enumerate(units()):
        bu = k.ws32.load(src, 2048, queue="sync")
        sl = i % 2
        bu.acquire(r.S)
        r.S.wait(s_sgfree)
        r.S.activation(SG, bu.ap, AF.Sign).inc(k.sch, 1)
        i2 = r.S.activation(AB, bu.ap, AF.Abs)
        i2.inc(k.sch, 1)
        bu.release(i2)
        r.V.wait(k.sch)
        r.V.wait(k.vch)
        r.V.tensor_scalar(MK, AB, THR if tag[0] == "w1" else THR2,
                          None, ALU.is_ge).inc(k.vch, 1)
        r.V.wait(k.vch)
        r.V.wait(s_wq[sl])
        if tag[0] == "w1":
            r.V.tensor_tensor(WQ[sl], SG, MK, ALU.mult).inc(k.vch, 1).inc(s_sgfree, 1)
        else:
            r.V.scalar_tensor_tensor(WQ[sl], SG, 0.5, MK, ALU.mult, ALU.mult
                                     ).inc(k.vch, 1).inc(s_sgfree, 1)
        r.Y.wait(k.vch)
        if tag[0] == "w1":
            dst = k.d_w1q.ap()[tag[1]]
        else:
            dst = k.d_w2q.ap()[tag[1]][:, tag[2] * 2048:(tag[2] + 1) * 2048]
        r.Y.dma_start(out=dst, in_=WQ[sl]).inc(s_wq[sl], 16)
    r.V.wait(s_wq[0])
    r.V.wait(s_wq[1])



def ph_ffn1(k):
    r = k.r
    if not getattr(k, "_b2h_done", False):
        k._b2h_done = True
        r.V.wait_val(r.bar, r.bar.v)
        r.V.tensor_scalar(k.SMT[:, 320:336], k.SMT[:, 200:216], 0.5, None,
                          ALU.mult).inc(k.vch, 1)
        k._b2h_v = k.vch.v
    B2H = k.SMT[:, 320:336]
    s_h = r.sem("f1_h")
    for tt2 in range(2):
        toff = tt2 * 512
        for fc in range(DFF // 128):
            bu = k.ws.load(k.d_w1q.ap()[fc], 2048, queue="sync")
            P, sp, sc = (k.PA, k.s_pap, k.s_pac) if fc % 2 == 0 else \
                         (k.PB, k.s_pbp, k.s_pbc)
            bu.acquire(r.T)
            r.T.wait(sc)
            if fc == 0:
                r.T.wait(s_h)
            for c in range(NC):
                r.T.matmul(P[:, 0:512], bu.ap[:, c * 128:(c + 1) * 128],
                           k.LNX[:, c * 1024 + toff: c * 1024 + toff + 512],
                           start=(c == 0), stop=(c == NC - 1))
            i = r.rlast("tensor").inc(sp, 1)
            bu.release(i)
            r.S.wait(sp)
            r.S.activation(k.ACT[:, fc * 512:(fc + 1) * 512], P[:, 0:512], AF.Gelu,
                           bias=k.SMT[:, 136 + fc:137 + fc]).inc(sc, 1).inc(s_h, 1)
        r.T.wait(s_h)
        for m in range(NC):
            P, sp, sc = (k.PC[:, 0:512], k.s_pcp, k.s_pcc) if m % 2 == 0 else \
                         (k.PD[:, 0:512], k.s_pdp, k.s_pdc)
            r.T.wait(sc)
            for q in range(4):
                bu = k.ws.load(k.d_w2q.ap()[m][:, q * 2048:(q + 1) * 2048], 2048,
                               queue="sync")
                bu.acquire(r.T)
                for c in range(NC):
                    r.T.matmul(P, bu.ap[:, c * 128:(c + 1) * 128],
                               k.ACT[:, (q * NC + c) * 512:(q * NC + c + 1) * 512],
                               start=(q == 0 and c == 0), stop=(q == 3 and c == NC - 1))
                bu.release(r.rlast("tensor"))
            r.rlast("tensor").inc(sp, 1)
            r.V.wait(sp)
            r.V.wait_val(k.vch, k._b2h_v)
            xsl = k.X[:, m * 1024 + toff: m * 1024 + toff + 512]
            r.V.scalar_tensor_tensor(xsl, P, B2H[:, m:m + 1], xsl,
                                     ALU.add, ALU.add).inc(sc, 1).inc(s_h, 1)



def ph_attn1(k):
    r = k.r
    s_br = r.sem("a1_brow")
    s_proj = r.sem("a1_proj")
    s_ar = r.sem("a1_ar")
    s_arb = r.sem("a1_arb")
    s_cc = r.sem("a1_cc")
    s_kv = r.sem("a1_kv")
    s_dec = r.sem("a1_dec")
    KF = k.ACT32[:, 0:1024]
    VF = k.ACT32[:, 1024:2048]
    KT = k.SMB[:, 2048:3072]
    VT = k.SMB[:, 3072:4096]
    KSUM = k.SMT[:, 0:16]

    r.G.wait_val(r.bar, r.bar.v)
    r.G.dma_start(out=k.BROW, in_=k.d_l1_bv).inc(s_br, 16)
    r.G.dma_start(out=k.BROW2, in_=k.d_l1_bk).inc(s_br, 16)
    r.Y.wait_val(r.bar, r.bar.v)
    r.Y.dma_start(out=k.ROWSF, in_=k.d_decay_row).inc(s_dec, 16)

    def proj(units, brow, h, P, sp, sc):
        bu = k.ws.load(units[h], 2048)
        bu.acquire(r.T)
        r.T.wait(sc)
        r.T.wait(s_br)
        for c in range(NC):
            for n in range(2):
                r.T.matmul(P[:, n * 512:(n + 1) * 512],
                           bu.ap[:, c * 128:(c + 1) * 128],
                           k.LNX[:, c * 1024 + n * 512: c * 1024 + n * 512 + 512],
                           start=(c == 0), stop=False)
        for n in range(2):
            r.T.matmul(P[:, n * 512:(n + 1) * 512],
                       brow[0:1, h * 128:(h + 1) * 128],
                       k.ONES_BF, start=False, stop=True)
        i = r.rlast("tensor").inc(sp, 1).inc(s_proj, 1)
        bu.release(i)

    ptile = [r.sem("a1_pt0"), r.sem("a1_pt1")]
    ptilec = [r.sem("a1_pt0c"), r.sem("a1_pt1c")]
    for h in range(NH):
        proj(k.d_l1_wk, k.BROW2, h, k.PA, k.s_pap, k.s_pac)
        r.S.wait(k.s_pap)
        r.S.wait(s_kv)
        r.S.activation(KF, k.PA, AF.Relu).inc(k.s_pac, 1).inc(k.sch, 1)
        proj(k.d_l1_wv, k.BROW, h, k.PB, k.s_pbp, k.s_pbc)
        r.S.wait(k.s_pbp)
        r.S.wait(s_kv)
        r.S.activation(VF, k.PB, AF.Copy).inc(k.s_pbc, 1).inc(k.sch, 1)
        r.T.wait(k.sch)
        for tt in range(8):
            sl = tt % 2
            r.T.wait(ptilec[sl])
            r.T.transpose(k.PD[:, sl * 512:sl * 512 + 128],
                          KF[:, tt * 128:(tt + 1) * 128], k.IDENT).inc(ptile[sl], 1)
            r.V.wait(ptile[sl])
            r.V.wait(s_dec)
            r.V.tensor_scalar(KT[:, tt * 128:(tt + 1) * 128],
                              k.PD[:, sl * 512:sl * 512 + 128],
                              k.SMT[:, 216 + tt:217 + tt], None, ALU.mult
                              ).inc(ptilec[sl], 1).inc(k.vch, 1)
        for tt in range(8):
            sl = tt % 2
            r.T.wait(ptilec[sl])
            i = r.T.transpose(k.PD[:, sl * 512:sl * 512 + 128],
                              VF[:, tt * 128:(tt + 1) * 128], k.IDENT
                              ).inc(ptile[sl], 1)
            if tt >= 6:
                i.inc(s_kv, 1)
            r.V.wait(ptile[sl])
            r.V.tensor_copy(VT[:, tt * 128:(tt + 1) * 128],
                            k.PD[:, sl * 512:sl * 512 + 128]
                            ).inc(ptilec[sl], 1).inc(k.vch, 1)
        r.T.wait(k.s_pcc)
        r.T.wait(k.vch)
        for tt in range(8):
            r.T.matmul(k.PC[:, 0:128], KT[:, tt * 128:(tt + 1) * 128],
                       VT[:, tt * 128:(tt + 1) * 128],
                       start=(tt == 0), stop=(tt == 7))
        for tt in range(8):
            r.T.matmul(k.PC[:, 128:129], KT[:, tt * 128:(tt + 1) * 128],
                       k.ONESCB[:, 0:1], start=(tt == 0), stop=(tt == 7))
        r.rlast("tensor").inc(k.s_pcp, 1)
        r.V.wait(k.s_pcp)
        r.V.tensor_copy(k.KVS[:, h * 128:(h + 1) * 128], k.PC[:, 0:128]
                        ).inc(k.vch, 1)
        r.V.tensor_copy(KSUM[:, h:h + 1], k.PC[:, 128:129]
                        ).inc(k.s_pcc, 1).inc(k.vch, 1)

    r.Y.wait(k.vch)
    r.Y.dma_start(out=k.d_l1cc_in.ap()[:, 0:2048], in_=k.KVS).inc(s_cc, 16)
    r.Y.dma_start(out=k.d_l1cc_in.ap()[:, 2048:2064], in_=KSUM).inc(s_cc, 16)
    r.G.wait_val(s_cc, s_cc.v)
    r.G.collective_compute("AllReduce", ALU.add,
                           replica_groups=[[0, 1, 2, 3], [4, 5, 6, 7]],
                           ins=[k.d_l1cc_in.ap()], outs=[k.d_l1cc_out.ap()]
                           ).inc(s_ar, 1)

    r.G.wait_val(s_proj, s_proj.v)
    r.G.dma_start(out=k.BROW2, in_=k.d_l1_bq).inc(s_br, 16)
    DECB = k.SMB[:, 0:1024]
    QTMP = k.SMB[:, 1024:2048]
    r.T.wait(s_dec)
    r.T.wait(k.s_pdc)
    r.T.wait(ptilec[0])
    r.T.wait(ptilec[1])
    for n in range(2):
        r.T.matmul(k.PD[:, n * 512:(n + 1) * 512], k.ONES_F,
                   k.ROWSF[0:1, n * 512:(n + 1) * 512], start=True, stop=True)
    r.rlast("tensor").inc(k.s_pdp, 1)
    r.V.wait(k.s_pdp)
    r.V.wait(s_kv)
    r.V.tensor_copy(DECB, k.PD).inc(k.s_pdc, 1).inc(k.vch, 1)
    for h in range(NH):
        P, sp, sc = (k.PA, k.s_pap, k.s_pac) if h % 2 == 0 else \
                    (k.PB, k.s_pbp, k.s_pbc)
        proj(k.d_l1_wq, k.BROW2, h, P, sp, sc)
        r.S.wait(sp)
        r.S.wait(k.vch)
        r.S.activation(QTMP, P, AF.Relu).inc(sc, 1).inc(k.sch, 1)
        r.V.wait(k.sch)
        r.V.tensor_tensor(k.ACT[:, h * 1024:(h + 1) * 1024], QTMP, DECB,
                          ALU.mult).inc(k.vch, 1)

    KVB = k.SMB[:, 2048:4096]
    KSB = k.ONESCB[:, 2:18]
    r.G.wait_val(s_ar, s_ar.v)
    r.G.dma_start(out=KVB, in_=k.d_l1cc_out.ap()[:, 0:2048]).inc(s_arb, 16)
    r.G.dma_start(out=KSB, in_=k.d_l1cc_out.ap()[:, 2048:2064]).inc(s_arb, 16)

    s_z = r.sem("a1_z")
    ZBB = k.SMB[:, 0:1024]
    for h in range(NH):
        P, sp, sc = (k.PA, k.s_pap, k.s_pac) if h % 2 == 0 else \
                    (k.PB, k.s_pbp, k.s_pbc)
        r.T.wait(sc)
        r.T.wait(s_arb)
        Q0 = k.ACT[:, h * 1024: h * 1024 + 512]
        Q1 = k.ACT[:, h * 1024 + 512: h * 1024 + 1024]
        r.T.wait(k.vch)
        for n, Qn in enumerate((Q0, Q1)):
            r.T.matmul(P[:, n * 512:(n + 1) * 512], KVB[:, h * 128:(h + 1) * 128],
                       Qn, start=True, stop=True)
        r.rlast("tensor").inc(sp, 1)
        r.T.wait(k.s_pcc)
        for n, Qn in enumerate((Q0, Q1)):
            r.T.matmul(k.PC[0:1, n * 512:(n + 1) * 512], KSB[:, h:h + 1],
                       Qn, start=True, stop=True)
        r.rlast("tensor").inc(k.s_pcp, 1)
        r.V.wait(k.s_pcp)
        r.V.tensor_scalar(k.ROWSF, k.PC[0:1, 0:1024], 1e-6, None, ALU.add
                          ).inc(k.s_pcc, 1).inc(k.vch, 1)
        r.V.wait(k.vch)
        r.V.reciprocal(k.ZROW, k.ROWSF).inc(s_z, 1)
        r.T.wait(s_z)
        r.T.wait(k.s_pdc)
        for n in range(2):
            r.T.matmul(k.PD[:, n * 512:(n + 1) * 512], k.ONES_F,
                       k.ZROW[0:1, n * 512:(n + 1) * 512], start=True, stop=True)
        r.rlast("tensor").inc(k.s_pdp, 1)
        r.V.wait(k.s_pdp)
        r.V.tensor_copy(ZBB, k.PD).inc(k.s_pdc, 1).inc(k.vch, 1)
        r.V.wait(k.vch)
        r.V.wait(sp)
        r.V.tensor_tensor(k.ACT[:, 16384 + h * 1024: 16384 + (h + 1) * 1024],
                          P, ZBB, ALU.mult).inc(sc, 1).inc(k.vch, 1)

    s_bo = r.sem("a1_bo")
    r.G.wait_val(s_proj, s_proj.v)
    r.G.wait(k.s_pdp)
    r.G.dma_start(out=k.BROW2, in_=k.d_l1_bo).inc(s_bo, 16)
    for m in range(NH):
        bu = k.ws.load(k.d_l1_wo[m], 2048)
        P, sp, sc = (k.PA, k.s_pap, k.s_pac) if m % 2 == 0 else \
                    (k.PB, k.s_pbp, k.s_pbc)
        bu.acquire(r.T)
        r.T.wait(sc)
        r.T.wait(s_bo)
        if m == 0:
            r.T.wait(k.vch)
        for c in range(NC):
            for n in range(2):
                r.T.matmul(P[:, n * 512:(n + 1) * 512],
                           bu.ap[:, c * 128:(c + 1) * 128],
                           k.ACT[:, 16384 + c * 1024 + n * 512:
                                 16384 + c * 1024 + n * 512 + 512],
                           start=(c == 0), stop=False)
        for n in range(2):
            r.T.matmul(P[:, n * 512:(n + 1) * 512],
                       k.BROW2[0:1, m * 128:(m + 1) * 128],
                       k.ONES_BF, start=False, stop=True)
        i = r.rlast("tensor").inc(sp, 1)
        bu.release(i)
        r.V.wait(sp)
        xsl = k.X[:, m * 1024:(m + 1) * 1024]
        r.V.tensor_tensor(xsl, xsl, P, ALU.add).inc(sc, 1)



_PROG_CACHE = {}


def _get_program(phases, dbg=None):
    key = (tuple(phases), dbg)
    if key not in _PROG_CACHE:
        k = build_program(tuple(phases), dbg=dbg)
        _PROG_CACHE[key] = k
    return _PROG_CACHE[key]


def kernel(**inputs):
    pre = _prep(inputs)
    x = pre["x_cores"]
    for g in GROUPS:
        k = _get_program(g)
        in_maps = []
        for c in range(8):
            m = {"x": x[c]}
            for nm in k.innames:
                if nm in ("decay_pp", "decay_row"):
                    m[nm] = pre[nm][c]
                else:
                    m[nm] = pre[nm]
            in_maps.append(m)
        res = run_bass_kernel_spmd(k.nc, in_maps, core_ids=list(range(8)))
        x = [res.results[c]["y"] for c in range(8)]
    out = np.empty((2, S, D), np.float32)
    for c in range(8):
        out[c // 4, (c % 4) * T:(c % 4 + 1) * T, :] = x[c].T
    return out


# version 03
# speedup vs baseline: 1.0336x; 1.0336x over previous
import numpy as np
import ml_dtypes
import concourse.bass as bass
import concourse.mybir as mybir
from concourse.bass_utils import run_bass_kernel_spmd

F32 = mybir.dt.float32
BF16 = mybir.dt.bfloat16
AF = mybir.ActivationFunctionType
ALU = mybir.AluOpType
AX = mybir.AxisListType

D = 2048
T = 1024
NC = 16
NH = 16
HD = 128
S = 4096
NE = 16
NPE = 18
FMOE = 1024
DFF = 8192
EPS_LN = 1e-5
INV_SQRT_HD = 1.0 / float(np.sqrt(128.0))

ALL_PHASES = ("ln", "moe", "ln", "attn0", "ln", "moe",
              "prep1", "ln", "ffn1", "ln", "attn1", "ln", "ffn1")

GROUPS = [ALL_PHASES]



def _blk_lhsT(WT):
    K, M = WT.shape
    a = WT.reshape(K // 128, 128, M // 128, 128)
    return np.ascontiguousarray(a.transpose(2, 1, 0, 3).reshape(M // 128, 128, K))


def _prep(inp):
    g = {}
    f32 = lambda a: np.ascontiguousarray(np.asarray(a), dtype=np.float32)

    x = f32(inp["L0_attn_x"])
    g["x_cores"] = [np.ascontiguousarray(x[c // 4, (c % 4) * T:(c % 4 + 1) * T, :].T)
                    for c in range(8)]

    for nm in ("Wq", "Wk", "Wv", "Wo"):
        W = f32(inp[f"L0_attn_{nm}"])
        g[f"l0_{nm.lower()}"] = _blk_lhsT(np.ascontiguousarray(W.T))
    for nm in ("bq", "bk", "bv", "bo"):
        g[f"l0_{nm}"] = f32(inp[f"L0_attn_{nm}"]).reshape(1, D)

    gw = f32(inp["L0_ffn_gate_weight"])
    g["gwt"] = np.ascontiguousarray(gw.T.reshape(NC, 128, NE))
    g["e_bias"] = f32(inp["L0_ffn_e_score_correction_bias"]).reshape(1, NE)
    gate_w = f32(inp["L0_ffn_gate_w"])
    up_w = f32(inp["L0_ffn_up_w"])
    down_w = f32(inp["L0_ffn_down_w"])
    shg = f32(inp["L0_ffn_sh_gate_w"])
    shu = f32(inp["L0_ffn_sh_up_w"])
    shd = f32(inp["L0_ffn_sh_down_w"])
    gu_units = np.empty((NPE, 8, 128, 4096), np.float32)
    for e in range(NPE):
        if e < NE:
            gmat, umat = gate_w[e], up_w[e]
        else:
            lo = (e - NE) * FMOE
            gmat, umat = shg[lo:lo + FMOE], shu[lo:lo + FMOE]
        gb = _blk_lhsT(np.ascontiguousarray(gmat.T))
        ub = _blk_lhsT(np.ascontiguousarray(umat.T))
        gu_units[e, :, :, :2048] = gb
        gu_units[e, :, :, 2048:] = ub
    g["l0_gu"] = gu_units.reshape(NPE * 8, 128, 4096)

    groups = [[0, 1, 2, 3], [4, 5, 6, 7], [8, 9, 10, 11], [12, 13, 14, 15], [16, 17]]
    dmats = [down_w[e].T if e < NE else
             (shd[:, (e - NE) * FMOE:(e - NE + 1) * FMOE]).T for e in range(NPE)]
    dblk = [_blk_lhsT(np.ascontiguousarray(m)) for m in dmats]
    full = np.empty((4 * NC, 128, 4096), np.float32)
    for gi in range(4):
        for hc in range(NC):
            full[gi * NC + hc] = np.concatenate(
                [dblk[e][hc] for e in groups[gi]], axis=1)
    g["l0_down"] = full
    shared = np.empty((NC, 128, 2048), np.float32)
    for hc in range(NC):
        shared[hc] = np.concatenate([dblk[16][hc], dblk[17][hc]], axis=1)
    g["l0_downsh"] = shared

    for nm in ("Wq", "Wk", "Wv", "Wo"):
        W = f32(inp[f"L1_attn_{nm}"])
        g[f"l1_{nm.lower()}"] = _blk_lhsT(np.ascontiguousarray(W.T))
    for nm in ("bq", "bk", "bv", "bo"):
        g[f"l1_{nm}"] = f32(inp[f"L1_attn_{nm}"]).reshape(1, D)

    w1 = f32(inp["L1_ffn_w1"])
    w2 = f32(inp["L1_ffn_w2"])
    g["l1_w1"] = _blk_lhsT(np.ascontiguousarray(w1.T))
    g["l1_w2"] = _blk_lhsT(np.ascontiguousarray(w2.T))
    b1 = f32(inp["L1_ffn_b1"])
    g["b1col"] = np.ascontiguousarray(b1.reshape(DFF // 128, 128).T)
    b2 = f32(inp["L1_ffn_b2"])
    g["b2col"] = np.ascontiguousarray(b2.reshape(NC, 128).T)

    g["ones_row"] = np.ones((1, 512), np.float32)
    g["onescol"] = np.ones((128, 1), np.float32)
    g["ident"] = np.eye(128, dtype=np.float32)
    sel = np.zeros((NE, NE, 128), np.float32)
    for e in range(NE):
        sel[e, e, :] = 1.0
    g["sel"] = sel
    pos = np.arange(S, dtype=np.float64)
    decay = np.exp(-pos / float(S)).astype(np.float32)
    g["decay_pp"] = [np.ascontiguousarray(
        decay[(c % 4) * T:(c % 4 + 1) * T].reshape(8, 128).T) for c in range(8)]
    g["decay_row"] = [np.ascontiguousarray(
        decay[(c % 4) * T:(c % 4 + 1) * T].reshape(1, T)) for c in range(8)]
    return g



ENGINES = ("tensor", "vector", "scalar", "gpsimd", "sync")


class Sem:
    __slots__ = ("h", "v")

    def __init__(self, nc, name):
        self.h = nc.alloc_semaphore(name)
        self.v = 0


class RecInst:
    __slots__ = ("item",)

    def __init__(self, item):
        self.item = item

    def inc(self, sem, n=1):
        sem.v += n
        self.item[3].append((sem.h, n))
        return self


class Eng:

    def __init__(self, rec, name):
        self.rec = rec
        self.name = name

    def __getattr__(self, meth):
        def call(*a, **k):
            item = [meth, a, k, []]
            self.rec.q[self.name].append(item)
            self.rec.last[self.name] = item
            return RecInst(item)
        return call

    def wait(self, sem):
        if sem.v > 0:
            self.rec.q[self.name].append(["wait_ge", (sem.h, sem.v), {}, []])

    def wait_val(self, sem, v):
        self.rec.q[self.name].append(["wait_ge", (sem.h, v), {}, []])


class Rec:
    def __init__(self, nc):
        self.nc = nc
        self.q = {e: [] for e in ENGINES}
        self.last = {e: None for e in ENGINES}
        self.T = Eng(self, "tensor")
        self.V = Eng(self, "vector")
        self.S = Eng(self, "scalar")
        self.G = Eng(self, "gpsimd")
        self.Y = Eng(self, "sync")
        self._sems = {}
        self.bar = Sem(nc, "bar")
        self._bar_last = {e: None for e in ENGINES}

    def sem(self, name):
        s = self._sems.get(name)
        if s is None:
            s = self._sems[name] = Sem(self.nc, name)
        return s

    def rlast(self, eng):
        return RecInst(self.last[eng])

    def barrier(self, engines=("tensor", "vector", "scalar")):
        n = 0
        for e in engines:
            it = self.last[e]
            if it is not None and it is not self._bar_last[e]:
                RecInst(it).inc(self.bar, 1)
                self._bar_last[e] = it
                n += 1
        if n:
            for e in engines:
                self.q[e].append(["wait_ge", (self.bar.h, self.bar.v), {}, []])

    def replay(self):
        with self.nc.Block() as blk:
            for name in ENGINES:
                qq = self.q[name]

                def run(eng, qq=qq, name=name):
                    for meth, a, k, incs in qq:
                        inst = getattr(eng, meth)(*a, **k)
                        if not incs:
                            continue
                        if meth in ("dma_start", "collective_compute"):
                            assert len(incs) == 1, (meth, incs)
                            inst.then_inc(*incs[0])
                            continue
                        inst.then_inc(*incs[0])
                        for (h, n) in incs[1:]:
                            eng.drain().then_inc(h, n)
                getattr(blk, name)(run)


class BufUse:

    __slots__ = ("ap", "loaded", "wait_v", "freed", "slots_span", "stream")

    def __init__(self, ap, loaded, wait_v, freed, slots_span, stream):
        self.ap = ap
        self.loaded = loaded
        self.wait_v = wait_v
        self.freed = freed
        self.slots_span = slots_span
        self.stream = stream

    def acquire(self, eng):
        eng.wait_val(self.loaded, self.wait_v)

    def release(self, inst):
        inst.inc(self.freed, 1)
        mark = (self.freed, self.freed.v)
        for s in self.slots_span:
            s.release_at = mark


class Slot:
    def __init__(self, rec, idx):
        self.idx = idx
        self.loaded = {"gpsimd": rec.sem(f"wb_ldg{idx}"),
                       "sync": rec.sem(f"wb_ldy{idx}")}
        self.freed = rec.sem(f"wb_fr{idx}")
        self.release_at = None


class Stream:

    def __init__(self, rec, wbuf, slots, ring, slot_cols=2048):
        self.rec = rec
        self.wbuf = wbuf
        self.slots = slots
        self.ring = ring
        self.slot_cols = slot_cols
        self.i = 0

    def load(self, src_ap, ncols, queue="gpsimd", ndma=1):
        first, span = self.ring[self.i % len(self.ring)]
        self.i += 1
        sl = self.slots[first]
        eng = self.rec.G if queue == "gpsimd" else self.rec.Y
        spanned = self.slots[first:first + span]
        for s in spanned:
            r = s.release_at
            if r is not None:
                eng.wait_val(r[0], r[1])
        dst = self.wbuf[:, first * self.slot_cols: first * self.slot_cols + ncols]
        ld = sl.loaded[queue]
        eng.dma_start(out=dst, in_=src_ap).inc(ld, 16)
        return BufUse(dst, ld, ld.v, sl.freed, spanned, self)



class K:
    pass


def _alias(nc, base_handle, name, shape, dtype, byte_off):
    addr = nc.lookup_mloc(base_handle).addr + byte_off
    return nc.alloc_sbuf_tensor_at(name, shape, dtype, offset=addr)


def build_program(phases, dbg=None):
    nc = bass.Bass("TRN2", target_bir_lowering=False)
    k = K()
    k.nc = nc
    k.phases = phases
    r = k.r = Rec(nc)
    k.dbg = dbg

    k.d_x = nc.dram_tensor("x", [D, T], F32, kind="ExternalInput").ap()
    k.d_y = nc.dram_tensor("y", [D, T], F32, kind="ExternalOutput").ap()
    k.innames = []

    def din(name, shape):
        k.innames.append(name)
        return nc.dram_tensor(name, list(shape), F32, kind="ExternalInput").ap()

    k.d_ones = din("ones_row", (1, 512))
    k.d_onescol = din("onescol", (128, 1))
    k.d_ident = din("ident", (128, 128))
    need = set(phases)
    if "moe" in need:
        k.d_gwt = din("gwt", (NC, 128, NE))
        k.d_ebias = din("e_bias", (1, NE))
        k.d_gu = din("l0_gu", (NPE * 8, 128, 4096))
        k.d_down = din("l0_down", (4 * NC, 128, 4096))
        k.d_downsh = din("l0_downsh", (NC, 128, 2048))
        k.d_sel = din("sel", (NE, NE, 128))
    if "attn0" in need:
        for nm in ("wq", "wk", "wv", "wo"):
            setattr(k, f"d_l0_{nm}", din(f"l0_{nm}", (NC, 128, D)))
        for nm in ("bq", "bk", "bv", "bo"):
            setattr(k, f"d_l0_{nm}", din(f"l0_{nm}", (1, D)))
        k.d_l0k_in = nc.dram_tensor("l0k_in", [2049, T], BF16, kind="Internal")
        k.d_l0k_out = nc.dram_tensor("l0k_out", [4 * 2049, T], BF16, kind="Internal")
        k.d_l0v_in = nc.dram_tensor("l0v_in", [T, D], BF16, kind="Internal")
        k.d_l0v_out = nc.dram_tensor("l0v_out", [S, D], BF16, kind="Internal")
    if "attn1" in need:
        for nm in ("wq", "wk", "wv", "wo"):
            setattr(k, f"d_l1_{nm}", din(f"l1_{nm}", (NC, 128, D)))
        for nm in ("bq", "bk", "bv", "bo"):
            setattr(k, f"d_l1_{nm}", din(f"l1_{nm}", (1, D)))
        k.d_decay_pp = din("decay_pp", (128, 8))
        k.d_decay_row = din("decay_row", (1, T))
        k.d_l1cc_in = nc.dram_tensor("l1cc_in", [128, 2064], F32, kind="Internal")
        k.d_l1cc_out = nc.dram_tensor("l1cc_out", [128, 2064], F32, kind="Internal")
    if "prep1" in need or "ffn1" in need:
        k.d_w1 = din("l1_w1", (DFF // 128, 128, D))
        k.d_w2 = din("l1_w2", (NC, 128, DFF))
        k.d_b1col = din("b1col", (128, 64))
        k.d_b2col = din("b2col", (128, NC))
        k.d_w1q = nc.dram_tensor("w1q", [DFF // 128, 128, D], BF16, kind="Internal")
        k.d_w2q = nc.dram_tensor("w2q", [NC, 128, DFF], BF16, kind="Internal")

    a = nc.alloc_sbuf_tensor
    k.X = a("Xr", [128, 16384], F32)[:, :]
    k.LNX = a("LNXr", [128, 16384], BF16)[:, :]
    k.ACTh = a("ACTr", [128, 32768], BF16)
    k.ACT = k.ACTh[:, :]
    k.WBUFh = a("WBUFr", [128, 12288], BF16)
    k.WBUF = k.WBUFh[:, :]
    k.SMFh = a("SMFr", [128, 2048], F32)
    k.SMF = k.SMFh[:, :]
    k.SMBh = a("SMBr", [128, 4096], BF16)
    k.SMB = k.SMBh[:, :]
    k.SMT = a("SMTr", [128, 360], F32)[:, :]
    k.ONES_BF = a("ONESBF", [1, 512], BF16)[:, :]
    k.ONES_F = a("ONESF", [1, 128], F32)[:, :]
    k.IDENT = a("IDENTr", [128, 128], F32)[:, :]
    k.IDENTB = a("IDENTBr", [128, 128], BF16)[:, :]
    k.ONESCB = a("ONESCBr", [128, 18], BF16)[:, :]
    k.MAXB = a("MAXBr", [1, 64], BF16)[:, :]
    k.ROWSF = a("ROWSFr", [1, 1024], F32)[:, :]

    k.ACT32 = _alias(nc, k.ACTh, "ACT32", [128, 16384], F32, 0)[:, :]
    k.KVS = _alias(nc, k.ACTh, "KVS", [128, 2048], F32, 32768)[:, :]
    k.BROW = _alias(nc, k.SMFh, "BROW", [1, 2048], BF16, 0)[:, :]
    k.BROW2 = _alias(nc, k.SMFh, "BROW2", [1, 2048], BF16, 4096)[:, :]
    k.WEM = _alias(nc, k.SMFh, "WEM", [16, 1024], F32, 4096)[:, :]
    k.ZROW = _alias(nc, k.SMFh, "ZROW", [1, 1024], F32, 4096)[:, :]
    k.SIL = _alias(nc, k.SMFh, "SIL", [128, 1024], F32, 0)[:, :]
    k.MROW = _alias(nc, k.SMBh, "MROW", [1, 1024], F32, 0)[:, :]
    k.RROW = _alias(nc, k.SMBh, "RROW", [1, 1024], F32, 4096)[:, :]

    k.PA = nc.alloc_psum_tensor("PA", [128, 1024], F32)[:, :]
    k.PB = nc.alloc_psum_tensor("PB", [128, 1024], F32)[:, :]
    k.PC = nc.alloc_psum_tensor("PC", [128, 1024], F32)[:, :]
    k.PD = nc.alloc_psum_tensor("PD", [128, 1024], F32)[:, :]
    for nm in ("pa", "pb", "pc", "pd"):
        setattr(k, f"s_{nm}p", r.sem(f"{nm}_p"))
        setattr(k, f"s_{nm}c", r.sem(f"{nm}_c"))

    k.slots = [Slot(r, i) for i in range(6)]
    k.ws = Stream(r, k.WBUF, k.slots, [(0, 2), (2, 2), (4, 2)])
    k.WB32 = _alias(nc, k.WBUFh, "WB32", [128, 6144], F32, 0)[:, :]
    k.ws32 = Stream(r, k.WB32, k.slots, [(0, 2), (2, 2), (4, 2)], slot_cols=1024)

    k.s_ld = r.sem("ld_const")
    k.s_ldg = r.sem("ld_constg")
    k.s_x = r.sem("x_io")
    k.vch = r.sem("vchain")
    k.sch = r.sem("schain")

    r.Y.dma_start(out=k.X.rearrange("p (c t) -> p c t", c=NC),
                  in_=k.d_x.rearrange("(c p) t -> p c t", p=128)).inc(k.s_x, 16)
    r.Y.dma_start(out=k.ONES_F, in_=k.d_ones[:, 0:128]).inc(k.s_ld, 16)
    r.Y.dma_start(out=k.IDENT, in_=k.d_ident).inc(k.s_ld, 16)
    r.G.dma_start(out=k.ONES_BF, in_=k.d_ones).inc(k.s_ldg, 16)
    r.G.dma_start(out=k.IDENTB, in_=k.d_ident).inc(k.s_ldg, 16)
    r.G.dma_start(out=k.ONESCB[:, 0:1], in_=k.d_onescol).inc(k.s_ldg, 16)
    if "prep1" in need or "ffn1" in need:
        r.Y.dma_start(out=k.SMT[:, 136:200], in_=k.d_b1col).inc(k.s_ld, 16)
        r.Y.dma_start(out=k.SMT[:, 200:216], in_=k.d_b2col).inc(k.s_ld, 16)
    if "attn1" in need:
        r.Y.dma_start(out=k.SMT[:, 216:224], in_=k.d_decay_pp).inc(k.s_ld, 16)

    for e in (r.T, r.V, r.S):
        e.wait(k.s_x)
        e.wait(k.s_ld)
        e.wait(k.s_ldg)
    r.G.wait(k.s_x)
    r.G.wait(k.s_ld)

    n_ln = 0
    for ph in phases:
        if ph == "ln":
            ph_ln(k)
            n_ln += 1
        elif ph == "moe":
            ph_moe(k)
        elif ph == "attn0":
            ph_attn0(k)
        elif ph == "prep1":
            ph_prep1(k)
        elif ph == "ffn1":
            ph_ffn1(k)
        elif ph == "attn1":
            ph_attn1(k)
        r.barrier()

    st = r.sem("store")
    r.Y.wait_val(r.bar, r.bar.v)
    r.Y.dma_start(out=k.d_y.rearrange("(c p) t -> p c t", p=128),
                  in_=k.X.rearrange("p (c t) -> p c t", c=NC)).inc(st, 16)
    nst = 16
    if dbg is not None:
        which, shape = dbg
        k.d_dbg = nc.dram_tensor("dbgout", list(shape), BF16 if which in ("LNX", "ACT", "WBUF", "SMB") else F32, kind="ExternalOutput").ap()
        src = getattr(k, which)
        r.Y.dma_start(out=k.d_dbg, in_=src[:shape[0], :shape[1]]).inc(st, 16)
        nst += 16
    r.Y.wait_val(st, nst)

    r.replay()
    return k



def ph_ln(k):
    r = k.r
    XB = k.ACT
    XSQ = lambda c: k.ACT[:, 16384 + c * 1024: 16384 + (c + 1) * 1024]
    sv = r.sem("ln_cast")
    for c in range(NC):
        xc = k.X[:, c * 1024:(c + 1) * 1024]
        r.V.tensor_copy(XB[:, c * 1024:(c + 1) * 1024], xc).inc(sv, 1)
        r.S.activation(XSQ(c), xc, AF.Square).inc(sv, 1)
    r.T.wait(k.s_pac)
    r.T.wait(k.s_pbc)
    r.T.wait(sv)
    oc = k.ONESCB[:, 0:1]
    for c in range(NC):
        for n in range(2):
            st = (c == 0)
            sp = (c == NC - 1)
            r.T.matmul(k.PA[0:1, n * 512:(n + 1) * 512], oc,
                       XB[:, c * 1024 + n * 512: c * 1024 + (n + 1) * 512],
                       start=st, stop=sp)
            i2 = r.T.matmul(k.PB[0:1, n * 512:(n + 1) * 512], oc,
                            XSQ(c)[:, n * 512:(n + 1) * 512], start=st, stop=sp)
    i2.inc(k.s_pap, 1).inc(k.s_pbp, 1)
    r.V.wait(k.s_pap)
    r.V.tensor_scalar(k.MROW, k.PA[0:1, :], 1.0 / D, None, ALU.mult).inc(k.s_pac, 1).inc(k.vch, 1)
    r.V.wait(k.vch)
    r.V.tensor_tensor(k.ROWSF, k.MROW, k.MROW, ALU.mult).inc(k.vch, 1)
    r.V.wait(k.s_pbp)
    r.V.wait(k.vch)
    r.V.scalar_tensor_tensor(k.RROW, k.PB[0:1, :], 1.0 / D, k.ROWSF,
                             ALU.mult, ALU.subtract).inc(k.s_pbc, 1).inc(k.vch, 1)
    sr = r.sem("ln_rstd")
    r.V.wait(k.vch)
    r.V.tensor_scalar(k.ROWSF, k.RROW, EPS_LN, None, ALU.add).inc(k.vch, 1)
    r.S.wait(k.vch)
    r.S.activation(k.RROW, k.ROWSF, AF.Ln).inc(k.sch, 1)
    r.S.wait(k.sch)
    r.S.activation(k.ROWSF, k.RROW, AF.Exp, scale=-0.5).inc(sr, 1)
    r.T.wait(k.s_pcc)
    r.T.wait(k.s_pdc)
    r.T.wait(sr)
    for n in range(2):
        r.T.matmul(k.PC[:, n * 512:(n + 1) * 512], k.ONES_F,
                   k.MROW[:, n * 512:(n + 1) * 512], start=True, stop=True
                   ).inc(k.s_pcp, 1)
        r.T.matmul(k.PD[:, n * 512:(n + 1) * 512], k.ONES_F,
                   k.ROWSF[:, n * 512:(n + 1) * 512], start=True, stop=True
                   ).inc(k.s_pdp, 1)
    MB = k.SMF[:, 0:1024]
    RB = k.SMF[:, 1024:2048]
    r.V.wait(k.s_pcp)
    r.V.tensor_copy(MB, k.PC).inc(k.s_pcc, 1).inc(k.vch, 1)
    r.V.wait(k.s_pdp)
    r.V.tensor_copy(RB, k.PD).inc(k.s_pdc, 1).inc(k.vch, 1)
    r.V.wait(k.vch)
    sdone = r.sem("ln_done")
    for c in range(NC):
        sl = slice(c * 1024, (c + 1) * 1024)
        tmp = k.ACT32[:, sl]
        r.V.tensor_tensor(tmp, k.X[:, sl], MB, ALU.subtract).inc(k.vch, 1)
        r.V.wait(k.vch)
        r.V.tensor_tensor(k.LNX[:, sl], tmp, RB, ALU.mult).inc(sdone, 1)



MOE_GROUPS = ([0, 1, 2, 3], [4, 5, 6, 7], [8, 9, 10, 11], [12, 13, 14, 15], [16, 17])
BIGM = 1.0e4


def ph_moe(k):
    r = k.r
    GWT = k.SMB[:, 2048:2304]
    SELF32 = _alias(k.nc, k.SMBh, f"SELF32_{r.bar.v}_{k.ws.i}", [16, 128], F32, 6656)[:, :]
    TMPB = k.SMB[:, 2048:3072]
    EB = k.SMT[:, 118:134]
    EBROW = k.SMT[0:1, 272:288]
    s_gwt = r.sem("moe_gwt")
    s_ebr = r.sem("moe_ebr")
    s_rt = r.sem("moe_rt")
    s_selfr = r.sem("moe_selfr")
    s_self = r.sem("moe_self")
    s_bcw = r.sem("moe_bcw")
    s_act = r.sem("moe_act")
    s_scfree = r.sem("moe_scfree")
    s_silfree = r.sem("moe_silfree")
    s_actfree = r.sem("moe_actfree")

    r.G.wait_val(r.bar, r.bar.v)
    r.G.dma_start(out=GWT.rearrange("p (c e) -> p c e", c=NC),
                  in_=k.d_gwt.rearrange("c p e -> p c e"),
                  allow_slow_non_contiguous=True).inc(s_gwt, 16)
    r.Y.wait_val(r.bar, r.bar.v)
    r.Y.dma_start(out=EBROW, in_=k.d_ebias).inc(s_ebr, 16)

    r.T.wait(s_gwt)
    r.T.wait(s_ebr)
    r.T.wait(k.s_pcc)
    r.T.matmul(k.PC[:, 0:16], k.ONES_F, EBROW, start=True, stop=True).inc(k.s_pcp, 1)
    r.V.wait(k.s_pcp)
    r.V.tensor_copy(EB, k.PC[:, 0:16]).inc(k.s_pcc, 1).inc(s_rt, 1)

    SC = k.SMT[:, 0:16]
    S4 = k.SMT[:, 16:32]
    PAIR = k.SMT[:, 32:56]
    GS = k.SMT[:, 56:60]
    MIN6 = k.SMT[:, 60:66]
    TH = k.SMT[:, 66:67]
    M0 = k.SMT[:, 68:84]
    MA = k.SMT[:, 84:100]
    MBv = k.SMT[:, 100:116]
    WR = k.SMT[:, 116:117]
    WRT = k.SMT[:, 134:135]
    SU = k.SMT[:, 135:136]
    pairs = [(0, 1), (0, 2), (0, 3), (1, 2), (1, 3), (2, 3)]
    for tt in range(8):
        P = (k.PA if tt % 2 == 0 else k.PB)[:, 0:16]
        sp = k.s_pap if tt % 2 == 0 else k.s_pbp
        sc_ = k.s_pac if tt % 2 == 0 else k.s_pbc
        r.T.wait(sc_)
        for c in range(NC):
            r.T.matmul(P, k.LNX[:, c * 1024 + tt * 128: c * 1024 + (tt + 1) * 128],
                       GWT[:, c * 16:(c + 1) * 16],
                       start=(c == 0), stop=(c == NC - 1))
        r.rlast("tensor").inc(sp, 1)
        r.S.wait(sp)
        r.S.wait(s_scfree)
        r.S.activation(SC, P, AF.Sigmoid).inc(sc_, 1).inc(s_rt, 1)
        r.V.wait(s_rt)
        r.V.tensor_tensor(S4, SC, EB, ALU.add).inc(k.vch, 1)
        r.V.wait(k.vch)
        for pi, (i, j) in enumerate(pairs):
            r.V.tensor_tensor(PAIR[:, pi * 4:(pi + 1) * 4],
                              S4[:, i:16:4], S4[:, j:16:4], ALU.add).inc(k.vch, 1)
        r.V.wait(k.vch)
        r.V.tensor_tensor(GS, PAIR[:, 0:4], PAIR[:, 4:8], ALU.max).inc(k.vch, 1)
        for pi in range(2, 6):
            r.V.wait(k.vch)
            r.V.tensor_tensor(GS, GS, PAIR[:, pi * 4:(pi + 1) * 4], ALU.max).inc(k.vch, 1)
        gpairs = [(0, 1), (0, 2), (0, 3), (1, 2), (1, 3), (2, 3)]
        r.V.wait(k.vch)
        for pi, (i, j) in enumerate(gpairs):
            r.V.tensor_tensor(MIN6[:, pi:pi + 1], GS[:, i:i + 1], GS[:, j:j + 1],
                              ALU.min).inc(k.vch, 1)
        r.V.wait(k.vch)
        r.V.reduce_max(TH, MIN6, axis=AX.X).inc(k.vch, 1)
        r.V.wait(k.vch)
        GM = MIN6[:, 0:4]
        r.V.tensor_scalar(GM, GS, TH, None, ALU.is_ge).inc(k.vch, 1)
        r.V.wait(k.vch)
        for i in range(4):
            r.V.tensor_copy(M0[:, i:16:4], GM).inc(k.vch, 1)
        r.V.wait(k.vch)
        r.V.tensor_scalar(MA, S4, BIGM, None, ALU.add).inc(k.vch, 1)
        r.V.wait(k.vch)
        r.V.tensor_tensor(MA, MA, M0, ALU.mult).inc(k.vch, 1)
        r.V.wait(k.vch)
        r.V.tensor_scalar(MA, MA, BIGM, None, ALU.subtract).inc(k.vch, 1)
        r.V.wait(k.vch)
        r.V.tensor_copy(MBv, MA).inc(k.vch, 1)
        for rnd in range(3):
            r.V.wait(k.vch)
            r.V.reduce_max(TH, MA, axis=AX.X).inc(k.vch, 1)
            r.V.wait(k.vch)
            E = PAIR[:, 0:16]
            r.V.tensor_scalar(E, MA, TH, None, ALU.is_equal).inc(k.vch, 1)
            r.V.wait(k.vch)
            r.V.scalar_tensor_tensor(MA, E, -2.0 * BIGM, MA, ALU.mult, ALU.add
                                     ).inc(k.vch, 1)
        r.V.wait(k.vch)
        r.V.reduce_max(TH, MA, axis=AX.X).inc(k.vch, 1)
        r.V.wait(k.vch)
        SEL = PAIR[:, 0:16]
        r.V.tensor_scalar(SEL, MBv, TH, None, ALU.is_ge).inc(k.vch, 1)
        r.V.wait(k.vch)
        WSEL = M0
        r.V.tensor_tensor(WSEL, SEL, SC, ALU.mult).inc(k.vch, 1).inc(s_scfree, 1)
        r.V.wait(k.vch)
        r.V.reduce_sum(SU, WSEL, axis=AX.X).inc(k.vch, 1)
        r.V.wait(k.vch)
        r.V.tensor_scalar(SU, SU, 1e-20, None, ALU.add).inc(k.vch, 1)
        r.V.wait(k.vch)
        r.V.reciprocal(WRT, SU).inc(k.vch, 1)
        r.V.wait(k.vch)
        r.V.tensor_scalar(MA, WSEL, WRT, 1.25, ALU.mult, ALU.mult).inc(k.vch, 1)
        r.V.wait(k.vch)
        r.T.wait(k.s_pcc)
        r.T.wait(k.vch)
        r.T.transpose(k.PC[0:16, 0:128], MA, k.IDENT).inc(k.s_pcp, 1)
        r.V.wait(k.s_pcp)
        r.V.tensor_copy(k.WEM[:, tt * 128:(tt + 1) * 128], k.PC[0:16, 0:128]
                        ).inc(k.s_pcc, 1).inc(s_rt, 1)

    for g, grp in enumerate(MOE_GROUPS):
        for ei, e in enumerate(grp):
            shared = e >= NE
            bslot = k.SMB[:, (ei % 2) * 1024:(ei % 2) * 1024 + 1024]
            if not shared:
                r.Y.wait(s_selfr)
                r.Y.dma_start(out=SELF32, in_=k.d_sel[e]).inc(s_self, 16)
                r.T.wait(s_self)
                r.T.wait(k.s_pcc)
                r.T.wait(s_rt)
                for n in range(2):
                    i = r.T.matmul(k.PC[:, n * 512:(n + 1) * 512], SELF32,
                                   k.WEM[:, n * 512:(n + 1) * 512],
                                   start=True, stop=True)
                i.inc(k.s_pcp, 1).inc(s_selfr, 1)
                r.V.wait(k.s_pcp)
                r.V.wait(s_act)
                r.V.tensor_copy(bslot, k.PC).inc(k.s_pcc, 1).inc(s_bcw, 1)
            else:
                r.V.wait(s_act)
                r.V.memset(bslot, 0.5).inc(s_bcw, 1)
            for fc in range(8):
                bu = k.ws.load(k.d_gu[e * 8 + fc], 4096)
                PG, sgp, sgc = (k.PA, k.s_pap, k.s_pac) if fc % 2 == 0 else \
                               (k.PC, k.s_pcp, k.s_pcc)
                PU, sup, suc = (k.PB, k.s_pbp, k.s_pbc) if fc % 2 == 0 else \
                               (k.PD, k.s_pdp, k.s_pdc)
                bu.acquire(r.T)
                r.T.wait(sgc)
                for c in range(NC):
                    for n in range(2):
                        r.T.matmul(PG[:, n * 512:(n + 1) * 512],
                                   bu.ap[:, c * 128:(c + 1) * 128],
                                   k.LNX[:, c * 1024 + n * 512: c * 1024 + n * 512 + 512],
                                   start=(c == 0), stop=(c == NC - 1))
                r.rlast("tensor").inc(sgp, 1)
                r.T.wait(suc)
                for c in range(NC):
                    for n in range(2):
                        r.T.matmul(PU[:, n * 512:(n + 1) * 512],
                                   bu.ap[:, 2048 + c * 128: 2048 + (c + 1) * 128],
                                   k.LNX[:, c * 1024 + n * 512: c * 1024 + n * 512 + 512],
                                   start=(c == 0), stop=(c == NC - 1))
                i = r.rlast("tensor").inc(sup, 1)
                bu.release(i)
                r.S.wait(sgp)
                r.S.wait(s_silfree)
                r.S.activation(k.SIL, PG, AF.Silu).inc(sgc, 1).inc(k.sch, 1)
                r.V.wait(k.sch)
                r.V.wait(sup)
                r.V.wait(s_act)
                r.V.tensor_tensor(TMPB, k.SIL, PU, ALU.mult
                                  ).inc(suc, 1).inc(k.vch, 1).inc(s_silfree, 1)
                r.V.wait(k.vch)
                r.V.wait(s_bcw)
                aslot = k.ACT[:, ei * 8192 + fc * 1024: ei * 8192 + (fc + 1) * 1024]
                r.V.tensor_tensor(aslot, TMPB, bslot, ALU.mult).inc(s_act, 1)
        nacts = len(grp) * 8
        for hc in range(NC):
            if len(grp) == 4:
                bu = k.ws.load(k.d_down[g * NC + hc], 4096)
            else:
                bu = k.ws.load(k.d_downsh[hc], 2048)
            PDn, sdp, sdc = (k.PA, k.s_pap, k.s_pac) if hc % 2 == 0 else \
                            (k.PC, k.s_pcp, k.s_pcc)
            bu.acquire(r.T)
            r.T.wait(sdc)
            if hc == 0:
                r.T.wait(s_act)
            nmm = 0
            for ei, e in enumerate(grp):
                for fc in range(8):
                    for n in range(2):
                        r.T.matmul(PDn[:, n * 512:(n + 1) * 512],
                                   bu.ap[:, (ei * 8 + fc) * 128:(ei * 8 + fc + 1) * 128],
                                   k.ACT[:, ei * 8192 + fc * 1024 + n * 512:
                                         ei * 8192 + fc * 1024 + n * 512 + 512],
                                   start=(ei == 0 and fc == 0),
                                   stop=(ei == len(grp) - 1 and fc == 7))
            i = r.rlast("tensor").inc(sdp, 1)
            bu.release(i)
            if hc == NC - 1:
                i.inc(s_actfree, 1)
            r.V.wait(sdp)
            xsl = k.X[:, hc * 1024:(hc + 1) * 1024]
            r.V.tensor_tensor(xsl, xsl, PDn, ALU.add).inc(sdc, 1).inc(k.vch, 1)
            r.V.wait(k.vch)



def ph_attn0(k):
    r = k.r
    s_br = r.sem("a0_brow")
    s_proj = r.sem("a0_proj")
    s_kstage = r.sem("a0_kstage")
    s_vstage = r.sem("a0_vstage")
    s_lmax = r.sem("a0_lmax")
    s_lmd = r.sem("a0_lmd")
    s_ksq = r.sem("a0_ksq")
    s_ccK = r.sem("a0_ccK")
    s_ccV = r.sem("a0_ccV")
    s_maxk = r.sem("a0_maxk")
    s_exp = r.sem("a0_exp")
    s_pvd = r.sem("a0_pvd")
    s_po = r.sem("a0_po")
    s_zr = r.sem("a0_zr")
    s_zb = r.sem("a0_zb")
    s_evac = r.sem("a0_evac")
    s_vstg = r.sem("a0_vstg")
    KTMP = k.SMB[:, 2048:3072]
    KSQ = k.SMB[:, 3072:4096]
    VTMPF = k.ACT32[:, 8192:9216]
    VTT = [k.SMB[:, 1024:1152], k.SMB[:, 1152:1280]]
    ZB = k.SMB[:, 2048:2560]

    r.G.wait_val(r.bar, r.bar.v)
    r.G.dma_start(out=k.BROW, in_=k.d_l0_bq).inc(s_br, 16)
    r.G.dma_start(out=k.BROW2, in_=k.d_l0_bk).inc(s_br, 16)

    def proj_fm(dunits, brow, h, ping, out_cb):
        bu = k.ws.load(dunits[h], 2048)
        P, sp, sc = (k.PA, k.s_pap, k.s_pac) if ping else (k.PB, k.s_pbp, k.s_pbc)
        bu.acquire(r.T)
        r.T.wait(sc)
        r.T.wait(s_br)
        for c in range(NC):
            for n in range(2):
                r.T.matmul(P[:, n * 512:(n + 1) * 512],
                           bu.ap[:, c * 128:(c + 1) * 128],
                           k.LNX[:, c * 1024 + n * 512: c * 1024 + n * 512 + 512],
                           start=(c == 0), stop=False)
        for n in range(2):
            r.T.matmul(P[:, n * 512:(n + 1) * 512],
                       brow[0:1, h * 128:(h + 1) * 128],
                       k.ONES_BF, start=False, stop=True)
        i = r.rlast("tensor").inc(sp, 1).inc(s_proj, 1)
        bu.release(i)
        out_cb(P, sp, sc)

    for h in range(NH):
        def evq(P, sp, sc, h=h):
            r.V.wait(sp)
            r.V.tensor_copy(k.ACT[:, h * 1024:(h + 1) * 1024], P).inc(sc, 1)
        proj_fm(k.d_l0_wq, k.BROW, h, h % 2 == 0, evq)

    for h in range(NH):
        def evk(P, sp, sc, h=h):
            r.V.wait(sp)
            r.V.wait(s_kstage)
            r.V.wait(s_ksq)
            r.V.tensor_copy(KTMP, P).inc(sc, 1).inc(k.vch, 1)
            r.S.wait(k.vch)
            r.S.wait(s_ksq)
            r.S.activation(KSQ, KTMP, AF.Square).inc(s_ksq, 1)
            r.Y.wait(k.vch)
            r.Y.dma_start(out=k.d_l0k_in.ap()[h * 128:(h + 1) * 128, :], in_=KTMP
                          ).inc(s_kstage, 16)
            r.T.wait(k.s_pcc)
            r.T.wait(s_ksq)
            for n in range(2):
                r.T.matmul(k.PC[0:1, n * 512:(n + 1) * 512], k.ONESCB[:, 0:1],
                           KSQ[:, n * 512:(n + 1) * 512], start=True, stop=True)
            r.rlast("tensor").inc(k.s_pcp, 1).inc(s_ksq, 1)
            r.V.wait(k.s_pcp)
            r.V.reduce_max(k.SMT[0:1, 224 + h:225 + h], k.PC[0:1, 0:1024],
                           axis=AX.X).inc(k.s_pcc, 1).inc(s_lmax, 1)
        proj_fm(k.d_l0_wk, k.BROW2, h, h % 2 == 0, evk)

    r.V.wait(s_lmax)
    r.V.memset(k.ROWSF, 0.0).inc(k.vch, 1)
    r.V.wait(k.vch)
    r.V.tensor_copy(k.ROWSF[0:1, 0:16], k.SMT[0:1, 224:240]).inc(s_lmax, 1)
    r.G.wait_val(s_lmax, s_lmax.v)
    r.G.dma_start(out=k.d_l0k_in.ap()[2048:2049, :], in_=k.ROWSF).inc(s_lmd, 16)

    r.G.wait_val(s_proj, s_proj.v)
    r.G.dma_start(out=k.BROW, in_=k.d_l0_bv).inc(s_br, 16)

    ptile = [r.sem("a0_pt0"), r.sem("a0_pt1")]
    ptilec = [r.sem("a0_pt0c"), r.sem("a0_pt1c")]
    vstg = [r.sem("a0_vs0"), r.sem("a0_vs1")]
    for dv in range(NH):
        def evv(P, sp, sc, dv=dv):
            r.V.wait(sp)
            r.V.wait(s_vstg)
            r.V.tensor_copy(VTMPF, P).inc(sc, 1).inc(k.vch, 1)
            r.T.wait(k.vch)
            for tt in range(8):
                sl = tt % 2
                r.T.wait(ptilec[sl])
                i = r.T.transpose(k.PD[:, sl * 512:sl * 512 + 128],
                                  VTMPF[:, tt * 128:(tt + 1) * 128], k.IDENT
                                  ).inc(ptile[sl], 1)
                if tt >= 6:
                    i.inc(s_vstg, 1)
                r.V.wait(ptile[sl])
                r.V.wait(vstg[sl])
                r.V.tensor_copy(VTT[sl], k.PD[:, sl * 512:sl * 512 + 128]
                                ).inc(ptilec[sl], 1).inc(k.vch, 1)
                r.Y.wait(k.vch)
                r.Y.dma_start(out=k.d_l0v_in.ap()[tt * 128:(tt + 1) * 128,
                                                  dv * 128:(dv + 1) * 128],
                              in_=VTT[sl]).inc(vstg[sl], 16)
        proj_fm(k.d_l0_wv, k.BROW, dv, dv % 2 == 0, evv)

    r.G.wait_val(s_kstage, s_kstage.v)
    r.G.wait(s_lmd)
    r.G.collective_compute("AllGather", ALU.bypass,
                           replica_groups=[[0, 1, 2, 3], [4, 5, 6, 7]],
                           ins=[k.d_l0k_in.ap()], outs=[k.d_l0k_out.ap()]
                           ).inc(s_ccK, 1)
    r.G.wait(vstg[0])
    r.G.wait(vstg[1])
    r.G.collective_compute("AllGather", ALU.bypass,
                           replica_groups=[[0, 1, 2, 3], [4, 5, 6, 7]],
                           ins=[k.d_l0v_in.ap()], outs=[k.d_l0v_out.ap()]
                           ).inc(s_ccV, 1)

    r.Y.wait(s_ccK)
    for rr in range(4):
        r.Y.dma_start(out=k.MAXB[0:1, rr * 16:(rr + 1) * 16],
                      in_=k.d_l0k_out.ap()[rr * 2049 + 2048: rr * 2049 + 2049, 0:16]
                      ).inc(s_maxk, 16)
    r.V.wait(s_maxk)
    r.V.wait(s_lmd)
    GAT = k.ROWSF[0:1, 0:64]
    r.V.tensor_copy(GAT, k.MAXB).inc(k.vch, 1)
    mk1, mk2 = k.ROWSF[0:1, 64:80], k.ROWSF[0:1, 80:96]
    r.V.wait(k.vch)
    r.V.tensor_tensor(mk1, k.ROWSF[0:1, 0:16], k.ROWSF[0:1, 16:32], ALU.max
                      ).inc(k.vch, 1)
    r.V.wait(k.vch)
    r.V.tensor_tensor(mk2, mk1, k.ROWSF[0:1, 32:48], ALU.max).inc(k.vch, 1)
    r.V.wait(k.vch)
    r.V.tensor_tensor(mk1, mk2, k.ROWSF[0:1, 48:64], ALU.max).inc(k.vch, 1)

    QSQ = k.SMB[:, 0:1024]
    for h in range(NH):
        r.S.wait(s_vstg)
        r.S.wait(k.s_pcc)
        r.S.activation(QSQ, k.ACT[:, h * 1024:(h + 1) * 1024], AF.Square
                       ).inc(k.sch, 1)
        r.T.wait(k.sch)
        r.T.wait(k.s_pcc)
        for n in range(2):
            r.T.matmul(k.PC[0:1, n * 512:(n + 1) * 512], k.ONESCB[:, 0:1],
                       QSQ[:, n * 512:(n + 1) * 512], start=True, stop=True)
        r.rlast("tensor").inc(k.s_pcp, 1)
        r.V.wait(k.s_pcp)
        r.V.wait(s_lmd)
        r.V.reduce_max(k.ROWSF[0:1, 160 + h:161 + h], k.PC[0:1, 0:1024], axis=AX.X
                       ).inc(k.s_pcc, 1).inc(k.vch, 1)
    r.S.wait(k.vch)
    r.S.activation(k.ROWSF[0:1, 96:112], k.ROWSF[0:1, 160:176], AF.Sqrt).inc(k.sch, 1)
    r.S.activation(k.ROWSF[0:1, 112:128], mk1, AF.Sqrt).inc(k.sch, 1)
    r.V.wait(k.sch)
    r.V.tensor_tensor(k.ROWSF[0:1, 128:144], k.ROWSF[0:1, 96:112],
                      k.ROWSF[0:1, 112:128], ALU.mult).inc(k.vch, 1)
    r.V.wait(k.vch)
    r.V.tensor_scalar(k.ROWSF[0:1, 144:160], k.ROWSF[0:1, 128:144],
                      -INV_SQRT_HD, None, ALU.mult).inc(k.vch, 1)
    NEGMB = k.SMT[:, 102:118]
    r.T.wait(k.vch)
    r.T.wait(k.s_pcc)
    r.T.matmul(k.PC[:, 0:16], k.ONES_F, k.ROWSF[0:1, 144:160], start=True, stop=True
               ).inc(k.s_pcp, 1)
    r.V.wait(k.s_pcp)
    r.V.tensor_copy(NEGMB, k.PC[:, 0:16]).inc(k.s_pcc, 1).inc(k.vch, 1)

    r.Y.wait(s_ccK)
    r.Y.wait(s_ccV)
    PBUF = [k.SMB[:, 0:1024], k.SMB[:, 1024:2048]]
    pv_done = [None, None]
    nm_done_v = k.vch.v
    for h in range(NH):
        ksrc = []
        for rr in range(4):
            ksrc.append((None, k.d_l0k_out.ap()[rr * 2049 + h * 128:
                                                rr * 2049 + (h + 1) * 128, :]))
        first, span = k.ws.ring[k.ws.i % 3]
        buK = None
        sl0 = k.ws.slots[first]
        for s_ in k.ws.slots[first:first + span]:
            if s_.release_at is not None:
                r.Y.wait_val(s_.release_at[0], s_.release_at[1])
        ldy = sl0.loaded["sync"]
        for rr in range(4):
            r.Y.dma_start(out=k.WBUF[:, first * 2048 + rr * 1024:
                                     first * 2048 + (rr + 1) * 1024],
                          in_=ksrc[rr][1]).inc(ldy, 16)
        buK = BufUse(k.WBUF[:, first * 2048:first * 2048 + 4096], ldy,
                     ldy.v, sl0.freed, k.ws.slots[first:first + span], k.ws)
        k.ws.i += 1
        buV = k.ws.load(
            k.d_l0v_out.ap()[:, h * 128:(h + 1) * 128].rearrange(
                "(kt p) j -> p kt j", p=128), 4096, queue="sync")
        for qb in range(2):
            Q = k.ACT[:, h * 1024 + qb * 512: h * 1024 + qb * 512 + 512]
            first_pv = True
            for kg in range(16):
                sl = kg % 2
                Ps, sps, spc = (k.PA, k.s_pap, k.s_pac) if sl == 0 else \
                               (k.PB, k.s_pbp, k.s_pbc)
                if kg == 0 and qb == 0:
                    buK.acquire(r.T)
                    buV.acquire(r.T)
                r.T.wait(spc)
                for kt2 in range(2):
                    kt = kg * 2 + kt2
                    r.T.matmul(Ps[:, kt2 * 512:(kt2 + 1) * 512],
                               buK.ap[:, kt * 128:(kt + 1) * 128], Q,
                               start=True, stop=True)
                r.rlast("tensor").inc(sps, 1)
                r.S.wait(sps)
                if h == 0 and qb == 0 and kg == 0:
                    r.S.wait_val(k.vch, nm_done_v)
                if pv_done[sl] is not None:
                    r.S.wait_val(s_pvd, pv_done[sl])
                r.S.activation(PBUF[sl], Ps, AF.Exp, bias=NEGMB[:, h:h + 1],
                               scale=INV_SQRT_HD).inc(spc, 1).inc(s_exp, 1)
                r.T.wait(s_exp)
                if first_pv:
                    r.T.wait(k.s_pcc)
                    first_pv = False
                for kt2 in range(2):
                    kt = kg * 2 + kt2
                    r.T.matmul(k.PC[:, 0:512], buV.ap[:, kt * 128:(kt + 1) * 128],
                               PBUF[sl][:, kt2 * 512:(kt2 + 1) * 512],
                               start=(kg == 0 and kt2 == 0),
                               stop=(kg == 15 and kt2 == 1))
                    r.T.matmul(k.PC[0:1, 512:1024], k.ONESCB[:, 0:1],
                               PBUF[sl][:, kt2 * 512:(kt2 + 1) * 512],
                               start=(kg == 0 and kt2 == 0),
                               stop=(kg == 15 and kt2 == 1))
                r.rlast("tensor").inc(s_pvd, 1)
                pv_done[sl] = s_pvd.v
            r.rlast("tensor").inc(k.s_pcp, 1)
            if qb == 1:
                i = r.rlast("tensor")
                buK.release(i)
                buV.release(i)
            r.V.wait(k.s_pcp)
            r.V.reciprocal(k.ZROW[0:1, 0:512], k.PC[0:1, 512:1024]).inc(s_zr, 1)
            r.T.wait(s_zr)
            r.T.matmul(k.PC[:, 512:1024], k.ONES_F, k.ZROW[0:1, 0:512],
                       start=True, stop=True).inc(s_zb, 1)
            r.V.wait(s_zb)
            r.V.tensor_copy(ZB, k.PC[:, 512:1024]).inc(k.vch, 1)
            r.V.wait(k.vch)
            r.V.tensor_tensor(k.LNX[:, h * 1024 + qb * 512:
                                    h * 1024 + qb * 512 + 512],
                              k.PC[:, 0:512], ZB, ALU.mult
                              ).inc(k.s_pcc, 1).inc(s_evac, 1)

    r.G.wait_val(s_proj, s_proj.v)
    r.G.dma_start(out=k.BROW, in_=k.d_l0_bo).inc(s_br, 16)
    for m in range(NH):
        def evo(P, sp, sc, m=m):
            r.V.wait(sp)
            xsl = k.X[:, m * 1024:(m + 1) * 1024]
            r.V.tensor_tensor(xsl, xsl, P, ALU.add).inc(sc, 1)
        if m == 0:
            r.T.wait(s_evac)
        proj_wo(k, k.d_l0_wo, k.BROW, m, m % 2 == 0, evo, s_br, s_proj)


def proj_wo(k, dunits, brow, h, ping, out_cb, s_br, s_proj):
    r = k.r
    bu = k.ws.load(dunits[h], 2048)
    P, sp, sc = (k.PA, k.s_pap, k.s_pac) if ping else (k.PB, k.s_pbp, k.s_pbc)
    bu.acquire(r.T)
    r.T.wait(sc)
    r.T.wait(s_br)
    for c in range(NC):
        for n in range(2):
            r.T.matmul(P[:, n * 512:(n + 1) * 512],
                       bu.ap[:, c * 128:(c + 1) * 128],
                       k.LNX[:, c * 1024 + n * 512: c * 1024 + n * 512 + 512],
                       start=(c == 0), stop=False)
    for n in range(2):
        r.T.matmul(P[:, n * 512:(n + 1) * 512],
                   brow[0:1, h * 128:(h + 1) * 128],
                   k.ONES_BF, start=False, stop=True)
    i = k.r.rlast("tensor").inc(sp, 1).inc(s_proj, 1)
    bu.release(i)
    out_cb(P, sp, sc)



def ph_prep1(k):
    r = k.r
    if getattr(k, "_prep_done", False):
        return
    k._prep_done = True
    s_q = r.sem("p1_q")
    s_sgfree = r.sem("p1_sgfree")
    A0 = k.SMT[:, 224:225]
    TMPC = k.SMT[:, 227:228]
    ONEC = k.SMT[:, 226:227]
    THR = k.SMT[:, 229:230]
    AB = k.SMF
    SG = k.SMB[:, 0:2048]
    MK = k.SMB[:, 2048:4096]
    WQ = [k.ACT[:, 0:2048], k.ACT[:, 2048:4096]]
    s_wq = [r.sem("p1_wq0"), r.sem("p1_wq1")]

    def units():
        for u in range(DFF // 128):
            yield k.d_w1[u], ("w1", u)
        for m in range(NC):
            for q in range(4):
                yield k.d_w2[m][:, q * 2048:(q + 1) * 2048], ("w2", m, q)

    A1 = k.SMT[:, 225:226]
    THR2 = k.SMT[:, 230:231]
    r.G.wait_val(r.bar, r.bar.v)
    s_onec = r.sem("p1_onec")
    r.Y.wait_val(r.bar, r.bar.v)
    r.Y.dma_start(out=ONEC, in_=k.d_onescol).inc(s_onec, 16)
    r.V.memset(A0, 0.0).inc(k.vch, 1)
    r.V.memset(A1, 0.0).inc(k.vch, 1)
    for i, (src, tag) in enumerate(units()):
        bu = k.ws32.load(src, 2048, queue="sync")
        bu.acquire(r.V)
        r.V.tensor_reduce(TMPC, bu.ap, AX.X, ALU.add, apply_absolute_value=True
                          ).inc(k.vch, 1)
        bu.release(r.rlast("vector"))
        r.V.wait(k.vch)
        acc = A0 if tag[0] == "w1" else A1
        r.V.tensor_tensor(acc, acc, TMPC, ALU.add).inc(k.vch, 1)
        r.V.wait(k.vch)
    for acc, thr in ((A0, THR), (A1, THR2)):
        r.T.wait(k.vch)
        r.T.wait(s_onec)
        r.T.wait(k.s_pcc)
        r.T.matmul(k.PC[0:1, 0:1], acc, ONEC, start=True, stop=True).inc(k.s_pcp, 1)
        r.V.wait(k.s_pcp)
        r.V.tensor_scalar(k.SMT[0:1, 228:229], k.PC[0:1, 0:1], 0.7 / float(DFF * D),
                          None, ALU.mult).inc(k.s_pcc, 1).inc(k.vch, 1)
        r.T.wait(k.vch)
        r.T.wait(k.s_pcc)
        r.T.matmul(k.PC[:, 0:1], k.ONES_F, k.SMT[0:1, 228:229], start=True, stop=True
                   ).inc(k.s_pcp, 1)
        r.V.wait(k.s_pcp)
        r.V.tensor_copy(thr, k.PC[:, 0:1]).inc(k.s_pcc, 1).inc(k.vch, 1)

    NTHR = k.SMT[:, 231:232]
    NTHR2 = k.SMT[:, 232:233]
    r.V.wait(k.vch)
    r.V.tensor_scalar(NTHR, THR, -1.0, None, ALU.mult).inc(k.vch, 1)
    r.V.tensor_scalar(NTHR2, THR2, -1.0, None, ALU.mult).inc(k.vch, 1)
    r.V.wait(k.vch)
    for i, (src, tag) in enumerate(units()):
        bu = k.ws32.load(src, 2048, queue="sync")
        sl = i % 2
        w1 = tag[0] == "w1"
        sc2 = 1.0 if w1 else 0.5
        bu.acquire(r.V)
        r.V.tensor_scalar(SG, bu.ap, THR if w1 else THR2, sc2,
                          ALU.is_ge, ALU.mult).inc(k.vch, 1)
        i2 = r.V.tensor_scalar(MK, bu.ap, NTHR if w1 else NTHR2, sc2,
                               ALU.is_le, ALU.mult)
        i2.inc(k.vch, 1)
        bu.release(i2)
        r.V.wait(k.vch)
        r.V.wait(s_wq[sl])
        r.V.tensor_tensor(WQ[sl], SG, MK, ALU.subtract).inc(k.vch, 1)
        r.Y.wait(k.vch)
        if w1:
            dst = k.d_w1q.ap()[tag[1]]
        else:
            dst = k.d_w2q.ap()[tag[1]][:, tag[2] * 2048:(tag[2] + 1) * 2048]
        r.Y.dma_start(out=dst, in_=WQ[sl]).inc(s_wq[sl], 16)
    r.V.wait(s_wq[0])
    r.V.wait(s_wq[1])



def ph_ffn1(k):
    r = k.r
    if not getattr(k, "_b2h_done", False):
        k._b2h_done = True
        r.V.wait_val(r.bar, r.bar.v)
        r.V.tensor_scalar(k.SMT[:, 320:336], k.SMT[:, 200:216], 0.5, None,
                          ALU.mult).inc(k.vch, 1)
        k._b2h_v = k.vch.v
    B2H = k.SMT[:, 320:336]
    s_h = r.sem("f1_h")
    for tt2 in range(2):
        toff = tt2 * 512
        for fc in range(DFF // 128):
            bu = k.ws.load(k.d_w1q.ap()[fc], 2048, queue="sync")
            P, sp, sc = (k.PA, k.s_pap, k.s_pac) if fc % 2 == 0 else \
                         (k.PB, k.s_pbp, k.s_pbc)
            bu.acquire(r.T)
            r.T.wait(sc)
            if fc == 0:
                r.T.wait(s_h)
            for c in range(NC):
                r.T.matmul(P[:, 0:512], bu.ap[:, c * 128:(c + 1) * 128],
                           k.LNX[:, c * 1024 + toff: c * 1024 + toff + 512],
                           start=(c == 0), stop=(c == NC - 1))
            i = r.rlast("tensor").inc(sp, 1)
            bu.release(i)
            r.S.wait(sp)
            r.S.activation(k.ACT[:, fc * 512:(fc + 1) * 512], P[:, 0:512], AF.Gelu,
                           bias=k.SMT[:, 136 + fc:137 + fc]).inc(sc, 1).inc(s_h, 1)
        r.T.wait(s_h)
        for m in range(NC):
            P, sp, sc = (k.PC[:, 0:512], k.s_pcp, k.s_pcc) if m % 2 == 0 else \
                         (k.PD[:, 0:512], k.s_pdp, k.s_pdc)
            r.T.wait(sc)
            for q in range(4):
                bu = k.ws.load(k.d_w2q.ap()[m][:, q * 2048:(q + 1) * 2048], 2048,
                               queue="sync")
                bu.acquire(r.T)
                for c in range(NC):
                    r.T.matmul(P, bu.ap[:, c * 128:(c + 1) * 128],
                               k.ACT[:, (q * NC + c) * 512:(q * NC + c + 1) * 512],
                               start=(q == 0 and c == 0), stop=(q == 3 and c == NC - 1))
                bu.release(r.rlast("tensor"))
            r.rlast("tensor").inc(sp, 1)
            r.V.wait(sp)
            r.V.wait_val(k.vch, k._b2h_v)
            xsl = k.X[:, m * 1024 + toff: m * 1024 + toff + 512]
            r.V.scalar_tensor_tensor(xsl, P, B2H[:, m:m + 1], xsl,
                                     ALU.add, ALU.add).inc(sc, 1).inc(s_h, 1)



def ph_attn1(k):
    r = k.r
    s_br = r.sem("a1_brow")
    s_proj = r.sem("a1_proj")
    s_ar = r.sem("a1_ar")
    s_arb = r.sem("a1_arb")
    s_cc = r.sem("a1_cc")
    s_kv = r.sem("a1_kv")
    s_dec = r.sem("a1_dec")
    KF = k.ACT32[:, 0:1024]
    VF = k.ACT32[:, 1024:2048]
    KT = k.SMB[:, 2048:3072]
    VT = k.SMB[:, 3072:4096]
    KSUM = k.SMT[:, 0:16]

    r.G.wait_val(r.bar, r.bar.v)
    r.G.dma_start(out=k.BROW, in_=k.d_l1_bv).inc(s_br, 16)
    r.G.dma_start(out=k.BROW2, in_=k.d_l1_bk).inc(s_br, 16)
    r.Y.wait_val(r.bar, r.bar.v)
    r.Y.dma_start(out=k.ROWSF, in_=k.d_decay_row).inc(s_dec, 16)

    def proj(units, brow, h, P, sp, sc):
        bu = k.ws.load(units[h], 2048)
        bu.acquire(r.T)
        r.T.wait(sc)
        r.T.wait(s_br)
        for c in range(NC):
            for n in range(2):
                r.T.matmul(P[:, n * 512:(n + 1) * 512],
                           bu.ap[:, c * 128:(c + 1) * 128],
                           k.LNX[:, c * 1024 + n * 512: c * 1024 + n * 512 + 512],
                           start=(c == 0), stop=False)
        for n in range(2):
            r.T.matmul(P[:, n * 512:(n + 1) * 512],
                       brow[0:1, h * 128:(h + 1) * 128],
                       k.ONES_BF, start=False, stop=True)
        i = r.rlast("tensor").inc(sp, 1).inc(s_proj, 1)
        bu.release(i)

    ptile = [r.sem("a1_pt0"), r.sem("a1_pt1")]
    ptilec = [r.sem("a1_pt0c"), r.sem("a1_pt1c")]
    for h in range(NH):
        proj(k.d_l1_wk, k.BROW2, h, k.PA, k.s_pap, k.s_pac)
        r.S.wait(k.s_pap)
        r.S.wait(s_kv)
        r.S.activation(KF, k.PA, AF.Relu).inc(k.s_pac, 1).inc(k.sch, 1)
        proj(k.d_l1_wv, k.BROW, h, k.PB, k.s_pbp, k.s_pbc)
        r.S.wait(k.s_pbp)
        r.S.wait(s_kv)
        r.S.activation(VF, k.PB, AF.Copy).inc(k.s_pbc, 1).inc(k.sch, 1)
        r.T.wait(k.sch)
        for tt in range(8):
            sl = tt % 2
            r.T.wait(ptilec[sl])
            r.T.transpose(k.PD[:, sl * 512:sl * 512 + 128],
                          KF[:, tt * 128:(tt + 1) * 128], k.IDENT).inc(ptile[sl], 1)
            r.V.wait(ptile[sl])
            r.V.wait(s_dec)
            r.V.tensor_scalar(KT[:, tt * 128:(tt + 1) * 128],
                              k.PD[:, sl * 512:sl * 512 + 128],
                              k.SMT[:, 216 + tt:217 + tt], None, ALU.mult
                              ).inc(ptilec[sl], 1).inc(k.vch, 1)
        for tt in range(8):
            sl = tt % 2
            r.T.wait(ptilec[sl])
            i = r.T.transpose(k.PD[:, sl * 512:sl * 512 + 128],
                              VF[:, tt * 128:(tt + 1) * 128], k.IDENT
                              ).inc(ptile[sl], 1)
            if tt >= 6:
                i.inc(s_kv, 1)
            r.V.wait(ptile[sl])
            r.V.tensor_copy(VT[:, tt * 128:(tt + 1) * 128],
                            k.PD[:, sl * 512:sl * 512 + 128]
                            ).inc(ptilec[sl], 1).inc(k.vch, 1)
        r.T.wait(k.s_pcc)
        r.T.wait(k.vch)
        for tt in range(8):
            r.T.matmul(k.PC[:, 0:128], KT[:, tt * 128:(tt + 1) * 128],
                       VT[:, tt * 128:(tt + 1) * 128],
                       start=(tt == 0), stop=(tt == 7))
        for tt in range(8):
            r.T.matmul(k.PC[:, 128:129], KT[:, tt * 128:(tt + 1) * 128],
                       k.ONESCB[:, 0:1], start=(tt == 0), stop=(tt == 7))
        r.rlast("tensor").inc(k.s_pcp, 1)
        r.V.wait(k.s_pcp)
        r.V.tensor_copy(k.KVS[:, h * 128:(h + 1) * 128], k.PC[:, 0:128]
                        ).inc(k.vch, 1)
        r.V.tensor_copy(KSUM[:, h:h + 1], k.PC[:, 128:129]
                        ).inc(k.s_pcc, 1).inc(k.vch, 1)

    r.Y.wait(k.vch)
    r.Y.dma_start(out=k.d_l1cc_in.ap()[:, 0:2048], in_=k.KVS).inc(s_cc, 16)
    r.Y.dma_start(out=k.d_l1cc_in.ap()[:, 2048:2064], in_=KSUM).inc(s_cc, 16)
    r.G.wait_val(s_cc, s_cc.v)
    r.G.collective_compute("AllReduce", ALU.add,
                           replica_groups=[[0, 1, 2, 3], [4, 5, 6, 7]],
                           ins=[k.d_l1cc_in.ap()], outs=[k.d_l1cc_out.ap()]
                           ).inc(s_ar, 1)

    r.G.wait_val(s_proj, s_proj.v)
    r.G.dma_start(out=k.BROW2, in_=k.d_l1_bq).inc(s_br, 16)
    DECB = k.SMB[:, 0:1024]
    QTMP = k.SMB[:, 1024:2048]
    r.T.wait(s_dec)
    r.T.wait(k.s_pdc)
    r.T.wait(ptilec[0])
    r.T.wait(ptilec[1])
    for n in range(2):
        r.T.matmul(k.PD[:, n * 512:(n + 1) * 512], k.ONES_F,
                   k.ROWSF[0:1, n * 512:(n + 1) * 512], start=True, stop=True)
    r.rlast("tensor").inc(k.s_pdp, 1)
    r.V.wait(k.s_pdp)
    r.V.wait(s_kv)
    r.V.tensor_copy(DECB, k.PD).inc(k.s_pdc, 1).inc(k.vch, 1)
    for h in range(NH):
        P, sp, sc = (k.PA, k.s_pap, k.s_pac) if h % 2 == 0 else \
                    (k.PB, k.s_pbp, k.s_pbc)
        proj(k.d_l1_wq, k.BROW2, h, P, sp, sc)
        r.S.wait(sp)
        r.S.wait(k.vch)
        r.S.activation(QTMP, P, AF.Relu).inc(sc, 1).inc(k.sch, 1)
        r.V.wait(k.sch)
        r.V.tensor_tensor(k.ACT[:, h * 1024:(h + 1) * 1024], QTMP, DECB,
                          ALU.mult).inc(k.vch, 1)

    KVB = k.SMB[:, 2048:4096]
    KSB = k.ONESCB[:, 2:18]
    r.G.wait_val(s_ar, s_ar.v)
    r.G.dma_start(out=KVB, in_=k.d_l1cc_out.ap()[:, 0:2048]).inc(s_arb, 16)
    r.G.dma_start(out=KSB, in_=k.d_l1cc_out.ap()[:, 2048:2064]).inc(s_arb, 16)

    s_z = r.sem("a1_z")
    ZBB = k.SMB[:, 0:1024]
    for h in range(NH):
        P, sp, sc = (k.PA, k.s_pap, k.s_pac) if h % 2 == 0 else \
                    (k.PB, k.s_pbp, k.s_pbc)
        r.T.wait(sc)
        r.T.wait(s_arb)
        Q0 = k.ACT[:, h * 1024: h * 1024 + 512]
        Q1 = k.ACT[:, h * 1024 + 512: h * 1024 + 1024]
        r.T.wait(k.vch)
        for n, Qn in enumerate((Q0, Q1)):
            r.T.matmul(P[:, n * 512:(n + 1) * 512], KVB[:, h * 128:(h + 1) * 128],
                       Qn, start=True, stop=True)
        r.rlast("tensor").inc(sp, 1)
        r.T.wait(k.s_pcc)
        for n, Qn in enumerate((Q0, Q1)):
            r.T.matmul(k.PC[0:1, n * 512:(n + 1) * 512], KSB[:, h:h + 1],
                       Qn, start=True, stop=True)
        r.rlast("tensor").inc(k.s_pcp, 1)
        r.V.wait(k.s_pcp)
        r.V.tensor_scalar(k.ROWSF, k.PC[0:1, 0:1024], 1e-6, None, ALU.add
                          ).inc(k.s_pcc, 1).inc(k.vch, 1)
        r.V.wait(k.vch)
        r.V.reciprocal(k.ZROW, k.ROWSF).inc(s_z, 1)
        r.T.wait(s_z)
        r.T.wait(k.s_pdc)
        for n in range(2):
            r.T.matmul(k.PD[:, n * 512:(n + 1) * 512], k.ONES_F,
                       k.ZROW[0:1, n * 512:(n + 1) * 512], start=True, stop=True)
        r.rlast("tensor").inc(k.s_pdp, 1)
        r.V.wait(k.s_pdp)
        r.V.tensor_copy(ZBB, k.PD).inc(k.s_pdc, 1).inc(k.vch, 1)
        r.V.wait(k.vch)
        r.V.wait(sp)
        r.V.tensor_tensor(k.ACT[:, 16384 + h * 1024: 16384 + (h + 1) * 1024],
                          P, ZBB, ALU.mult).inc(sc, 1).inc(k.vch, 1)

    s_bo = r.sem("a1_bo")
    r.G.wait_val(s_proj, s_proj.v)
    r.G.wait(k.s_pdp)
    r.G.dma_start(out=k.BROW2, in_=k.d_l1_bo).inc(s_bo, 16)
    for m in range(NH):
        bu = k.ws.load(k.d_l1_wo[m], 2048)
        P, sp, sc = (k.PA, k.s_pap, k.s_pac) if m % 2 == 0 else \
                    (k.PB, k.s_pbp, k.s_pbc)
        bu.acquire(r.T)
        r.T.wait(sc)
        r.T.wait(s_bo)
        if m == 0:
            r.T.wait(k.vch)
        for c in range(NC):
            for n in range(2):
                r.T.matmul(P[:, n * 512:(n + 1) * 512],
                           bu.ap[:, c * 128:(c + 1) * 128],
                           k.ACT[:, 16384 + c * 1024 + n * 512:
                                 16384 + c * 1024 + n * 512 + 512],
                           start=(c == 0), stop=False)
        for n in range(2):
            r.T.matmul(P[:, n * 512:(n + 1) * 512],
                       k.BROW2[0:1, m * 128:(m + 1) * 128],
                       k.ONES_BF, start=False, stop=True)
        i = r.rlast("tensor").inc(sp, 1)
        bu.release(i)
        r.V.wait(sp)
        xsl = k.X[:, m * 1024:(m + 1) * 1024]
        r.V.tensor_tensor(xsl, xsl, P, ALU.add).inc(sc, 1)



_PROG_CACHE = {}


def _get_program(phases, dbg=None):
    key = (tuple(phases), dbg)
    if key not in _PROG_CACHE:
        k = build_program(tuple(phases), dbg=dbg)
        _PROG_CACHE[key] = k
    return _PROG_CACHE[key]


def kernel(**inputs):
    pre = _prep(inputs)
    x = pre["x_cores"]
    for g in GROUPS:
        k = _get_program(g)
        in_maps = []
        for c in range(8):
            m = {"x": x[c]}
            for nm in k.innames:
                if nm in ("decay_pp", "decay_row"):
                    m[nm] = pre[nm][c]
                else:
                    m[nm] = pre[nm]
            in_maps.append(m)
        res = run_bass_kernel_spmd(k.nc, in_maps, core_ids=list(range(8)))
        x = [res.results[c]["y"] for c in range(8)]
    out = np.empty((2, S, D), np.float32)
    for c in range(8):
        out[c // 4, (c % 4) * T:(c % 4 + 1) * T, :] = x[c].T
    return out


# version 04 (final)
# speedup vs baseline: 1.0613x; 1.0268x over previous
import numpy as np
import ml_dtypes
import concourse.bass as bass
import concourse.mybir as mybir
from concourse.bass_utils import run_bass_kernel_spmd

F32 = mybir.dt.float32
BF16 = mybir.dt.bfloat16
AF = mybir.ActivationFunctionType
ALU = mybir.AluOpType
AX = mybir.AxisListType

D = 2048
T = 1024
NC = 16
NH = 16
HD = 128
S = 4096
NE = 16
NPE = 18
FMOE = 1024
DFF = 8192
EPS_LN = 1e-5
INV_SQRT_HD = 1.0 / float(np.sqrt(128.0))

ALL_PHASES = ("ln", "moe", "ln", "attn0", "ln", "moe",
              "prep1", "ln", "ffn1", "ln", "attn1", "ln", "ffn1")

GROUPS = [ALL_PHASES]



def _blk_lhsT(WT):
    K, M = WT.shape
    a = WT.reshape(K // 128, 128, M // 128, 128)
    return np.ascontiguousarray(a.transpose(2, 1, 0, 3).reshape(M // 128, 128, K))


def _prep(inp):
    g = {}
    f32 = lambda a: np.ascontiguousarray(np.asarray(a), dtype=np.float32)

    x = f32(inp["L0_attn_x"])
    g["x_cores"] = [np.ascontiguousarray(x[c // 4, (c % 4) * T:(c % 4 + 1) * T, :].T)
                    for c in range(8)]

    for nm in ("Wq", "Wk", "Wv", "Wo"):
        W = f32(inp[f"L0_attn_{nm}"])
        g[f"l0_{nm.lower()}"] = _blk_lhsT(np.ascontiguousarray(W.T))
    for nm in ("bq", "bk", "bv", "bo"):
        g[f"l0_{nm}"] = f32(inp[f"L0_attn_{nm}"]).reshape(1, D)

    gw = f32(inp["L0_ffn_gate_weight"])
    g["gwt"] = np.ascontiguousarray(gw.T.reshape(NC, 128, NE))
    g["e_bias"] = f32(inp["L0_ffn_e_score_correction_bias"]).reshape(1, NE)
    gate_w = f32(inp["L0_ffn_gate_w"])
    up_w = f32(inp["L0_ffn_up_w"])
    down_w = f32(inp["L0_ffn_down_w"])
    shg = f32(inp["L0_ffn_sh_gate_w"])
    shu = f32(inp["L0_ffn_sh_up_w"])
    shd = f32(inp["L0_ffn_sh_down_w"])
    gu_units = np.empty((NPE, 8, 128, 4096), np.float32)
    for e in range(NPE):
        if e < NE:
            gmat, umat = gate_w[e], up_w[e]
        else:
            lo = (e - NE) * FMOE
            gmat, umat = shg[lo:lo + FMOE], shu[lo:lo + FMOE]
        gb = _blk_lhsT(np.ascontiguousarray(gmat.T))
        ub = _blk_lhsT(np.ascontiguousarray(umat.T))
        gu_units[e, :, :, :2048] = gb
        gu_units[e, :, :, 2048:] = ub
    g["l0_gu"] = gu_units.reshape(NPE * 8, 128, 4096)

    groups = [[0, 1, 2, 3], [4, 5, 6, 7], [8, 9, 10, 11], [12, 13, 14, 15], [16, 17]]
    dmats = [down_w[e].T if e < NE else
             (shd[:, (e - NE) * FMOE:(e - NE + 1) * FMOE]).T for e in range(NPE)]
    dblk = [_blk_lhsT(np.ascontiguousarray(m)) for m in dmats]
    full = np.empty((4 * NC, 128, 4096), np.float32)
    for gi in range(4):
        for hc in range(NC):
            full[gi * NC + hc] = np.concatenate(
                [dblk[e][hc] for e in groups[gi]], axis=1)
    g["l0_down"] = full
    shared = np.empty((NC, 128, 2048), np.float32)
    for hc in range(NC):
        shared[hc] = np.concatenate([dblk[16][hc], dblk[17][hc]], axis=1)
    g["l0_downsh"] = shared

    for nm in ("Wq", "Wk", "Wv", "Wo"):
        W = f32(inp[f"L1_attn_{nm}"])
        g[f"l1_{nm.lower()}"] = _blk_lhsT(np.ascontiguousarray(W.T))
    for nm in ("bq", "bk", "bv", "bo"):
        g[f"l1_{nm}"] = f32(inp[f"L1_attn_{nm}"]).reshape(1, D)

    w1 = f32(inp["L1_ffn_w1"])
    w2 = f32(inp["L1_ffn_w2"])
    g["l1_w1"] = _blk_lhsT(np.ascontiguousarray(w1.T))
    g["l1_w2"] = _blk_lhsT(np.ascontiguousarray(w2.T))
    b1 = f32(inp["L1_ffn_b1"])
    g["b1col"] = np.ascontiguousarray(b1.reshape(DFF // 128, 128).T)
    b2 = f32(inp["L1_ffn_b2"])
    g["b2col"] = np.ascontiguousarray(b2.reshape(NC, 128).T)

    g["ones_row"] = np.ones((1, 512), np.float32)
    g["onescol"] = np.ones((128, 1), np.float32)
    g["ident"] = np.eye(128, dtype=np.float32)
    sel = np.zeros((NE, NE, 128), np.float32)
    for e in range(NE):
        sel[e, e, :] = 1.0
    g["sel"] = sel
    pos = np.arange(S, dtype=np.float64)
    decay = np.exp(-pos / float(S)).astype(np.float32)
    g["decay_pp"] = [np.ascontiguousarray(
        decay[(c % 4) * T:(c % 4 + 1) * T].reshape(8, 128).T) for c in range(8)]
    g["decay_row"] = [np.ascontiguousarray(
        decay[(c % 4) * T:(c % 4 + 1) * T].reshape(1, T)) for c in range(8)]
    return g



ENGINES = ("tensor", "vector", "scalar", "gpsimd", "sync")


class Sem:
    __slots__ = ("h", "v")

    def __init__(self, nc, name):
        self.h = nc.alloc_semaphore(name)
        self.v = 0


class RecInst:
    __slots__ = ("item",)

    def __init__(self, item):
        self.item = item

    def inc(self, sem, n=1):
        sem.v += n
        self.item[3].append((sem.h, n))
        return self


class Eng:

    def __init__(self, rec, name):
        self.rec = rec
        self.name = name

    def __getattr__(self, meth):
        def call(*a, **k):
            item = [meth, a, k, []]
            self.rec.q[self.name].append(item)
            self.rec.last[self.name] = item
            return RecInst(item)
        return call

    def wait(self, sem):
        if sem.v > 0:
            self.rec.q[self.name].append(["wait_ge", (sem.h, sem.v), {}, []])

    def wait_val(self, sem, v):
        self.rec.q[self.name].append(["wait_ge", (sem.h, v), {}, []])


class Rec:
    def __init__(self, nc):
        self.nc = nc
        self.q = {e: [] for e in ENGINES}
        self.last = {e: None for e in ENGINES}
        self.T = Eng(self, "tensor")
        self.V = Eng(self, "vector")
        self.S = Eng(self, "scalar")
        self.G = Eng(self, "gpsimd")
        self.Y = Eng(self, "sync")
        self._sems = {}
        self.bar = Sem(nc, "bar")
        self._bar_last = {e: None for e in ENGINES}

    def sem(self, name):
        s = self._sems.get(name)
        if s is None:
            s = self._sems[name] = Sem(self.nc, name)
        return s

    def rlast(self, eng):
        return RecInst(self.last[eng])

    def barrier(self, engines=("tensor", "vector", "scalar")):
        n = 0
        for e in engines:
            it = self.last[e]
            if it is not None and it is not self._bar_last[e]:
                RecInst(it).inc(self.bar, 1)
                self._bar_last[e] = it
                n += 1
        if n:
            for e in engines:
                self.q[e].append(["wait_ge", (self.bar.h, self.bar.v), {}, []])

    def replay(self):
        with self.nc.Block() as blk:
            for name in ENGINES:
                qq = self.q[name]

                def run(eng, qq=qq, name=name):
                    for meth, a, k, incs in qq:
                        inst = getattr(eng, meth)(*a, **k)
                        if not incs:
                            continue
                        if meth in ("dma_start", "collective_compute"):
                            assert len(incs) == 1, (meth, incs)
                            inst.then_inc(*incs[0])
                            continue
                        inst.then_inc(*incs[0])
                        for (h, n) in incs[1:]:
                            eng.drain().then_inc(h, n)
                getattr(blk, name)(run)


class BufUse:

    __slots__ = ("ap", "loaded", "wait_v", "freed", "slots_span", "stream")

    def __init__(self, ap, loaded, wait_v, freed, slots_span, stream):
        self.ap = ap
        self.loaded = loaded
        self.wait_v = wait_v
        self.freed = freed
        self.slots_span = slots_span
        self.stream = stream

    def acquire(self, eng):
        eng.wait_val(self.loaded, self.wait_v)

    def release(self, inst):
        inst.inc(self.freed, 1)
        mark = (self.freed, self.freed.v)
        for s in self.slots_span:
            s.release_at = mark


class Slot:
    def __init__(self, rec, idx):
        self.idx = idx
        self.loaded = {"gpsimd": rec.sem(f"wb_ldg{idx}"),
                       "sync": rec.sem(f"wb_ldy{idx}")}
        self.freed = rec.sem(f"wb_fr{idx}")
        self.release_at = None


class Stream:

    def __init__(self, rec, wbuf, slots, ring, slot_cols=2048):
        self.rec = rec
        self.wbuf = wbuf
        self.slots = slots
        self.ring = ring
        self.slot_cols = slot_cols
        self.i = 0

    def load(self, src_ap, ncols, queue="gpsimd", ndma=1):
        first, span = self.ring[self.i % len(self.ring)]
        self.i += 1
        sl = self.slots[first]
        eng = self.rec.G if queue == "gpsimd" else self.rec.Y
        spanned = self.slots[first:first + span]
        for s in spanned:
            r = s.release_at
            if r is not None:
                eng.wait_val(r[0], r[1])
        dst = self.wbuf[:, first * self.slot_cols: first * self.slot_cols + ncols]
        ld = sl.loaded[queue]
        eng.dma_start(out=dst, in_=src_ap).inc(ld, 16)
        return BufUse(dst, ld, ld.v, sl.freed, spanned, self)



class K:
    pass


def _alias(nc, base_handle, name, shape, dtype, byte_off):
    addr = nc.lookup_mloc(base_handle).addr + byte_off
    return nc.alloc_sbuf_tensor_at(name, shape, dtype, offset=addr)


def build_program(phases, dbg=None):
    nc = bass.Bass("TRN2", target_bir_lowering=False)
    k = K()
    k.nc = nc
    k.phases = phases
    r = k.r = Rec(nc)
    k.dbg = dbg

    k.d_x = nc.dram_tensor("x", [D, T], F32, kind="ExternalInput").ap()
    k.d_y = nc.dram_tensor("y", [D, T], F32, kind="ExternalOutput").ap()
    k.innames = []

    def din(name, shape):
        k.innames.append(name)
        return nc.dram_tensor(name, list(shape), F32, kind="ExternalInput").ap()

    k.d_ones = din("ones_row", (1, 512))
    k.d_onescol = din("onescol", (128, 1))
    k.d_ident = din("ident", (128, 128))
    need = set(phases)
    if "moe" in need:
        k.d_gwt = din("gwt", (NC, 128, NE))
        k.d_ebias = din("e_bias", (1, NE))
        k.d_gu = din("l0_gu", (NPE * 8, 128, 4096))
        k.d_down = din("l0_down", (4 * NC, 128, 4096))
        k.d_downsh = din("l0_downsh", (NC, 128, 2048))
        k.d_sel = din("sel", (NE, NE, 128))
    if "attn0" in need:
        for nm in ("wq", "wk", "wv", "wo"):
            setattr(k, f"d_l0_{nm}", din(f"l0_{nm}", (NC, 128, D)))
        for nm in ("bq", "bk", "bv", "bo"):
            setattr(k, f"d_l0_{nm}", din(f"l0_{nm}", (1, D)))
        k.d_l0k_in = nc.dram_tensor("l0k_in", [2049, T], BF16, kind="Internal")
        k.d_l0k_out = nc.dram_tensor("l0k_out", [4 * 2049, T], BF16, kind="Internal")
        k.d_l0v_in = nc.dram_tensor("l0v_in", [T, D], BF16, kind="Internal")
        k.d_l0v_out = nc.dram_tensor("l0v_out", [S, D], BF16, kind="Internal")
    if "attn1" in need:
        for nm in ("wq", "wk", "wv", "wo"):
            setattr(k, f"d_l1_{nm}", din(f"l1_{nm}", (NC, 128, D)))
        for nm in ("bq", "bk", "bv", "bo"):
            setattr(k, f"d_l1_{nm}", din(f"l1_{nm}", (1, D)))
        k.d_decay_pp = din("decay_pp", (128, 8))
        k.d_decay_row = din("decay_row", (1, T))
        k.d_l1cc_in = nc.dram_tensor("l1cc_in", [128, 2064], F32, kind="Internal")
        k.d_l1cc_out = nc.dram_tensor("l1cc_out", [128, 2064], F32, kind="Internal")
    if "prep1" in need or "ffn1" in need:
        k.d_w1 = din("l1_w1", (DFF // 128, 128, D))
        k.d_w2 = din("l1_w2", (NC, 128, DFF))
        k.d_b1col = din("b1col", (128, 64))
        k.d_b2col = din("b2col", (128, NC))
        k.d_w1q = nc.dram_tensor("w1q", [DFF // 128, 128, D], BF16, kind="Internal")
        k.d_w2q = nc.dram_tensor("w2q", [NC, 128, DFF], BF16, kind="Internal")

    a = nc.alloc_sbuf_tensor
    k.X = a("Xr", [128, 16384], F32)[:, :]
    k.LNX = a("LNXr", [128, 16384], BF16)[:, :]
    k.ACTh = a("ACTr", [128, 32768], BF16)
    k.ACT = k.ACTh[:, :]
    k.WBUFh = a("WBUFr", [128, 12288], BF16)
    k.WBUF = k.WBUFh[:, :]
    k.SMFh = a("SMFr", [128, 2048], F32)
    k.SMF = k.SMFh[:, :]
    k.SMBh = a("SMBr", [128, 4096], BF16)
    k.SMB = k.SMBh[:, :]
    k.SMT = a("SMTr", [128, 360], F32)[:, :]
    k.ONES_BF = a("ONESBF", [1, 512], BF16)[:, :]
    k.ONES_F = a("ONESF", [1, 128], F32)[:, :]
    k.IDENT = a("IDENTr", [128, 128], F32)[:, :]
    k.IDENTB = a("IDENTBr", [128, 128], BF16)[:, :]
    k.ONESCB = a("ONESCBr", [128, 18], BF16)[:, :]
    k.MAXB = a("MAXBr", [1, 64], BF16)[:, :]
    k.ROWSF = a("ROWSFr", [1, 1024], F32)[:, :]

    k.ACT32 = _alias(nc, k.ACTh, "ACT32", [128, 16384], F32, 0)[:, :]
    k.KVS = _alias(nc, k.ACTh, "KVS", [128, 2048], F32, 32768)[:, :]
    k.BROW = _alias(nc, k.SMFh, "BROW", [1, 2048], BF16, 0)[:, :]
    k.BROW2 = _alias(nc, k.SMFh, "BROW2", [1, 2048], BF16, 4096)[:, :]
    k.WEM = _alias(nc, k.SMFh, "WEM", [16, 1024], F32, 4096)[:, :]
    k.ZROW = _alias(nc, k.SMFh, "ZROW", [1, 1024], F32, 4096)[:, :]
    k.SIL = _alias(nc, k.SMFh, "SIL", [128, 1024], F32, 0)[:, :]
    k.MROW = _alias(nc, k.SMBh, "MROW", [1, 1024], F32, 0)[:, :]
    k.RROW = _alias(nc, k.SMBh, "RROW", [1, 1024], F32, 4096)[:, :]

    k.PA = nc.alloc_psum_tensor("PA", [128, 1024], F32)[:, :]
    k.PB = nc.alloc_psum_tensor("PB", [128, 1024], F32)[:, :]
    k.PC = nc.alloc_psum_tensor("PC", [128, 1024], F32)[:, :]
    k.PD = nc.alloc_psum_tensor("PD", [128, 1024], F32)[:, :]
    for nm in ("pa", "pb", "pc", "pd"):
        setattr(k, f"s_{nm}p", r.sem(f"{nm}_p"))
        setattr(k, f"s_{nm}c", r.sem(f"{nm}_c"))

    k.slots = [Slot(r, i) for i in range(6)]
    k.ws = Stream(r, k.WBUF, k.slots, [(0, 2), (2, 2), (4, 2)])
    k.WB32 = _alias(nc, k.WBUFh, "WB32", [128, 6144], F32, 0)[:, :]
    k.ws32 = Stream(r, k.WB32, k.slots, [(0, 2), (2, 2), (4, 2)], slot_cols=1024)

    k.s_ld = r.sem("ld_const")
    k.s_ldg = r.sem("ld_constg")
    k.s_x = r.sem("x_io")
    k.vch = r.sem("vchain")
    k.sch = r.sem("schain")

    r.Y.dma_start(out=k.X.rearrange("p (c t) -> p c t", c=NC),
                  in_=k.d_x.rearrange("(c p) t -> p c t", p=128)).inc(k.s_x, 16)
    r.Y.dma_start(out=k.ONES_F, in_=k.d_ones[:, 0:128]).inc(k.s_ld, 16)
    r.Y.dma_start(out=k.IDENT, in_=k.d_ident).inc(k.s_ld, 16)
    r.G.dma_start(out=k.ONES_BF, in_=k.d_ones).inc(k.s_ldg, 16)
    r.G.dma_start(out=k.IDENTB, in_=k.d_ident).inc(k.s_ldg, 16)
    r.G.dma_start(out=k.ONESCB[:, 0:1], in_=k.d_onescol).inc(k.s_ldg, 16)
    if "prep1" in need or "ffn1" in need:
        r.Y.dma_start(out=k.SMT[:, 136:200], in_=k.d_b1col).inc(k.s_ld, 16)
        r.Y.dma_start(out=k.SMT[:, 200:216], in_=k.d_b2col).inc(k.s_ld, 16)
    if "attn1" in need:
        r.Y.dma_start(out=k.SMT[:, 216:224], in_=k.d_decay_pp).inc(k.s_ld, 16)

    for e in (r.T, r.V, r.S):
        e.wait(k.s_x)
        e.wait(k.s_ld)
        e.wait(k.s_ldg)
    r.G.wait(k.s_x)
    r.G.wait(k.s_ld)

    n_ln = 0
    for ph in phases:
        if ph == "ln":
            ph_ln(k)
            n_ln += 1
        elif ph == "moe":
            ph_moe(k)
        elif ph == "attn0":
            ph_attn0(k)
        elif ph == "prep1":
            ph_prep1(k)
        elif ph == "ffn1":
            ph_ffn1(k)
        elif ph == "attn1":
            ph_attn1(k)
        r.barrier()

    st = r.sem("store")
    r.Y.wait_val(r.bar, r.bar.v)
    r.Y.dma_start(out=k.d_y.rearrange("(c p) t -> p c t", p=128),
                  in_=k.X.rearrange("p (c t) -> p c t", c=NC)).inc(st, 16)
    nst = 16
    if dbg is not None:
        which, shape = dbg
        k.d_dbg = nc.dram_tensor("dbgout", list(shape), BF16 if which in ("LNX", "ACT", "WBUF", "SMB") else F32, kind="ExternalOutput").ap()
        src = getattr(k, which)
        r.Y.dma_start(out=k.d_dbg, in_=src[:shape[0], :shape[1]]).inc(st, 16)
        nst += 16
    r.Y.wait_val(st, nst)

    r.replay()
    return k



def ph_ln(k):
    r = k.r
    XB = k.ACT
    XSQ = lambda c: k.ACT[:, 16384 + c * 1024: 16384 + (c + 1) * 1024]
    sv = r.sem("ln_cast")
    for c in range(NC):
        xc = k.X[:, c * 1024:(c + 1) * 1024]
        r.V.tensor_copy(XB[:, c * 1024:(c + 1) * 1024], xc).inc(sv, 1)
        r.S.activation(XSQ(c), xc, AF.Square).inc(sv, 1)
    r.T.wait(k.s_pac)
    r.T.wait(k.s_pbc)
    r.T.wait(sv)
    oc = k.ONESCB[:, 0:1]
    for c in range(NC):
        for n in range(2):
            st = (c == 0)
            sp = (c == NC - 1)
            r.T.matmul(k.PA[0:1, n * 512:(n + 1) * 512], oc,
                       XB[:, c * 1024 + n * 512: c * 1024 + (n + 1) * 512],
                       start=st, stop=sp)
            i2 = r.T.matmul(k.PB[0:1, n * 512:(n + 1) * 512], oc,
                            XSQ(c)[:, n * 512:(n + 1) * 512], start=st, stop=sp)
    i2.inc(k.s_pap, 1).inc(k.s_pbp, 1)
    r.V.wait(k.s_pap)
    r.V.tensor_scalar(k.MROW, k.PA[0:1, :], 1.0 / D, None, ALU.mult).inc(k.s_pac, 1).inc(k.vch, 1)
    r.V.wait(k.vch)
    r.V.tensor_tensor(k.ROWSF, k.MROW, k.MROW, ALU.mult).inc(k.vch, 1)
    r.V.wait(k.s_pbp)
    r.V.wait(k.vch)
    r.V.scalar_tensor_tensor(k.RROW, k.PB[0:1, :], 1.0 / D, k.ROWSF,
                             ALU.mult, ALU.subtract).inc(k.s_pbc, 1).inc(k.vch, 1)
    sr = r.sem("ln_rstd")
    r.V.wait(k.vch)
    r.V.tensor_scalar(k.ROWSF, k.RROW, EPS_LN, None, ALU.add).inc(k.vch, 1)
    r.S.wait(k.vch)
    r.S.activation(k.RROW, k.ROWSF, AF.Ln).inc(k.sch, 1)
    r.S.wait(k.sch)
    r.S.activation(k.ROWSF, k.RROW, AF.Exp, scale=-0.5).inc(sr, 1)
    r.T.wait(k.s_pcc)
    r.T.wait(k.s_pdc)
    r.T.wait(sr)
    for n in range(2):
        r.T.matmul(k.PC[:, n * 512:(n + 1) * 512], k.ONES_F,
                   k.MROW[:, n * 512:(n + 1) * 512], start=True, stop=True
                   ).inc(k.s_pcp, 1)
        r.T.matmul(k.PD[:, n * 512:(n + 1) * 512], k.ONES_F,
                   k.ROWSF[:, n * 512:(n + 1) * 512], start=True, stop=True
                   ).inc(k.s_pdp, 1)
    MB = k.SMF[:, 0:1024]
    RB = k.SMF[:, 1024:2048]
    r.V.wait(k.s_pcp)
    r.V.tensor_copy(MB, k.PC).inc(k.s_pcc, 1).inc(k.vch, 1)
    r.V.wait(k.s_pdp)
    r.V.tensor_copy(RB, k.PD).inc(k.s_pdc, 1).inc(k.vch, 1)
    r.V.wait(k.vch)
    sdone = r.sem("ln_done")
    for c in range(NC):
        sl = slice(c * 1024, (c + 1) * 1024)
        tmp = k.ACT32[:, sl]
        r.V.tensor_tensor(tmp, k.X[:, sl], MB, ALU.subtract).inc(k.vch, 1)
        r.V.wait(k.vch)
        r.V.tensor_tensor(k.LNX[:, sl], tmp, RB, ALU.mult).inc(sdone, 1)



MOE_GROUPS = ([0, 1, 2, 3], [4, 5, 6, 7], [8, 9, 10, 11], [12, 13, 14, 15], [16, 17])
BIGM = 1.0e4


def ph_moe(k):
    r = k.r
    GWT = k.SMB[:, 2048:2304]
    SELF32 = _alias(k.nc, k.SMBh, f"SELF32_{r.bar.v}_{k.ws.i}", [16, 128], F32, 6656)[:, :]
    TMPB = k.SMB[:, 2048:3072]
    EB = k.SMT[:, 118:134]
    EBROW = k.SMT[0:1, 272:288]
    s_gwt = r.sem("moe_gwt")
    s_ebr = r.sem("moe_ebr")
    s_rt = r.sem("moe_rt")
    s_selfr = r.sem("moe_selfr")
    s_self = r.sem("moe_self")
    s_bcw = r.sem("moe_bcw")
    s_act = r.sem("moe_act")
    s_scfree = r.sem("moe_scfree")
    s_silfree = r.sem("moe_silfree")
    s_actfree = r.sem("moe_actfree")

    r.G.wait_val(r.bar, r.bar.v)
    r.G.dma_start(out=GWT.rearrange("p (c e) -> p c e", c=NC),
                  in_=k.d_gwt.rearrange("c p e -> p c e"),
                  allow_slow_non_contiguous=True).inc(s_gwt, 16)
    r.Y.wait_val(r.bar, r.bar.v)
    r.Y.dma_start(out=EBROW, in_=k.d_ebias).inc(s_ebr, 16)

    r.T.wait(s_gwt)
    r.T.wait(s_ebr)
    r.T.wait(k.s_pcc)
    r.T.matmul(k.PC[:, 0:16], k.ONES_F, EBROW, start=True, stop=True).inc(k.s_pcp, 1)
    r.V.wait(k.s_pcp)
    r.V.tensor_copy(EB, k.PC[:, 0:16]).inc(k.s_pcc, 1).inc(s_rt, 1)

    SC = k.SMT[:, 0:16]
    S4 = k.SMT[:, 16:32]
    PAIR = k.SMT[:, 32:56]
    GS = k.SMT[:, 56:60]
    MIN6 = k.SMT[:, 60:66]
    TH = k.SMT[:, 66:67]
    M0 = k.SMT[:, 68:84]
    MA = k.SMT[:, 84:100]
    MBv = k.SMT[:, 100:116]
    WR = k.SMT[:, 116:117]
    WRT = k.SMT[:, 134:135]
    SU = k.SMT[:, 135:136]
    pairs = [(0, 1), (0, 2), (0, 3), (1, 2), (1, 3), (2, 3)]
    for tt in range(8):
        P = (k.PA if tt % 2 == 0 else k.PB)[:, 0:16]
        sp = k.s_pap if tt % 2 == 0 else k.s_pbp
        sc_ = k.s_pac if tt % 2 == 0 else k.s_pbc
        r.T.wait(sc_)
        for c in range(NC):
            r.T.matmul(P, k.LNX[:, c * 1024 + tt * 128: c * 1024 + (tt + 1) * 128],
                       GWT[:, c * 16:(c + 1) * 16],
                       start=(c == 0), stop=(c == NC - 1))
        r.rlast("tensor").inc(sp, 1)
        r.S.wait(sp)
        r.S.wait(s_scfree)
        r.S.activation(SC, P, AF.Sigmoid).inc(sc_, 1).inc(s_rt, 1)
        r.V.wait(s_rt)
        r.V.tensor_tensor(S4, SC, EB, ALU.add).inc(k.vch, 1)
        r.V.wait(k.vch)
        for pi, (i, j) in enumerate(pairs):
            r.V.tensor_tensor(PAIR[:, pi * 4:(pi + 1) * 4],
                              S4[:, i:16:4], S4[:, j:16:4], ALU.add).inc(k.vch, 1)
        r.V.wait(k.vch)
        r.V.tensor_tensor(GS, PAIR[:, 0:4], PAIR[:, 4:8], ALU.max).inc(k.vch, 1)
        for pi in range(2, 6):
            r.V.wait(k.vch)
            r.V.tensor_tensor(GS, GS, PAIR[:, pi * 4:(pi + 1) * 4], ALU.max).inc(k.vch, 1)
        gpairs = [(0, 1), (0, 2), (0, 3), (1, 2), (1, 3), (2, 3)]
        r.V.wait(k.vch)
        for pi, (i, j) in enumerate(gpairs):
            r.V.tensor_tensor(MIN6[:, pi:pi + 1], GS[:, i:i + 1], GS[:, j:j + 1],
                              ALU.min).inc(k.vch, 1)
        r.V.wait(k.vch)
        r.V.reduce_max(TH, MIN6, axis=AX.X).inc(k.vch, 1)
        r.V.wait(k.vch)
        GM = MIN6[:, 0:4]
        r.V.tensor_scalar(GM, GS, TH, None, ALU.is_ge).inc(k.vch, 1)
        r.V.wait(k.vch)
        for i in range(4):
            r.V.tensor_copy(M0[:, i:16:4], GM).inc(k.vch, 1)
        r.V.wait(k.vch)
        r.V.tensor_scalar(MA, S4, BIGM, None, ALU.add).inc(k.vch, 1)
        r.V.wait(k.vch)
        r.V.tensor_tensor(MA, MA, M0, ALU.mult).inc(k.vch, 1)
        r.V.wait(k.vch)
        r.V.tensor_scalar(MA, MA, BIGM, None, ALU.subtract).inc(k.vch, 1)
        r.V.wait(k.vch)
        r.V.tensor_copy(MBv, MA).inc(k.vch, 1)
        for rnd in range(3):
            r.V.wait(k.vch)
            r.V.reduce_max(TH, MA, axis=AX.X).inc(k.vch, 1)
            r.V.wait(k.vch)
            E = PAIR[:, 0:16]
            r.V.tensor_scalar(E, MA, TH, None, ALU.is_equal).inc(k.vch, 1)
            r.V.wait(k.vch)
            r.V.scalar_tensor_tensor(MA, E, -2.0 * BIGM, MA, ALU.mult, ALU.add
                                     ).inc(k.vch, 1)
        r.V.wait(k.vch)
        r.V.reduce_max(TH, MA, axis=AX.X).inc(k.vch, 1)
        r.V.wait(k.vch)
        SEL = PAIR[:, 0:16]
        r.V.tensor_scalar(SEL, MBv, TH, None, ALU.is_ge).inc(k.vch, 1)
        r.V.wait(k.vch)
        WSEL = M0
        r.V.tensor_tensor(WSEL, SEL, SC, ALU.mult).inc(k.vch, 1).inc(s_scfree, 1)
        r.V.wait(k.vch)
        r.V.reduce_sum(SU, WSEL, axis=AX.X).inc(k.vch, 1)
        r.V.wait(k.vch)
        r.V.tensor_scalar(SU, SU, 1e-20, None, ALU.add).inc(k.vch, 1)
        r.V.wait(k.vch)
        r.V.reciprocal(WRT, SU).inc(k.vch, 1)
        r.V.wait(k.vch)
        r.V.tensor_scalar(MA, WSEL, WRT, 1.25, ALU.mult, ALU.mult).inc(k.vch, 1)
        r.V.wait(k.vch)
        r.T.wait(k.s_pcc)
        r.T.wait(k.vch)
        r.T.transpose(k.PC[0:16, 0:128], MA, k.IDENT).inc(k.s_pcp, 1)
        r.V.wait(k.s_pcp)
        r.V.tensor_copy(k.WEM[:, tt * 128:(tt + 1) * 128], k.PC[0:16, 0:128]
                        ).inc(k.s_pcc, 1).inc(s_rt, 1)

    for g, grp in enumerate(MOE_GROUPS):
        for ei, e in enumerate(grp):
            shared = e >= NE
            bslot = k.SMB[:, (ei % 2) * 1024:(ei % 2) * 1024 + 1024]
            if not shared:
                r.Y.wait(s_selfr)
                r.Y.dma_start(out=SELF32, in_=k.d_sel[e]).inc(s_self, 16)
                r.T.wait(s_self)
                r.T.wait(k.s_pcc)
                r.T.wait(s_rt)
                for n in range(2):
                    i = r.T.matmul(k.PC[:, n * 512:(n + 1) * 512], SELF32,
                                   k.WEM[:, n * 512:(n + 1) * 512],
                                   start=True, stop=True)
                i.inc(k.s_pcp, 1).inc(s_selfr, 1)
                r.V.wait(k.s_pcp)
                r.V.wait(s_act)
                r.V.tensor_copy(bslot, k.PC).inc(k.s_pcc, 1).inc(s_bcw, 1)
            else:
                r.V.wait(s_act)
                r.V.memset(bslot, 0.5).inc(s_bcw, 1)
            for fc in range(8):
                bu = k.ws.load(k.d_gu[e * 8 + fc], 4096)
                PG, sgp, sgc = (k.PA, k.s_pap, k.s_pac) if fc % 2 == 0 else \
                               (k.PC, k.s_pcp, k.s_pcc)
                PU, sup, suc = (k.PB, k.s_pbp, k.s_pbc) if fc % 2 == 0 else \
                               (k.PD, k.s_pdp, k.s_pdc)
                bu.acquire(r.T)
                r.T.wait(sgc)
                for c in range(NC):
                    for n in range(2):
                        r.T.matmul(PG[:, n * 512:(n + 1) * 512],
                                   bu.ap[:, c * 128:(c + 1) * 128],
                                   k.LNX[:, c * 1024 + n * 512: c * 1024 + n * 512 + 512],
                                   start=(c == 0), stop=(c == NC - 1))
                r.rlast("tensor").inc(sgp, 1)
                r.T.wait(suc)
                for c in range(NC):
                    for n in range(2):
                        r.T.matmul(PU[:, n * 512:(n + 1) * 512],
                                   bu.ap[:, 2048 + c * 128: 2048 + (c + 1) * 128],
                                   k.LNX[:, c * 1024 + n * 512: c * 1024 + n * 512 + 512],
                                   start=(c == 0), stop=(c == NC - 1))
                i = r.rlast("tensor").inc(sup, 1)
                bu.release(i)
                r.S.wait(sgp)
                r.S.wait(s_silfree)
                r.S.activation(k.SIL, PG, AF.Silu).inc(sgc, 1).inc(k.sch, 1)
                r.V.wait(k.sch)
                r.V.wait(sup)
                r.V.wait(s_act)
                r.V.tensor_tensor(TMPB, k.SIL, PU, ALU.mult
                                  ).inc(suc, 1).inc(k.vch, 1).inc(s_silfree, 1)
                r.V.wait(k.vch)
                r.V.wait(s_bcw)
                aslot = k.ACT[:, ei * 8192 + fc * 1024: ei * 8192 + (fc + 1) * 1024]
                r.V.tensor_tensor(aslot, TMPB, bslot, ALU.mult).inc(s_act, 1)
        nacts = len(grp) * 8
        for hc in range(NC):
            if len(grp) == 4:
                bu = k.ws.load(k.d_down[g * NC + hc], 4096)
            else:
                bu = k.ws.load(k.d_downsh[hc], 2048)
            PDn, sdp, sdc = (k.PA, k.s_pap, k.s_pac) if hc % 2 == 0 else \
                            (k.PC, k.s_pcp, k.s_pcc)
            bu.acquire(r.T)
            r.T.wait(sdc)
            if hc == 0:
                r.T.wait(s_act)
            nmm = 0
            for ei, e in enumerate(grp):
                for fc in range(8):
                    for n in range(2):
                        r.T.matmul(PDn[:, n * 512:(n + 1) * 512],
                                   bu.ap[:, (ei * 8 + fc) * 128:(ei * 8 + fc + 1) * 128],
                                   k.ACT[:, ei * 8192 + fc * 1024 + n * 512:
                                         ei * 8192 + fc * 1024 + n * 512 + 512],
                                   start=(ei == 0 and fc == 0),
                                   stop=(ei == len(grp) - 1 and fc == 7))
            i = r.rlast("tensor").inc(sdp, 1)
            bu.release(i)
            if hc == NC - 1:
                i.inc(s_actfree, 1)
            r.V.wait(sdp)
            xsl = k.X[:, hc * 1024:(hc + 1) * 1024]
            r.V.tensor_tensor(xsl, xsl, PDn, ALU.add).inc(sdc, 1).inc(k.vch, 1)
            r.V.wait(k.vch)



def ph_attn0(k):
    r = k.r
    s_br = r.sem("a0_brow")
    s_proj = r.sem("a0_proj")
    s_kstage = r.sem("a0_kstage")
    s_vstage = r.sem("a0_vstage")
    s_lmax = r.sem("a0_lmax")
    s_lmd = r.sem("a0_lmd")
    s_ksq = r.sem("a0_ksq")
    s_ccK = r.sem("a0_ccK")
    s_ccV = r.sem("a0_ccV")
    s_maxk = r.sem("a0_maxk")
    s_exp = r.sem("a0_exp")
    s_pvd = r.sem("a0_pvd")
    s_po = r.sem("a0_po")
    s_zr = r.sem("a0_zr")
    s_zb = r.sem("a0_zb")
    s_evac = r.sem("a0_evac")
    s_vstg = r.sem("a0_vstg")
    KTMP = k.SMB[:, 2048:3072]
    KSQ = k.SMB[:, 3072:4096]
    VTMPF = k.ACT32[:, 8192:9216]
    VTT = [k.SMB[:, 1024:1152], k.SMB[:, 1152:1280]]
    ZB = k.SMB[:, 2048:2560]

    r.G.wait_val(r.bar, r.bar.v)
    r.G.dma_start(out=k.BROW, in_=k.d_l0_bq).inc(s_br, 16)
    r.G.dma_start(out=k.BROW2, in_=k.d_l0_bk).inc(s_br, 16)

    def proj_fm(dunits, brow, h, ping, out_cb):
        bu = k.ws.load(dunits[h], 2048)
        P, sp, sc = (k.PA, k.s_pap, k.s_pac) if ping else (k.PB, k.s_pbp, k.s_pbc)
        bu.acquire(r.T)
        r.T.wait(sc)
        r.T.wait(s_br)
        for c in range(NC):
            for n in range(2):
                r.T.matmul(P[:, n * 512:(n + 1) * 512],
                           bu.ap[:, c * 128:(c + 1) * 128],
                           k.LNX[:, c * 1024 + n * 512: c * 1024 + n * 512 + 512],
                           start=(c == 0), stop=False)
        for n in range(2):
            r.T.matmul(P[:, n * 512:(n + 1) * 512],
                       brow[0:1, h * 128:(h + 1) * 128],
                       k.ONES_BF, start=False, stop=True)
        i = r.rlast("tensor").inc(sp, 1).inc(s_proj, 1)
        bu.release(i)
        out_cb(P, sp, sc)

    for h in range(NH):
        def evq(P, sp, sc, h=h):
            r.V.wait(sp)
            r.V.tensor_copy(k.ACT[:, h * 1024:(h + 1) * 1024], P).inc(sc, 1)
        proj_fm(k.d_l0_wq, k.BROW, h, h % 2 == 0, evq)

    for h in range(NH):
        def evk(P, sp, sc, h=h):
            r.V.wait(sp)
            r.V.wait(s_kstage)
            r.V.wait(s_ksq)
            r.V.tensor_copy(KTMP, P).inc(sc, 1).inc(k.vch, 1)
            r.S.wait(k.vch)
            r.S.wait(s_ksq)
            r.S.activation(KSQ, KTMP, AF.Square).inc(s_ksq, 1)
            r.Y.wait(k.vch)
            r.Y.dma_start(out=k.d_l0k_in.ap()[h * 128:(h + 1) * 128, :], in_=KTMP
                          ).inc(s_kstage, 16)
            r.T.wait(k.s_pcc)
            r.T.wait(s_ksq)
            for n in range(2):
                r.T.matmul(k.PC[0:1, n * 512:(n + 1) * 512], k.ONESCB[:, 0:1],
                           KSQ[:, n * 512:(n + 1) * 512], start=True, stop=True)
            r.rlast("tensor").inc(k.s_pcp, 1).inc(s_ksq, 1)
            r.V.wait(k.s_pcp)
            r.V.reduce_max(k.SMT[0:1, 224 + h:225 + h], k.PC[0:1, 0:1024],
                           axis=AX.X).inc(k.s_pcc, 1).inc(s_lmax, 1)
        proj_fm(k.d_l0_wk, k.BROW2, h, h % 2 == 0, evk)

    r.V.wait(s_lmax)
    r.V.memset(k.ROWSF, 0.0).inc(k.vch, 1)
    r.V.wait(k.vch)
    r.V.tensor_copy(k.ROWSF[0:1, 0:16], k.SMT[0:1, 224:240]).inc(s_lmax, 1)
    r.G.wait_val(s_lmax, s_lmax.v)
    r.G.dma_start(out=k.d_l0k_in.ap()[2048:2049, :], in_=k.ROWSF).inc(s_lmd, 16)

    r.G.wait_val(s_proj, s_proj.v)
    r.G.dma_start(out=k.BROW, in_=k.d_l0_bv).inc(s_br, 16)

    ptile = [r.sem("a0_pt0"), r.sem("a0_pt1")]
    ptilec = [r.sem("a0_pt0c"), r.sem("a0_pt1c")]
    vstg = [r.sem("a0_vs0"), r.sem("a0_vs1")]
    for dv in range(NH):
        def evv(P, sp, sc, dv=dv):
            r.V.wait(sp)
            r.V.wait(s_vstg)
            r.V.tensor_copy(VTMPF, P).inc(sc, 1).inc(k.vch, 1)
            r.T.wait(k.vch)
            for tt in range(8):
                sl = tt % 2
                r.T.wait(ptilec[sl])
                i = r.T.transpose(k.PD[:, sl * 512:sl * 512 + 128],
                                  VTMPF[:, tt * 128:(tt + 1) * 128], k.IDENT
                                  ).inc(ptile[sl], 1)
                if tt >= 6:
                    i.inc(s_vstg, 1)
                r.V.wait(ptile[sl])
                r.V.wait(vstg[sl])
                r.V.tensor_copy(VTT[sl], k.PD[:, sl * 512:sl * 512 + 128]
                                ).inc(ptilec[sl], 1).inc(k.vch, 1)
                r.Y.wait(k.vch)
                r.Y.dma_start(out=k.d_l0v_in.ap()[tt * 128:(tt + 1) * 128,
                                                  dv * 128:(dv + 1) * 128],
                              in_=VTT[sl]).inc(vstg[sl], 16)
        proj_fm(k.d_l0_wv, k.BROW, dv, dv % 2 == 0, evv)

    r.G.wait_val(s_kstage, s_kstage.v)
    r.G.wait(s_lmd)
    r.G.collective_compute("AllGather", ALU.bypass,
                           replica_groups=[[0, 1, 2, 3], [4, 5, 6, 7]],
                           ins=[k.d_l0k_in.ap()], outs=[k.d_l0k_out.ap()]
                           ).inc(s_ccK, 1)
    r.G.wait(vstg[0])
    r.G.wait(vstg[1])
    r.G.collective_compute("AllGather", ALU.bypass,
                           replica_groups=[[0, 1, 2, 3], [4, 5, 6, 7]],
                           ins=[k.d_l0v_in.ap()], outs=[k.d_l0v_out.ap()]
                           ).inc(s_ccV, 1)

    r.Y.wait(s_ccK)
    for rr in range(4):
        r.Y.dma_start(out=k.MAXB[0:1, rr * 16:(rr + 1) * 16],
                      in_=k.d_l0k_out.ap()[rr * 2049 + 2048: rr * 2049 + 2049, 0:16]
                      ).inc(s_maxk, 16)
    r.V.wait(s_maxk)
    r.V.wait(s_lmd)
    GAT = k.ROWSF[0:1, 0:64]
    r.V.tensor_copy(GAT, k.MAXB).inc(k.vch, 1)
    mk1, mk2 = k.ROWSF[0:1, 64:80], k.ROWSF[0:1, 80:96]
    r.V.wait(k.vch)
    r.V.tensor_tensor(mk1, k.ROWSF[0:1, 0:16], k.ROWSF[0:1, 16:32], ALU.max
                      ).inc(k.vch, 1)
    r.V.wait(k.vch)
    r.V.tensor_tensor(mk2, mk1, k.ROWSF[0:1, 32:48], ALU.max).inc(k.vch, 1)
    r.V.wait(k.vch)
    r.V.tensor_tensor(mk1, mk2, k.ROWSF[0:1, 48:64], ALU.max).inc(k.vch, 1)

    QSQ = k.SMB[:, 0:1024]
    for h in range(NH):
        r.S.wait(s_vstg)
        r.S.wait(k.s_pcc)
        r.S.activation(QSQ, k.ACT[:, h * 1024:(h + 1) * 1024], AF.Square
                       ).inc(k.sch, 1)
        r.T.wait(k.sch)
        r.T.wait(k.s_pcc)
        for n in range(2):
            r.T.matmul(k.PC[0:1, n * 512:(n + 1) * 512], k.ONESCB[:, 0:1],
                       QSQ[:, n * 512:(n + 1) * 512], start=True, stop=True)
        r.rlast("tensor").inc(k.s_pcp, 1)
        r.V.wait(k.s_pcp)
        r.V.wait(s_lmd)
        r.V.reduce_max(k.ROWSF[0:1, 160 + h:161 + h], k.PC[0:1, 0:1024], axis=AX.X
                       ).inc(k.s_pcc, 1).inc(k.vch, 1)
    r.S.wait(k.vch)
    r.S.activation(k.ROWSF[0:1, 96:112], k.ROWSF[0:1, 160:176], AF.Sqrt).inc(k.sch, 1)
    r.S.activation(k.ROWSF[0:1, 112:128], mk1, AF.Sqrt).inc(k.sch, 1)
    r.V.wait(k.sch)
    r.V.tensor_tensor(k.ROWSF[0:1, 128:144], k.ROWSF[0:1, 96:112],
                      k.ROWSF[0:1, 112:128], ALU.mult).inc(k.vch, 1)
    r.V.wait(k.vch)
    r.V.tensor_scalar(k.ROWSF[0:1, 144:160], k.ROWSF[0:1, 128:144],
                      -INV_SQRT_HD, None, ALU.mult).inc(k.vch, 1)
    NEGMB = k.SMT[:, 102:118]
    r.T.wait(k.vch)
    r.T.wait(k.s_pcc)
    r.T.matmul(k.PC[:, 0:16], k.ONES_F, k.ROWSF[0:1, 144:160], start=True, stop=True
               ).inc(k.s_pcp, 1)
    r.V.wait(k.s_pcp)
    r.V.tensor_copy(NEGMB, k.PC[:, 0:16]).inc(k.s_pcc, 1).inc(k.vch, 1)

    r.Y.wait(s_ccK)
    r.Y.wait(s_ccV)
    PBUF = [k.SMB[:, 0:1024], k.SMB[:, 1024:2048]]
    pv_done = [None, None]
    nm_done_v = k.vch.v
    for h in range(NH):
        ksrc = []
        for rr in range(4):
            ksrc.append((None, k.d_l0k_out.ap()[rr * 2049 + h * 128:
                                                rr * 2049 + (h + 1) * 128, :]))
        first, span = k.ws.ring[k.ws.i % 3]
        buK = None
        sl0 = k.ws.slots[first]
        for s_ in k.ws.slots[first:first + span]:
            if s_.release_at is not None:
                r.Y.wait_val(s_.release_at[0], s_.release_at[1])
        ldy = sl0.loaded["sync"]
        for rr in range(4):
            r.Y.dma_start(out=k.WBUF[:, first * 2048 + rr * 1024:
                                     first * 2048 + (rr + 1) * 1024],
                          in_=ksrc[rr][1]).inc(ldy, 16)
        buK = BufUse(k.WBUF[:, first * 2048:first * 2048 + 4096], ldy,
                     ldy.v, sl0.freed, k.ws.slots[first:first + span], k.ws)
        k.ws.i += 1
        buV = k.ws.load(
            k.d_l0v_out.ap()[:, h * 128:(h + 1) * 128].rearrange(
                "(kt p) j -> p kt j", p=128), 4096, queue="sync")
        for qb in range(2):
            Q = k.ACT[:, h * 1024 + qb * 512: h * 1024 + qb * 512 + 512]
            first_pv = True
            for kg in range(16):
                sl = kg % 2
                Ps, sps, spc = (k.PA, k.s_pap, k.s_pac) if sl == 0 else \
                               (k.PB, k.s_pbp, k.s_pbc)
                if kg == 0 and qb == 0:
                    buK.acquire(r.T)
                    buV.acquire(r.T)
                r.T.wait(spc)
                for kt2 in range(2):
                    kt = kg * 2 + kt2
                    r.T.matmul(Ps[:, kt2 * 512:(kt2 + 1) * 512],
                               buK.ap[:, kt * 128:(kt + 1) * 128], Q,
                               start=True, stop=True)
                r.rlast("tensor").inc(sps, 1)
                r.S.wait(sps)
                if h == 0 and qb == 0 and kg == 0:
                    r.S.wait_val(k.vch, nm_done_v)
                if pv_done[sl] is not None:
                    r.S.wait_val(s_pvd, pv_done[sl])
                r.S.activation(PBUF[sl], Ps, AF.Exp, bias=NEGMB[:, h:h + 1],
                               scale=INV_SQRT_HD).inc(spc, 1).inc(s_exp, 1)
                r.T.wait(s_exp)
                if first_pv:
                    r.T.wait(k.s_pcc)
                    first_pv = False
                for kt2 in range(2):
                    kt = kg * 2 + kt2
                    r.T.matmul(k.PC[:, 0:512], buV.ap[:, kt * 128:(kt + 1) * 128],
                               PBUF[sl][:, kt2 * 512:(kt2 + 1) * 512],
                               start=(kg == 0 and kt2 == 0),
                               stop=(kg == 15 and kt2 == 1))
                    r.T.matmul(k.PC[0:1, 512:1024], k.ONESCB[:, 0:1],
                               PBUF[sl][:, kt2 * 512:(kt2 + 1) * 512],
                               start=(kg == 0 and kt2 == 0),
                               stop=(kg == 15 and kt2 == 1))
                r.rlast("tensor").inc(s_pvd, 1)
                pv_done[sl] = s_pvd.v
            r.rlast("tensor").inc(k.s_pcp, 1)
            if qb == 1:
                i = r.rlast("tensor")
                buK.release(i)
                buV.release(i)
            r.V.wait(k.s_pcp)
            r.V.reciprocal(k.ZROW[0:1, 0:512], k.PC[0:1, 512:1024]).inc(s_zr, 1)
            r.T.wait(s_zr)
            r.T.matmul(k.PC[:, 512:1024], k.ONES_F, k.ZROW[0:1, 0:512],
                       start=True, stop=True).inc(s_zb, 1)
            r.V.wait(s_zb)
            r.V.tensor_copy(ZB, k.PC[:, 512:1024]).inc(k.vch, 1)
            r.V.wait(k.vch)
            r.V.tensor_tensor(k.LNX[:, h * 1024 + qb * 512:
                                    h * 1024 + qb * 512 + 512],
                              k.PC[:, 0:512], ZB, ALU.mult
                              ).inc(k.s_pcc, 1).inc(s_evac, 1)

    r.G.wait_val(s_proj, s_proj.v)
    r.G.dma_start(out=k.BROW, in_=k.d_l0_bo).inc(s_br, 16)
    for m in range(NH):
        def evo(P, sp, sc, m=m):
            r.V.wait(sp)
            xsl = k.X[:, m * 1024:(m + 1) * 1024]
            r.V.tensor_tensor(xsl, xsl, P, ALU.add).inc(sc, 1)
        if m == 0:
            r.T.wait(s_evac)
        proj_wo(k, k.d_l0_wo, k.BROW, m, m % 2 == 0, evo, s_br, s_proj)


def proj_wo(k, dunits, brow, h, ping, out_cb, s_br, s_proj):
    r = k.r
    bu = k.ws.load(dunits[h], 2048)
    P, sp, sc = (k.PA, k.s_pap, k.s_pac) if ping else (k.PB, k.s_pbp, k.s_pbc)
    bu.acquire(r.T)
    r.T.wait(sc)
    r.T.wait(s_br)
    for c in range(NC):
        for n in range(2):
            r.T.matmul(P[:, n * 512:(n + 1) * 512],
                       bu.ap[:, c * 128:(c + 1) * 128],
                       k.LNX[:, c * 1024 + n * 512: c * 1024 + n * 512 + 512],
                       start=(c == 0), stop=False)
    for n in range(2):
        r.T.matmul(P[:, n * 512:(n + 1) * 512],
                   brow[0:1, h * 128:(h + 1) * 128],
                   k.ONES_BF, start=False, stop=True)
    i = k.r.rlast("tensor").inc(sp, 1).inc(s_proj, 1)
    bu.release(i)
    out_cb(P, sp, sc)



def ph_prep1(k):
    r = k.r
    if getattr(k, "_prep_done", False):
        return
    k._prep_done = True
    s_q = r.sem("p1_q")
    s_sgfree = r.sem("p1_sgfree")
    A0 = k.SMT[:, 224:225]
    TMPC = k.SMT[:, 227:228]
    ONEC = k.SMT[:, 226:227]
    THR = k.SMT[:, 229:230]
    AB = k.SMF
    SG = k.SMB[:, 0:2048]
    MK = k.SMB[:, 2048:4096]
    WQ = [k.ACT[:, 0:2048], k.ACT[:, 2048:4096]]
    s_wq = [r.sem("p1_wq0"), r.sem("p1_wq1")]

    def units():
        for u in range(DFF // 128):
            yield k.d_w1[u], ("w1", u)
        for m in range(NC):
            for q in range(4):
                yield k.d_w2[m][:, q * 2048:(q + 1) * 2048], ("w2", m, q)

    A1 = k.SMT[:, 225:226]
    THR2 = k.SMT[:, 230:231]
    r.G.wait_val(r.bar, r.bar.v)
    s_onec = r.sem("p1_onec")
    r.Y.wait_val(r.bar, r.bar.v)
    r.Y.dma_start(out=ONEC, in_=k.d_onescol).inc(s_onec, 16)
    r.V.memset(A0, 0.0).inc(k.vch, 1)
    r.V.memset(A1, 0.0).inc(k.vch, 1)
    for i, (src, tag) in enumerate(units()):
        bu = k.ws32.load(src, 2048, queue="gpsimd")
        bu.acquire(r.V)
        r.V.tensor_reduce(TMPC, bu.ap, AX.X, ALU.add, apply_absolute_value=True
                          ).inc(k.vch, 1)
        bu.release(r.rlast("vector"))
        r.V.wait(k.vch)
        acc = A0 if tag[0] == "w1" else A1
        r.V.tensor_tensor(acc, acc, TMPC, ALU.add).inc(k.vch, 1)
        r.V.wait(k.vch)
    for acc, thr in ((A0, THR), (A1, THR2)):
        r.T.wait(k.vch)
        r.T.wait(s_onec)
        r.T.wait(k.s_pcc)
        r.T.matmul(k.PC[0:1, 0:1], acc, ONEC, start=True, stop=True).inc(k.s_pcp, 1)
        r.V.wait(k.s_pcp)
        r.V.tensor_scalar(k.SMT[0:1, 228:229], k.PC[0:1, 0:1], 0.7 / float(DFF * D),
                          None, ALU.mult).inc(k.s_pcc, 1).inc(k.vch, 1)
        r.T.wait(k.vch)
        r.T.wait(k.s_pcc)
        r.T.matmul(k.PC[:, 0:1], k.ONES_F, k.SMT[0:1, 228:229], start=True, stop=True
                   ).inc(k.s_pcp, 1)
        r.V.wait(k.s_pcp)
        r.V.tensor_copy(thr, k.PC[:, 0:1]).inc(k.s_pcc, 1).inc(k.vch, 1)

    NTHR = k.SMT[:, 231:232]
    NTHR2 = k.SMT[:, 232:233]
    r.V.wait(k.vch)
    r.V.tensor_scalar(NTHR, THR, -1.0, None, ALU.mult).inc(k.vch, 1)
    r.V.tensor_scalar(NTHR2, THR2, -1.0, None, ALU.mult).inc(k.vch, 1)
    r.V.wait(k.vch)
    for i, (src, tag) in enumerate(units()):
        bu = k.ws32.load(src, 2048, queue="gpsimd")
        sl = i % 2
        w1 = tag[0] == "w1"
        sc2 = 1.0 if w1 else 0.5
        bu.acquire(r.V)
        r.V.tensor_scalar(SG, bu.ap, THR if w1 else THR2, sc2,
                          ALU.is_ge, ALU.mult).inc(k.vch, 1)
        i2 = r.V.tensor_scalar(MK, bu.ap, NTHR if w1 else NTHR2, sc2,
                               ALU.is_le, ALU.mult)
        i2.inc(k.vch, 1)
        bu.release(i2)
        r.V.wait(k.vch)
        r.V.wait(s_wq[sl])
        r.V.tensor_tensor(WQ[sl], SG, MK, ALU.subtract).inc(k.vch, 1)
        r.Y.wait(k.vch)
        if w1:
            dst = k.d_w1q.ap()[tag[1]]
        else:
            dst = k.d_w2q.ap()[tag[1]][:, tag[2] * 2048:(tag[2] + 1) * 2048]
        r.Y.dma_start(out=dst, in_=WQ[sl]).inc(s_wq[sl], 16)
    r.V.wait(s_wq[0])
    r.V.wait(s_wq[1])



def ph_ffn1(k):
    r = k.r
    if not getattr(k, "_b2h_done", False):
        k._b2h_done = True
        r.V.wait_val(r.bar, r.bar.v)
        r.V.tensor_scalar(k.SMT[:, 320:336], k.SMT[:, 200:216], 0.5, None,
                          ALU.mult).inc(k.vch, 1)
        k._b2h_v = k.vch.v
    B2H = k.SMT[:, 320:336]
    s_h = r.sem("f1_h")
    for tt2 in range(2):
        toff = tt2 * 512
        for fc in range(DFF // 128):
            bu = k.ws.load(k.d_w1q.ap()[fc], 2048, queue="sync")
            P, sp, sc = (k.PA, k.s_pap, k.s_pac) if fc % 2 == 0 else \
                         (k.PB, k.s_pbp, k.s_pbc)
            bu.acquire(r.T)
            r.T.wait(sc)
            if fc == 0:
                r.T.wait(s_h)
            for c in range(NC):
                r.T.matmul(P[:, 0:512], bu.ap[:, c * 128:(c + 1) * 128],
                           k.LNX[:, c * 1024 + toff: c * 1024 + toff + 512],
                           start=(c == 0), stop=(c == NC - 1))
            i = r.rlast("tensor").inc(sp, 1)
            bu.release(i)
            r.S.wait(sp)
            r.S.activation(k.ACT[:, fc * 512:(fc + 1) * 512], P[:, 0:512], AF.Gelu,
                           bias=k.SMT[:, 136 + fc:137 + fc]).inc(sc, 1).inc(s_h, 1)
        r.T.wait(s_h)
        for m in range(NC):
            P, sp, sc = (k.PC[:, 0:512], k.s_pcp, k.s_pcc) if m % 2 == 0 else \
                         (k.PD[:, 0:512], k.s_pdp, k.s_pdc)
            r.T.wait(sc)
            for q in range(4):
                bu = k.ws.load(k.d_w2q.ap()[m][:, q * 2048:(q + 1) * 2048], 2048,
                               queue="sync")
                bu.acquire(r.T)
                for c in range(NC):
                    r.T.matmul(P, bu.ap[:, c * 128:(c + 1) * 128],
                               k.ACT[:, (q * NC + c) * 512:(q * NC + c + 1) * 512],
                               start=(q == 0 and c == 0), stop=(q == 3 and c == NC - 1))
                bu.release(r.rlast("tensor"))
            r.rlast("tensor").inc(sp, 1)
            r.V.wait(sp)
            r.V.wait_val(k.vch, k._b2h_v)
            xsl = k.X[:, m * 1024 + toff: m * 1024 + toff + 512]
            r.V.scalar_tensor_tensor(xsl, P, B2H[:, m:m + 1], xsl,
                                     ALU.add, ALU.add).inc(sc, 1).inc(s_h, 1)



def ph_attn1(k):
    r = k.r
    s_br = r.sem("a1_brow")
    s_proj = r.sem("a1_proj")
    s_ar = r.sem("a1_ar")
    s_arb = r.sem("a1_arb")
    s_cc = r.sem("a1_cc")
    s_kv = r.sem("a1_kv")
    s_dec = r.sem("a1_dec")
    KF = k.ACT32[:, 0:1024]
    VF = k.ACT32[:, 1024:2048]
    KT = k.SMB[:, 2048:3072]
    VT = k.SMB[:, 3072:4096]
    KSUM = k.SMT[:, 0:16]

    r.G.wait_val(r.bar, r.bar.v)
    r.G.dma_start(out=k.BROW, in_=k.d_l1_bv).inc(s_br, 16)
    r.G.dma_start(out=k.BROW2, in_=k.d_l1_bk).inc(s_br, 16)
    r.Y.wait_val(r.bar, r.bar.v)
    r.Y.dma_start(out=k.ROWSF, in_=k.d_decay_row).inc(s_dec, 16)

    def proj(units, brow, h, P, sp, sc):
        bu = k.ws.load(units[h], 2048)
        bu.acquire(r.T)
        r.T.wait(sc)
        r.T.wait(s_br)
        for c in range(NC):
            for n in range(2):
                r.T.matmul(P[:, n * 512:(n + 1) * 512],
                           bu.ap[:, c * 128:(c + 1) * 128],
                           k.LNX[:, c * 1024 + n * 512: c * 1024 + n * 512 + 512],
                           start=(c == 0), stop=False)
        for n in range(2):
            r.T.matmul(P[:, n * 512:(n + 1) * 512],
                       brow[0:1, h * 128:(h + 1) * 128],
                       k.ONES_BF, start=False, stop=True)
        i = r.rlast("tensor").inc(sp, 1).inc(s_proj, 1)
        bu.release(i)

    ptile = [r.sem("a1_pt0"), r.sem("a1_pt1")]
    ptilec = [r.sem("a1_pt0c"), r.sem("a1_pt1c")]
    for h in range(NH):
        proj(k.d_l1_wk, k.BROW2, h, k.PA, k.s_pap, k.s_pac)
        r.S.wait(k.s_pap)
        r.S.wait(s_kv)
        r.S.activation(KF, k.PA, AF.Relu).inc(k.s_pac, 1).inc(k.sch, 1)
        proj(k.d_l1_wv, k.BROW, h, k.PB, k.s_pbp, k.s_pbc)
        r.S.wait(k.s_pbp)
        r.S.wait(s_kv)
        r.S.activation(VF, k.PB, AF.Copy).inc(k.s_pbc, 1).inc(k.sch, 1)
        r.T.wait(k.sch)
        for tt in range(8):
            sl = tt % 2
            r.T.wait(ptilec[sl])
            r.T.transpose(k.PD[:, sl * 512:sl * 512 + 128],
                          KF[:, tt * 128:(tt + 1) * 128], k.IDENT).inc(ptile[sl], 1)
            r.V.wait(ptile[sl])
            r.V.wait(s_dec)
            r.V.tensor_scalar(KT[:, tt * 128:(tt + 1) * 128],
                              k.PD[:, sl * 512:sl * 512 + 128],
                              k.SMT[:, 216 + tt:217 + tt], None, ALU.mult
                              ).inc(ptilec[sl], 1).inc(k.vch, 1)
        for tt in range(8):
            sl = tt % 2
            r.T.wait(ptilec[sl])
            i = r.T.transpose(k.PD[:, sl * 512:sl * 512 + 128],
                              VF[:, tt * 128:(tt + 1) * 128], k.IDENT
                              ).inc(ptile[sl], 1)
            if tt >= 6:
                i.inc(s_kv, 1)
            r.V.wait(ptile[sl])
            r.V.tensor_copy(VT[:, tt * 128:(tt + 1) * 128],
                            k.PD[:, sl * 512:sl * 512 + 128]
                            ).inc(ptilec[sl], 1).inc(k.vch, 1)
        r.T.wait(k.s_pcc)
        r.T.wait(k.vch)
        for tt in range(8):
            r.T.matmul(k.PC[:, 0:128], KT[:, tt * 128:(tt + 1) * 128],
                       VT[:, tt * 128:(tt + 1) * 128],
                       start=(tt == 0), stop=(tt == 7))
        for tt in range(8):
            r.T.matmul(k.PC[:, 128:129], KT[:, tt * 128:(tt + 1) * 128],
                       k.ONESCB[:, 0:1], start=(tt == 0), stop=(tt == 7))
        r.rlast("tensor").inc(k.s_pcp, 1)
        r.V.wait(k.s_pcp)
        r.V.tensor_copy(k.KVS[:, h * 128:(h + 1) * 128], k.PC[:, 0:128]
                        ).inc(k.vch, 1)
        r.V.tensor_copy(KSUM[:, h:h + 1], k.PC[:, 128:129]
                        ).inc(k.s_pcc, 1).inc(k.vch, 1)

    r.Y.wait(k.vch)
    r.Y.dma_start(out=k.d_l1cc_in.ap()[:, 0:2048], in_=k.KVS).inc(s_cc, 16)
    r.Y.dma_start(out=k.d_l1cc_in.ap()[:, 2048:2064], in_=KSUM).inc(s_cc, 16)
    r.G.wait_val(s_cc, s_cc.v)
    r.G.collective_compute("AllReduce", ALU.add,
                           replica_groups=[[0, 1, 2, 3], [4, 5, 6, 7]],
                           ins=[k.d_l1cc_in.ap()], outs=[k.d_l1cc_out.ap()]
                           ).inc(s_ar, 1)

    r.G.wait_val(s_proj, s_proj.v)
    r.G.dma_start(out=k.BROW2, in_=k.d_l1_bq).inc(s_br, 16)
    DECB = k.SMB[:, 0:1024]
    QTMP = k.SMB[:, 1024:2048]
    r.T.wait(s_dec)
    r.T.wait(k.s_pdc)
    r.T.wait(ptilec[0])
    r.T.wait(ptilec[1])
    for n in range(2):
        r.T.matmul(k.PD[:, n * 512:(n + 1) * 512], k.ONES_F,
                   k.ROWSF[0:1, n * 512:(n + 1) * 512], start=True, stop=True)
    r.rlast("tensor").inc(k.s_pdp, 1)
    r.V.wait(k.s_pdp)
    r.V.wait(s_kv)
    r.V.tensor_copy(DECB, k.PD).inc(k.s_pdc, 1).inc(k.vch, 1)
    for h in range(NH):
        P, sp, sc = (k.PA, k.s_pap, k.s_pac) if h % 2 == 0 else \
                    (k.PB, k.s_pbp, k.s_pbc)
        proj(k.d_l1_wq, k.BROW2, h, P, sp, sc)
        r.S.wait(sp)
        r.S.wait(k.vch)
        r.S.activation(QTMP, P, AF.Relu).inc(sc, 1).inc(k.sch, 1)
        r.V.wait(k.sch)
        r.V.tensor_tensor(k.ACT[:, h * 1024:(h + 1) * 1024], QTMP, DECB,
                          ALU.mult).inc(k.vch, 1)

    KVB = k.SMB[:, 2048:4096]
    KSB = k.ONESCB[:, 2:18]
    r.G.wait_val(s_ar, s_ar.v)
    r.G.dma_start(out=KVB, in_=k.d_l1cc_out.ap()[:, 0:2048]).inc(s_arb, 16)
    r.G.dma_start(out=KSB, in_=k.d_l1cc_out.ap()[:, 2048:2064]).inc(s_arb, 16)

    s_z = r.sem("a1_z")
    ZBB = k.SMB[:, 0:1024]
    for h in range(NH):
        P, sp, sc = (k.PA, k.s_pap, k.s_pac) if h % 2 == 0 else \
                    (k.PB, k.s_pbp, k.s_pbc)
        r.T.wait(sc)
        r.T.wait(s_arb)
        Q0 = k.ACT[:, h * 1024: h * 1024 + 512]
        Q1 = k.ACT[:, h * 1024 + 512: h * 1024 + 1024]
        r.T.wait(k.vch)
        for n, Qn in enumerate((Q0, Q1)):
            r.T.matmul(P[:, n * 512:(n + 1) * 512], KVB[:, h * 128:(h + 1) * 128],
                       Qn, start=True, stop=True)
        r.rlast("tensor").inc(sp, 1)
        r.T.wait(k.s_pcc)
        for n, Qn in enumerate((Q0, Q1)):
            r.T.matmul(k.PC[0:1, n * 512:(n + 1) * 512], KSB[:, h:h + 1],
                       Qn, start=True, stop=True)
        r.rlast("tensor").inc(k.s_pcp, 1)
        r.V.wait(k.s_pcp)
        r.V.tensor_scalar(k.ROWSF, k.PC[0:1, 0:1024], 1e-6, None, ALU.add
                          ).inc(k.s_pcc, 1).inc(k.vch, 1)
        r.V.wait(k.vch)
        r.V.reciprocal(k.ZROW, k.ROWSF).inc(s_z, 1)
        r.T.wait(s_z)
        r.T.wait(k.s_pdc)
        for n in range(2):
            r.T.matmul(k.PD[:, n * 512:(n + 1) * 512], k.ONES_F,
                       k.ZROW[0:1, n * 512:(n + 1) * 512], start=True, stop=True)
        r.rlast("tensor").inc(k.s_pdp, 1)
        r.V.wait(k.s_pdp)
        r.V.tensor_copy(ZBB, k.PD).inc(k.s_pdc, 1).inc(k.vch, 1)
        r.V.wait(k.vch)
        r.V.wait(sp)
        r.V.tensor_tensor(k.ACT[:, 16384 + h * 1024: 16384 + (h + 1) * 1024],
                          P, ZBB, ALU.mult).inc(sc, 1).inc(k.vch, 1)

    s_bo = r.sem("a1_bo")
    r.G.wait_val(s_proj, s_proj.v)
    r.G.wait(k.s_pdp)
    r.G.dma_start(out=k.BROW2, in_=k.d_l1_bo).inc(s_bo, 16)
    for m in range(NH):
        bu = k.ws.load(k.d_l1_wo[m], 2048)
        P, sp, sc = (k.PA, k.s_pap, k.s_pac) if m % 2 == 0 else \
                    (k.PB, k.s_pbp, k.s_pbc)
        bu.acquire(r.T)
        r.T.wait(sc)
        r.T.wait(s_bo)
        if m == 0:
            r.T.wait(k.vch)
        for c in range(NC):
            for n in range(2):
                r.T.matmul(P[:, n * 512:(n + 1) * 512],
                           bu.ap[:, c * 128:(c + 1) * 128],
                           k.ACT[:, 16384 + c * 1024 + n * 512:
                                 16384 + c * 1024 + n * 512 + 512],
                           start=(c == 0), stop=False)
        for n in range(2):
            r.T.matmul(P[:, n * 512:(n + 1) * 512],
                       k.BROW2[0:1, m * 128:(m + 1) * 128],
                       k.ONES_BF, start=False, stop=True)
        i = r.rlast("tensor").inc(sp, 1)
        bu.release(i)
        r.V.wait(sp)
        xsl = k.X[:, m * 1024:(m + 1) * 1024]
        r.V.tensor_tensor(xsl, xsl, P, ALU.add).inc(sc, 1)



_PROG_CACHE = {}


def _get_program(phases, dbg=None):
    key = (tuple(phases), dbg)
    if key not in _PROG_CACHE:
        k = build_program(tuple(phases), dbg=dbg)
        _PROG_CACHE[key] = k
    return _PROG_CACHE[key]


def kernel(**inputs):
    pre = _prep(inputs)
    x = pre["x_cores"]
    for g in GROUPS:
        k = _get_program(g)
        in_maps = []
        for c in range(8):
            m = {"x": x[c]}
            for nm in k.innames:
                if nm in ("decay_pp", "decay_row"):
                    m[nm] = pre[nm][c]
                else:
                    m[nm] = pre[nm]
            in_maps.append(m)
        res = run_bass_kernel_spmd(k.nc, in_maps, core_ids=list(range(8)))
        x = [res.results[c]["y"] for c in range(8)]
    out = np.empty((2, S, D), np.float32)
    for c in range(8):
        out[c // 4, (c % 4) * T:(c % 4 + 1) * T, :] = x[c].T
    return out


# version 05
# speedup vs baseline: 1.0921x; 1.0290x over previous
import numpy as np
import ml_dtypes
import concourse.bass as bass
import concourse.mybir as mybir
from concourse.bass_utils import run_bass_kernel_spmd

F32 = mybir.dt.float32
BF16 = mybir.dt.bfloat16
AF = mybir.ActivationFunctionType
ALU = mybir.AluOpType
AX = mybir.AxisListType

D = 2048
T = 1024
NC = 16
NH = 16
HD = 128
S = 4096
NE = 16
NPE = 18
FMOE = 1024
DFF = 8192
EPS_LN = 1e-5
INV_SQRT_HD = 1.0 / float(np.sqrt(128.0))

ALL_PHASES = ("ln", "moe", "ln", "attn0", "ln", "moe",
              "prep1", "ln", "ffn1", "ln", "attn1", "ln", "ffn1")

GROUPS = [ALL_PHASES]



def _blk_lhsT(WT):
    K, M = WT.shape
    a = WT.reshape(K // 128, 128, M // 128, 128)
    return np.ascontiguousarray(a.transpose(2, 1, 0, 3).reshape(M // 128, 128, K))


def _prep(inp):
    g = {}
    f32 = lambda a: np.ascontiguousarray(np.asarray(a), dtype=np.float32)

    x = f32(inp["L0_attn_x"])
    g["x_cores"] = [np.ascontiguousarray(x[c // 4, (c % 4) * T:(c % 4 + 1) * T, :].T)
                    for c in range(8)]

    for nm in ("Wq", "Wk", "Wv", "Wo"):
        W = f32(inp[f"L0_attn_{nm}"])
        g[f"l0_{nm.lower()}"] = _blk_lhsT(np.ascontiguousarray(W.T))
    for nm in ("bq", "bk", "bv", "bo"):
        g[f"l0_{nm}"] = f32(inp[f"L0_attn_{nm}"]).reshape(1, D)

    gw = f32(inp["L0_ffn_gate_weight"])
    g["gwt"] = np.ascontiguousarray(gw.T.reshape(NC, 128, NE))
    g["e_bias"] = f32(inp["L0_ffn_e_score_correction_bias"]).reshape(1, NE)
    gate_w = f32(inp["L0_ffn_gate_w"])
    up_w = f32(inp["L0_ffn_up_w"])
    down_w = f32(inp["L0_ffn_down_w"])
    shg = f32(inp["L0_ffn_sh_gate_w"])
    shu = f32(inp["L0_ffn_sh_up_w"])
    shd = f32(inp["L0_ffn_sh_down_w"])
    gu_units = np.empty((NPE, 8, 128, 4096), np.float32)
    for e in range(NPE):
        if e < NE:
            gmat, umat = gate_w[e], up_w[e]
        else:
            lo = (e - NE) * FMOE
            gmat, umat = shg[lo:lo + FMOE], shu[lo:lo + FMOE]
        gb = _blk_lhsT(np.ascontiguousarray(gmat.T))
        ub = _blk_lhsT(np.ascontiguousarray(umat.T))
        gu_units[e, :, :, :2048] = gb
        gu_units[e, :, :, 2048:] = ub
    g["l0_gu"] = gu_units.reshape(NPE * 8, 128, 4096)

    groups = [[0, 1, 2, 3], [4, 5, 6, 7], [8, 9, 10, 11], [12, 13, 14, 15], [16, 17]]
    dmats = [down_w[e].T if e < NE else
             (shd[:, (e - NE) * FMOE:(e - NE + 1) * FMOE]).T for e in range(NPE)]
    dblk = [_blk_lhsT(np.ascontiguousarray(m)) for m in dmats]
    full = np.empty((4 * NC, 128, 4096), np.float32)
    for gi in range(4):
        for hc in range(NC):
            full[gi * NC + hc] = np.concatenate(
                [dblk[e][hc] for e in groups[gi]], axis=1)
    g["l0_down"] = full
    shared = np.empty((NC, 128, 2048), np.float32)
    for hc in range(NC):
        shared[hc] = np.concatenate([dblk[16][hc], dblk[17][hc]], axis=1)
    g["l0_downsh"] = shared

    for nm in ("Wq", "Wk", "Wv", "Wo"):
        W = f32(inp[f"L1_attn_{nm}"])
        g[f"l1_{nm.lower()}"] = _blk_lhsT(np.ascontiguousarray(W.T))
    for nm in ("bq", "bk", "bv", "bo"):
        g[f"l1_{nm}"] = f32(inp[f"L1_attn_{nm}"]).reshape(1, D)

    w1 = f32(inp["L1_ffn_w1"])
    w2 = f32(inp["L1_ffn_w2"])
    g["l1_w1"] = _blk_lhsT(np.ascontiguousarray(w1.T))
    g["l1_w2"] = _blk_lhsT(np.ascontiguousarray(w2.T))
    b1 = f32(inp["L1_ffn_b1"])
    g["b1col"] = np.ascontiguousarray(b1.reshape(DFF // 128, 128).T)
    b2 = f32(inp["L1_ffn_b2"])
    g["b2col"] = np.ascontiguousarray(b2.reshape(NC, 128).T)

    g["ones_row"] = np.ones((1, 512), np.float32)
    g["onescol"] = np.ones((128, 1), np.float32)
    g["ident"] = np.eye(128, dtype=np.float32)
    sel = np.zeros((NE, NE, 128), np.float32)
    for e in range(NE):
        sel[e, e, :] = 1.0
    g["sel"] = sel
    pos = np.arange(S, dtype=np.float64)
    decay = np.exp(-pos / float(S)).astype(np.float32)
    g["decay_pp"] = [np.ascontiguousarray(
        decay[(c % 4) * T:(c % 4 + 1) * T].reshape(8, 128).T) for c in range(8)]
    g["decay_row"] = [np.ascontiguousarray(
        decay[(c % 4) * T:(c % 4 + 1) * T].reshape(1, T)) for c in range(8)]
    return g



ENGINES = ("tensor", "vector", "scalar", "gpsimd", "sync")


class Sem:
    __slots__ = ("h", "v")

    def __init__(self, nc, name):
        self.h = nc.alloc_semaphore(name)
        self.v = 0


class RecInst:
    __slots__ = ("item",)

    def __init__(self, item):
        self.item = item

    def inc(self, sem, n=1):
        sem.v += n
        self.item[3].append((sem.h, n))
        return self


class Eng:

    def __init__(self, rec, name):
        self.rec = rec
        self.name = name

    def __getattr__(self, meth):
        def call(*a, **k):
            item = [meth, a, k, []]
            self.rec.q[self.name].append(item)
            self.rec.last[self.name] = item
            return RecInst(item)
        return call

    def wait(self, sem):
        if sem.v > 0:
            self.rec.q[self.name].append(["wait_ge", (sem.h, sem.v), {}, []])

    def wait_val(self, sem, v):
        self.rec.q[self.name].append(["wait_ge", (sem.h, v), {}, []])


class Rec:
    def __init__(self, nc):
        self.nc = nc
        self.q = {e: [] for e in ENGINES}
        self.last = {e: None for e in ENGINES}
        self.T = Eng(self, "tensor")
        self.V = Eng(self, "vector")
        self.S = Eng(self, "scalar")
        self.G = Eng(self, "gpsimd")
        self.Y = Eng(self, "sync")
        self._sems = {}
        self.bar = Sem(nc, "bar")
        self._bar_last = {e: None for e in ENGINES}

    def sem(self, name):
        s = self._sems.get(name)
        if s is None:
            s = self._sems[name] = Sem(self.nc, name)
        return s

    def rlast(self, eng):
        return RecInst(self.last[eng])

    def barrier(self, engines=("tensor", "vector", "scalar")):
        n = 0
        for e in engines:
            it = self.last[e]
            if it is not None and it is not self._bar_last[e]:
                RecInst(it).inc(self.bar, 1)
                self._bar_last[e] = it
                n += 1
        if n:
            for e in engines:
                self.q[e].append(["wait_ge", (self.bar.h, self.bar.v), {}, []])

    def replay(self):
        with self.nc.Block() as blk:
            for name in ENGINES:
                qq = self.q[name]

                def run(eng, qq=qq, name=name):
                    for meth, a, k, incs in qq:
                        inst = getattr(eng, meth)(*a, **k)
                        if not incs:
                            continue
                        if meth in ("dma_start", "collective_compute"):
                            assert len(incs) == 1, (meth, incs)
                            inst.then_inc(*incs[0])
                            continue
                        inst.then_inc(*incs[0])
                        for (h, n) in incs[1:]:
                            eng.drain().then_inc(h, n)
                getattr(blk, name)(run)


class BufUse:

    __slots__ = ("ap", "loaded", "wait_v", "freed", "slots_span", "stream")

    def __init__(self, ap, loaded, wait_v, freed, slots_span, stream):
        self.ap = ap
        self.loaded = loaded
        self.wait_v = wait_v
        self.freed = freed
        self.slots_span = slots_span
        self.stream = stream

    def acquire(self, eng):
        eng.wait_val(self.loaded, self.wait_v)

    def release(self, inst):
        inst.inc(self.freed, 1)
        mark = (self.freed, self.freed.v)
        for s in self.slots_span:
            s.release_at = mark


class Slot:
    def __init__(self, rec, idx):
        self.idx = idx
        self.loaded = {"gpsimd": rec.sem(f"wb_ldg{idx}"),
                       "sync": rec.sem(f"wb_ldy{idx}")}
        self.freed = rec.sem(f"wb_fr{idx}")
        self.release_at = None


class Stream:

    def __init__(self, rec, wbuf, slots, ring, slot_cols=2048):
        self.rec = rec
        self.wbuf = wbuf
        self.slots = slots
        self.ring = ring
        self.slot_cols = slot_cols
        self.i = 0

    def load(self, src_ap, ncols, queue="gpsimd", ndma=1):
        first, span = self.ring[self.i % len(self.ring)]
        self.i += 1
        sl = self.slots[first]
        eng = self.rec.G if queue == "gpsimd" else self.rec.Y
        spanned = self.slots[first:first + span]
        for s in spanned:
            r = s.release_at
            if r is not None:
                eng.wait_val(r[0], r[1])
        dst = self.wbuf[:, first * self.slot_cols: first * self.slot_cols + ncols]
        ld = sl.loaded[queue]
        eng.dma_start(out=dst, in_=src_ap).inc(ld, 16)
        return BufUse(dst, ld, ld.v, sl.freed, spanned, self)



class K:
    pass


def _alias(nc, base_handle, name, shape, dtype, byte_off):
    addr = nc.lookup_mloc(base_handle).addr + byte_off
    return nc.alloc_sbuf_tensor_at(name, shape, dtype, offset=addr)


def build_program(phases, dbg=None):
    nc = bass.Bass("TRN2", target_bir_lowering=False)
    k = K()
    k.nc = nc
    k.phases = phases
    r = k.r = Rec(nc)
    k.dbg = dbg

    k.d_x = nc.dram_tensor("x", [D, T], F32, kind="ExternalInput").ap()
    k.d_y = nc.dram_tensor("y", [D, T], F32, kind="ExternalOutput").ap()
    k.innames = []

    def din(name, shape):
        k.innames.append(name)
        return nc.dram_tensor(name, list(shape), F32, kind="ExternalInput").ap()

    k.d_ones = din("ones_row", (1, 512))
    k.d_onescol = din("onescol", (128, 1))
    k.d_ident = din("ident", (128, 128))
    need = set(phases)
    if "moe" in need:
        k.d_gwt = din("gwt", (NC, 128, NE))
        k.d_ebias = din("e_bias", (1, NE))
        k.d_gu = din("l0_gu", (NPE * 8, 128, 4096))
        k.d_down = din("l0_down", (4 * NC, 128, 4096))
        k.d_downsh = din("l0_downsh", (NC, 128, 2048))
        k.d_sel = din("sel", (NE, NE, 128))
    if "attn0" in need:
        for nm in ("wq", "wk", "wv", "wo"):
            setattr(k, f"d_l0_{nm}", din(f"l0_{nm}", (NC, 128, D)))
        for nm in ("bq", "bk", "bv", "bo"):
            setattr(k, f"d_l0_{nm}", din(f"l0_{nm}", (1, D)))
        k.d_l0k_in = nc.dram_tensor("l0k_in", [2049, T], BF16, kind="Internal")
        k.d_l0k_out = nc.dram_tensor("l0k_out", [4 * 2049, T], BF16, kind="Internal")
        k.d_l0v_in = nc.dram_tensor("l0v_in", [T, D], BF16, kind="Internal")
        k.d_l0v_out = nc.dram_tensor("l0v_out", [S, D], BF16, kind="Internal")
    if "attn1" in need:
        for nm in ("wq", "wk", "wv", "wo"):
            setattr(k, f"d_l1_{nm}", din(f"l1_{nm}", (NC, 128, D)))
        for nm in ("bq", "bk", "bv", "bo"):
            setattr(k, f"d_l1_{nm}", din(f"l1_{nm}", (1, D)))
        k.d_decay_pp = din("decay_pp", (128, 8))
        k.d_decay_row = din("decay_row", (1, T))
        k.d_l1cc_in = nc.dram_tensor("l1cc_in", [128, 2064], F32, kind="Internal")
        k.d_l1cc_out = nc.dram_tensor("l1cc_out", [128, 2064], F32, kind="Internal")
    if "prep1" in need or "ffn1" in need:
        k.d_w1 = din("l1_w1", (DFF // 128, 128, D))
        k.d_w2 = din("l1_w2", (NC, 128, DFF))
        k.d_b1col = din("b1col", (128, 64))
        k.d_b2col = din("b2col", (128, NC))
        k.d_w1q = nc.dram_tensor("w1q", [DFF // 128, 128, D], BF16, kind="Internal")
        k.d_w2q = nc.dram_tensor("w2q", [NC, 128, DFF], BF16, kind="Internal")

    a = nc.alloc_sbuf_tensor
    k.X = a("Xr", [128, 16384], F32)[:, :]
    k.LNX = a("LNXr", [128, 16384], BF16)[:, :]
    k.ACTh = a("ACTr", [128, 32768], BF16)
    k.ACT = k.ACTh[:, :]
    k.WBUFh = a("WBUFr", [128, 12288], BF16)
    k.WBUF = k.WBUFh[:, :]
    k.SMFh = a("SMFr", [128, 2048], F32)
    k.SMF = k.SMFh[:, :]
    k.SMBh = a("SMBr", [128, 4096], BF16)
    k.SMB = k.SMBh[:, :]
    k.SMT = a("SMTr", [128, 360], F32)[:, :]
    k.ONES_BF = a("ONESBF", [1, 512], BF16)[:, :]
    k.ONES_F = a("ONESF", [1, 128], F32)[:, :]
    k.IDENT = a("IDENTr", [128, 128], F32)[:, :]
    k.IDENTB = a("IDENTBr", [128, 128], BF16)[:, :]
    k.ONESCB = a("ONESCBr", [128, 18], BF16)[:, :]
    k.MAXB = a("MAXBr", [1, 64], BF16)[:, :]
    k.ROWSF = a("ROWSFr", [1, 1024], F32)[:, :]

    k.ACT32 = _alias(nc, k.ACTh, "ACT32", [128, 16384], F32, 0)[:, :]
    k.KVS = _alias(nc, k.ACTh, "KVS", [128, 2048], F32, 32768)[:, :]
    k.BROW = _alias(nc, k.SMFh, "BROW", [1, 2048], BF16, 0)[:, :]
    k.BROW2 = _alias(nc, k.SMFh, "BROW2", [1, 2048], BF16, 4096)[:, :]
    k.WEM = _alias(nc, k.SMFh, "WEM", [16, 1024], F32, 4096)[:, :]
    k.ZROW = _alias(nc, k.SMFh, "ZROW", [1, 1024], F32, 4096)[:, :]
    k.SIL = _alias(nc, k.SMFh, "SIL", [128, 1024], F32, 0)[:, :]
    k.MROW = _alias(nc, k.SMBh, "MROW", [1, 1024], F32, 0)[:, :]
    k.RROW = _alias(nc, k.SMBh, "RROW", [1, 1024], F32, 4096)[:, :]

    k.PA = nc.alloc_psum_tensor("PA", [128, 1024], F32)[:, :]
    k.PB = nc.alloc_psum_tensor("PB", [128, 1024], F32)[:, :]
    k.PC = nc.alloc_psum_tensor("PC", [128, 1024], F32)[:, :]
    k.PD = nc.alloc_psum_tensor("PD", [128, 1024], F32)[:, :]
    for nm in ("pa", "pb", "pc", "pd"):
        setattr(k, f"s_{nm}p", r.sem(f"{nm}_p"))
        setattr(k, f"s_{nm}c", r.sem(f"{nm}_c"))

    k.slots = [Slot(r, i) for i in range(6)]
    k.ws = Stream(r, k.WBUF, k.slots, [(0, 2), (2, 2), (4, 2)])
    k.WB32 = _alias(nc, k.WBUFh, "WB32", [128, 6144], F32, 0)[:, :]
    k.ws32 = Stream(r, k.WB32, k.slots, [(0, 2), (2, 2), (4, 2)], slot_cols=1024)

    k.s_ld = r.sem("ld_const")
    k.s_ldg = r.sem("ld_constg")
    k.s_x = r.sem("x_io")
    k.vch = r.sem("vchain")
    k.sch = r.sem("schain")

    r.Y.dma_start(out=k.X.rearrange("p (c t) -> p c t", c=NC),
                  in_=k.d_x.rearrange("(c p) t -> p c t", p=128)).inc(k.s_x, 16)
    r.Y.dma_start(out=k.ONES_F, in_=k.d_ones[:, 0:128]).inc(k.s_ld, 16)
    r.Y.dma_start(out=k.IDENT, in_=k.d_ident).inc(k.s_ld, 16)
    r.G.dma_start(out=k.ONES_BF, in_=k.d_ones).inc(k.s_ldg, 16)
    r.G.dma_start(out=k.IDENTB, in_=k.d_ident).inc(k.s_ldg, 16)
    r.G.dma_start(out=k.ONESCB[:, 0:1], in_=k.d_onescol).inc(k.s_ldg, 16)
    if "prep1" in need or "ffn1" in need:
        r.Y.dma_start(out=k.SMT[:, 136:200], in_=k.d_b1col).inc(k.s_ld, 16)
        r.Y.dma_start(out=k.SMT[:, 200:216], in_=k.d_b2col).inc(k.s_ld, 16)
    if "attn1" in need:
        r.Y.dma_start(out=k.SMT[:, 216:224], in_=k.d_decay_pp).inc(k.s_ld, 16)

    for e in (r.T, r.V, r.S):
        e.wait(k.s_x)
        e.wait(k.s_ld)
        e.wait(k.s_ldg)
    r.G.wait(k.s_x)
    r.G.wait(k.s_ld)

    n_ln = 0
    for ph in phases:
        if ph == "ln":
            ph_ln(k)
            n_ln += 1
        elif ph == "moe":
            ph_moe(k)
        elif ph == "attn0":
            ph_attn0(k)
        elif ph == "prep1":
            ph_prep1(k)
        elif ph == "ffn1":
            ph_ffn1(k)
        elif ph == "attn1":
            ph_attn1(k)
        r.barrier()

    st = r.sem("store")
    r.Y.wait_val(r.bar, r.bar.v)
    r.Y.dma_start(out=k.d_y.rearrange("(c p) t -> p c t", p=128),
                  in_=k.X.rearrange("p (c t) -> p c t", c=NC)).inc(st, 16)
    nst = 16
    if dbg is not None:
        which, shape = dbg
        k.d_dbg = nc.dram_tensor("dbgout", list(shape), BF16 if which in ("LNX", "ACT", "WBUF", "SMB") else F32, kind="ExternalOutput").ap()
        src = getattr(k, which)
        r.Y.dma_start(out=k.d_dbg, in_=src[:shape[0], :shape[1]]).inc(st, 16)
        nst += 16
    r.Y.wait_val(st, nst)

    r.replay()
    return k



def ph_ln(k):
    r = k.r
    XB = k.ACT
    XSQ = lambda c: k.ACT[:, 16384 + c * 1024: 16384 + (c + 1) * 1024]
    sv = r.sem("ln_cast")
    ss = r.sem("ln_sq")
    for c in range(NC):
        xc = k.X[:, c * 1024:(c + 1) * 1024]
        r.V.tensor_copy(XB[:, c * 1024:(c + 1) * 1024], xc).inc(sv, 1)
        r.S.activation(XSQ(c), xc, AF.Square).inc(ss, 1)
    r.T.wait(k.s_pac)
    r.T.wait(k.s_pbc)
    sv0 = sv.v - NC
    ss0 = ss.v - NC
    oc = k.ONESCB[:, 0:1]
    for c in range(NC):
        r.T.wait_val(sv, sv0 + c + 1)
        r.T.wait_val(ss, ss0 + c + 1)
        for n in range(2):
            st = (c == 0)
            sp = (c == NC - 1)
            r.T.matmul(k.PA[0:1, n * 512:(n + 1) * 512], oc,
                       XB[:, c * 1024 + n * 512: c * 1024 + (n + 1) * 512],
                       start=st, stop=sp)
            i2 = r.T.matmul(k.PB[0:1, n * 512:(n + 1) * 512], oc,
                            XSQ(c)[:, n * 512:(n + 1) * 512], start=st, stop=sp)
    i2.inc(k.s_pap, 1).inc(k.s_pbp, 1)
    r.V.wait(k.s_pap)
    r.V.tensor_scalar(k.MROW, k.PA[0:1, :], 1.0 / D, None, ALU.mult).inc(k.s_pac, 1).inc(k.vch, 1)
    r.V.wait(k.vch)
    r.V.tensor_tensor(k.ROWSF, k.MROW, k.MROW, ALU.mult).inc(k.vch, 1)
    r.V.wait(k.s_pbp)
    r.V.wait(k.vch)
    r.V.scalar_tensor_tensor(k.RROW, k.PB[0:1, :], 1.0 / D, k.ROWSF,
                             ALU.mult, ALU.subtract).inc(k.s_pbc, 1).inc(k.vch, 1)
    sr = r.sem("ln_rstd")
    r.V.wait(k.vch)
    r.V.tensor_scalar(k.ROWSF, k.RROW, EPS_LN, None, ALU.add).inc(k.vch, 1)
    r.S.wait(k.vch)
    r.S.activation(k.RROW, k.ROWSF, AF.Ln).inc(k.sch, 1)
    r.S.wait(k.sch)
    r.S.activation(k.ROWSF, k.RROW, AF.Exp, scale=-0.5).inc(sr, 1)
    r.T.wait(k.s_pcc)
    r.T.wait(k.s_pdc)
    r.T.wait(sr)
    for n in range(2):
        r.T.matmul(k.PC[:, n * 512:(n + 1) * 512], k.ONES_F,
                   k.MROW[:, n * 512:(n + 1) * 512], start=True, stop=True
                   ).inc(k.s_pcp, 1)
        r.T.matmul(k.PD[:, n * 512:(n + 1) * 512], k.ONES_F,
                   k.ROWSF[:, n * 512:(n + 1) * 512], start=True, stop=True
                   ).inc(k.s_pdp, 1)
    MB = k.SMF[:, 0:1024]
    RB = k.SMF[:, 1024:2048]
    r.V.wait(k.s_pcp)
    r.V.tensor_copy(MB, k.PC).inc(k.s_pcc, 1).inc(k.vch, 1)
    r.V.wait(k.s_pdp)
    r.V.tensor_copy(RB, k.PD).inc(k.s_pdc, 1).inc(k.vch, 1)
    r.V.wait(k.vch)
    sdone = r.sem("ln_done")
    for c in range(NC):
        sl = slice(c * 1024, (c + 1) * 1024)
        tmp = k.ACT32[:, sl]
        r.V.tensor_tensor(tmp, k.X[:, sl], MB, ALU.subtract).inc(k.vch, 1)
        r.V.wait(k.vch)
        r.V.tensor_tensor(k.LNX[:, sl], tmp, RB, ALU.mult).inc(sdone, 1)



MOE_GROUPS = ([0, 1, 2, 3], [4, 5, 6, 7], [8, 9, 10, 11], [12, 13, 14, 15], [16, 17])
BIGM = 1.0e4


def ph_moe(k):
    r = k.r
    GWT = k.SMB[:, 2048:2304]
    SELF32 = _alias(k.nc, k.SMBh, f"SELF32_{r.bar.v}_{k.ws.i}", [16, 128], F32, 6656)[:, :]
    TMPB = k.SMB[:, 2048:3072]
    EB = k.SMT[:, 118:134]
    EBROW = k.SMT[0:1, 272:288]
    s_gwt = r.sem("moe_gwt")
    s_ebr = r.sem("moe_ebr")
    s_rt = r.sem("moe_rt")
    s_selfr = r.sem("moe_selfr")
    s_self = r.sem("moe_self")
    s_bcw = r.sem("moe_bcw")
    s_act = r.sem("moe_act")
    s_scfree = r.sem("moe_scfree")
    s_silfree = r.sem("moe_silfree")
    s_actfree = r.sem("moe_actfree")

    r.G.wait_val(r.bar, r.bar.v)
    r.G.dma_start(out=GWT.rearrange("p (c e) -> p c e", c=NC),
                  in_=k.d_gwt.rearrange("c p e -> p c e"),
                  allow_slow_non_contiguous=True).inc(s_gwt, 16)
    r.Y.wait_val(r.bar, r.bar.v)
    r.Y.dma_start(out=EBROW, in_=k.d_ebias).inc(s_ebr, 16)

    r.T.wait(s_gwt)
    r.T.wait(s_ebr)
    r.T.wait(k.s_pcc)
    r.T.matmul(k.PC[:, 0:16], k.ONES_F, EBROW, start=True, stop=True).inc(k.s_pcp, 1)
    r.V.wait(k.s_pcp)
    r.V.tensor_copy(EB, k.PC[:, 0:16]).inc(k.s_pcc, 1).inc(s_rt, 1)

    SC = k.SMT[:, 0:16]
    S4 = k.SMT[:, 16:32]
    PAIR = k.SMT[:, 32:56]
    GS = k.SMT[:, 56:60]
    MIN6 = k.SMT[:, 60:66]
    TH = k.SMT[:, 66:67]
    M0 = k.SMT[:, 68:84]
    MA = k.SMT[:, 84:100]
    MBv = k.SMT[:, 100:116]
    WR = k.SMT[:, 116:117]
    WRT = k.SMT[:, 134:135]
    SU = k.SMT[:, 135:136]
    pairs = [(0, 1), (0, 2), (0, 3), (1, 2), (1, 3), (2, 3)]
    for tt in range(8):
        P = (k.PA if tt % 2 == 0 else k.PB)[:, 0:16]
        sp = k.s_pap if tt % 2 == 0 else k.s_pbp
        sc_ = k.s_pac if tt % 2 == 0 else k.s_pbc
        r.T.wait(sc_)
        for c in range(NC):
            r.T.matmul(P, k.LNX[:, c * 1024 + tt * 128: c * 1024 + (tt + 1) * 128],
                       GWT[:, c * 16:(c + 1) * 16],
                       start=(c == 0), stop=(c == NC - 1))
        r.rlast("tensor").inc(sp, 1)
        r.S.wait(sp)
        r.S.wait(s_scfree)
        r.S.activation(SC, P, AF.Sigmoid).inc(sc_, 1).inc(s_rt, 1)
        r.V.wait(s_rt)
        r.V.tensor_tensor(S4, SC, EB, ALU.add).inc(k.vch, 1)
        r.V.wait(k.vch)
        for pi, (i, j) in enumerate(pairs):
            r.V.tensor_tensor(PAIR[:, pi * 4:(pi + 1) * 4],
                              S4[:, i:16:4], S4[:, j:16:4], ALU.add).inc(k.vch, 1)
        r.V.wait(k.vch)
        r.V.tensor_tensor(GS, PAIR[:, 0:4], PAIR[:, 4:8], ALU.max).inc(k.vch, 1)
        for pi in range(2, 6):
            r.V.wait(k.vch)
            r.V.tensor_tensor(GS, GS, PAIR[:, pi * 4:(pi + 1) * 4], ALU.max).inc(k.vch, 1)
        gpairs = [(0, 1), (0, 2), (0, 3), (1, 2), (1, 3), (2, 3)]
        r.V.wait(k.vch)
        for pi, (i, j) in enumerate(gpairs):
            r.V.tensor_tensor(MIN6[:, pi:pi + 1], GS[:, i:i + 1], GS[:, j:j + 1],
                              ALU.min).inc(k.vch, 1)
        r.V.wait(k.vch)
        r.V.reduce_max(TH, MIN6, axis=AX.X).inc(k.vch, 1)
        r.V.wait(k.vch)
        GM = MIN6[:, 0:4]
        r.V.tensor_scalar(GM, GS, TH, None, ALU.is_ge).inc(k.vch, 1)
        r.V.wait(k.vch)
        for i in range(4):
            r.V.tensor_copy(M0[:, i:16:4], GM).inc(k.vch, 1)
        r.V.wait(k.vch)
        r.V.tensor_scalar(MA, S4, BIGM, None, ALU.add).inc(k.vch, 1)
        r.V.wait(k.vch)
        r.V.tensor_tensor(MA, MA, M0, ALU.mult).inc(k.vch, 1)
        r.V.wait(k.vch)
        r.V.tensor_scalar(MA, MA, BIGM, None, ALU.subtract).inc(k.vch, 1)
        r.V.wait(k.vch)
        r.V.tensor_copy(MBv, MA).inc(k.vch, 1)
        for rnd in range(3):
            r.V.wait(k.vch)
            r.V.reduce_max(TH, MA, axis=AX.X).inc(k.vch, 1)
            r.V.wait(k.vch)
            E = PAIR[:, 0:16]
            r.V.tensor_scalar(E, MA, TH, None, ALU.is_equal).inc(k.vch, 1)
            r.V.wait(k.vch)
            r.V.scalar_tensor_tensor(MA, E, -2.0 * BIGM, MA, ALU.mult, ALU.add
                                     ).inc(k.vch, 1)
        r.V.wait(k.vch)
        r.V.reduce_max(TH, MA, axis=AX.X).inc(k.vch, 1)
        r.V.wait(k.vch)
        SEL = PAIR[:, 0:16]
        r.V.tensor_scalar(SEL, MBv, TH, None, ALU.is_ge).inc(k.vch, 1)
        r.V.wait(k.vch)
        WSEL = M0
        r.V.tensor_tensor(WSEL, SEL, SC, ALU.mult).inc(k.vch, 1).inc(s_scfree, 1)
        r.V.wait(k.vch)
        r.V.reduce_sum(SU, WSEL, axis=AX.X).inc(k.vch, 1)
        r.V.wait(k.vch)
        r.V.tensor_scalar(SU, SU, 1e-20, None, ALU.add).inc(k.vch, 1)
        r.V.wait(k.vch)
        r.V.reciprocal(WRT, SU).inc(k.vch, 1)
        r.V.wait(k.vch)
        r.V.tensor_scalar(MA, WSEL, WRT, 1.25, ALU.mult, ALU.mult).inc(k.vch, 1)
        r.V.wait(k.vch)
        r.T.wait(k.s_pcc)
        r.T.wait(k.vch)
        r.T.transpose(k.PC[0:16, 0:128], MA, k.IDENT).inc(k.s_pcp, 1)
        r.V.wait(k.s_pcp)
        r.V.tensor_copy(k.WEM[:, tt * 128:(tt + 1) * 128], k.PC[0:16, 0:128]
                        ).inc(k.s_pcc, 1).inc(s_rt, 1)

    for g, grp in enumerate(MOE_GROUPS):
        for ei, e in enumerate(grp):
            shared = e >= NE
            bslot = k.SMB[:, (ei % 2) * 1024:(ei % 2) * 1024 + 1024]
            if not shared:
                r.Y.wait(s_selfr)
                r.Y.dma_start(out=SELF32, in_=k.d_sel[e]).inc(s_self, 16)
                r.T.wait(s_self)
                r.T.wait(k.s_pcc)
                r.T.wait(s_rt)
                for n in range(2):
                    i = r.T.matmul(k.PC[:, n * 512:(n + 1) * 512], SELF32,
                                   k.WEM[:, n * 512:(n + 1) * 512],
                                   start=True, stop=True)
                i.inc(k.s_pcp, 1).inc(s_selfr, 1)
                r.V.wait(k.s_pcp)
                r.V.wait(s_act)
                r.V.tensor_copy(bslot, k.PC).inc(k.s_pcc, 1).inc(s_bcw, 1)
            else:
                r.V.wait(s_act)
                r.V.memset(bslot, 0.5).inc(s_bcw, 1)
            for fc in range(8):
                bu = k.ws.load(k.d_gu[e * 8 + fc], 4096)
                PG, sgp, sgc = (k.PA, k.s_pap, k.s_pac) if fc % 2 == 0 else \
                               (k.PC, k.s_pcp, k.s_pcc)
                PU, sup, suc = (k.PB, k.s_pbp, k.s_pbc) if fc % 2 == 0 else \
                               (k.PD, k.s_pdp, k.s_pdc)
                bu.acquire(r.T)
                r.T.wait(sgc)
                for c in range(NC):
                    for n in range(2):
                        r.T.matmul(PG[:, n * 512:(n + 1) * 512],
                                   bu.ap[:, c * 128:(c + 1) * 128],
                                   k.LNX[:, c * 1024 + n * 512: c * 1024 + n * 512 + 512],
                                   start=(c == 0), stop=(c == NC - 1))
                r.rlast("tensor").inc(sgp, 1)
                r.T.wait(suc)
                for c in range(NC):
                    for n in range(2):
                        r.T.matmul(PU[:, n * 512:(n + 1) * 512],
                                   bu.ap[:, 2048 + c * 128: 2048 + (c + 1) * 128],
                                   k.LNX[:, c * 1024 + n * 512: c * 1024 + n * 512 + 512],
                                   start=(c == 0), stop=(c == NC - 1))
                i = r.rlast("tensor").inc(sup, 1)
                bu.release(i)
                r.S.wait(sgp)
                r.S.wait(s_silfree)
                r.S.activation(k.SIL, PG, AF.Silu).inc(sgc, 1).inc(k.sch, 1)
                r.V.wait(k.sch)
                r.V.wait(sup)
                r.V.wait(s_act)
                r.V.tensor_tensor(TMPB, k.SIL, PU, ALU.mult
                                  ).inc(suc, 1).inc(k.vch, 1).inc(s_silfree, 1)
                r.V.wait(k.vch)
                r.V.wait(s_bcw)
                aslot = k.ACT[:, ei * 8192 + fc * 1024: ei * 8192 + (fc + 1) * 1024]
                r.V.tensor_tensor(aslot, TMPB, bslot, ALU.mult).inc(s_act, 1)
        nacts = len(grp) * 8
        for hc in range(NC):
            if len(grp) == 4:
                bu = k.ws.load(k.d_down[g * NC + hc], 4096)
            else:
                bu = k.ws.load(k.d_downsh[hc], 2048)
            PDn, sdp, sdc = (k.PA, k.s_pap, k.s_pac) if hc % 2 == 0 else \
                            (k.PC, k.s_pcp, k.s_pcc)
            bu.acquire(r.T)
            r.T.wait(sdc)
            if hc == 0:
                r.T.wait(s_act)
            nmm = 0
            for ei, e in enumerate(grp):
                for fc in range(8):
                    for n in range(2):
                        r.T.matmul(PDn[:, n * 512:(n + 1) * 512],
                                   bu.ap[:, (ei * 8 + fc) * 128:(ei * 8 + fc + 1) * 128],
                                   k.ACT[:, ei * 8192 + fc * 1024 + n * 512:
                                         ei * 8192 + fc * 1024 + n * 512 + 512],
                                   start=(ei == 0 and fc == 0),
                                   stop=(ei == len(grp) - 1 and fc == 7))
            i = r.rlast("tensor").inc(sdp, 1)
            bu.release(i)
            if hc == NC - 1:
                i.inc(s_actfree, 1)
            r.V.wait(sdp)
            xsl = k.X[:, hc * 1024:(hc + 1) * 1024]
            r.V.tensor_tensor(xsl, xsl, PDn, ALU.add).inc(sdc, 1).inc(k.vch, 1)
            r.V.wait(k.vch)



def ph_attn0(k):
    r = k.r
    s_br = r.sem("a0_brow")
    s_proj = r.sem("a0_proj")
    s_kstage = r.sem("a0_kstage")
    s_vstage = r.sem("a0_vstage")
    s_lmax = r.sem("a0_lmax")
    s_lmd = r.sem("a0_lmd")
    s_ksq = r.sem("a0_ksq")
    s_ccK = r.sem("a0_ccK")
    s_ccV = r.sem("a0_ccV")
    s_maxk = r.sem("a0_maxk")
    s_exp = r.sem("a0_exp")
    s_pvd = r.sem("a0_pvd")
    s_po = r.sem("a0_po")
    s_zr = r.sem("a0_zr")
    s_zb = r.sem("a0_zb")
    s_evac = r.sem("a0_evac")
    s_vstg = r.sem("a0_vstg")
    KTMP = k.SMB[:, 2048:3072]
    KSQ = k.SMB[:, 3072:4096]
    VTMPF = k.ACT32[:, 8192:9216]
    VTT = [k.SMB[:, 1024:1152], k.SMB[:, 1152:1280]]
    ZB = k.SMB[:, 2048:2560]

    r.G.wait_val(r.bar, r.bar.v)
    r.G.dma_start(out=k.BROW, in_=k.d_l0_bq).inc(s_br, 16)
    r.G.dma_start(out=k.BROW2, in_=k.d_l0_bk).inc(s_br, 16)

    def proj_fm(dunits, brow, h, ping, out_cb):
        bu = k.ws.load(dunits[h], 2048)
        P, sp, sc = (k.PA, k.s_pap, k.s_pac) if ping else (k.PB, k.s_pbp, k.s_pbc)
        bu.acquire(r.T)
        r.T.wait(sc)
        r.T.wait(s_br)
        for c in range(NC):
            for n in range(2):
                r.T.matmul(P[:, n * 512:(n + 1) * 512],
                           bu.ap[:, c * 128:(c + 1) * 128],
                           k.LNX[:, c * 1024 + n * 512: c * 1024 + n * 512 + 512],
                           start=(c == 0), stop=False)
        for n in range(2):
            r.T.matmul(P[:, n * 512:(n + 1) * 512],
                       brow[0:1, h * 128:(h + 1) * 128],
                       k.ONES_BF, start=False, stop=True)
        i = r.rlast("tensor").inc(sp, 1).inc(s_proj, 1)
        bu.release(i)
        out_cb(P, sp, sc)

    for h in range(NH):
        def evq(P, sp, sc, h=h):
            r.V.wait(sp)
            r.V.tensor_copy(k.ACT[:, h * 1024:(h + 1) * 1024], P).inc(sc, 1)
        proj_fm(k.d_l0_wq, k.BROW, h, h % 2 == 0, evq)

    for h in range(NH):
        def evk(P, sp, sc, h=h):
            r.V.wait(sp)
            r.V.wait(s_kstage)
            r.V.wait(s_ksq)
            r.V.tensor_copy(KTMP, P).inc(sc, 1).inc(k.vch, 1)
            r.S.wait(k.vch)
            r.S.wait(s_ksq)
            r.S.activation(KSQ, KTMP, AF.Square).inc(s_ksq, 1)
            r.Y.wait(k.vch)
            r.Y.dma_start(out=k.d_l0k_in.ap()[h * 128:(h + 1) * 128, :], in_=KTMP
                          ).inc(s_kstage, 16)
            r.T.wait(k.s_pcc)
            r.T.wait(s_ksq)
            for n in range(2):
                r.T.matmul(k.PC[0:1, n * 512:(n + 1) * 512], k.ONESCB[:, 0:1],
                           KSQ[:, n * 512:(n + 1) * 512], start=True, stop=True)
            r.rlast("tensor").inc(k.s_pcp, 1).inc(s_ksq, 1)
            r.V.wait(k.s_pcp)
            r.V.reduce_max(k.SMT[0:1, 224 + h:225 + h], k.PC[0:1, 0:1024],
                           axis=AX.X).inc(k.s_pcc, 1).inc(s_lmax, 1)
        proj_fm(k.d_l0_wk, k.BROW2, h, h % 2 == 0, evk)

    r.V.wait(s_lmax)
    r.V.memset(k.ROWSF, 0.0).inc(k.vch, 1)
    r.V.wait(k.vch)
    r.V.tensor_copy(k.ROWSF[0:1, 0:16], k.SMT[0:1, 224:240]).inc(s_lmax, 1)
    r.G.wait_val(s_lmax, s_lmax.v)
    r.G.dma_start(out=k.d_l0k_in.ap()[2048:2049, :], in_=k.ROWSF).inc(s_lmd, 16)

    r.G.wait_val(s_proj, s_proj.v)
    r.G.dma_start(out=k.BROW, in_=k.d_l0_bv).inc(s_br, 16)

    ptile = [r.sem("a0_pt0"), r.sem("a0_pt1")]
    ptilec = [r.sem("a0_pt0c"), r.sem("a0_pt1c")]
    vstg = [r.sem("a0_vs0"), r.sem("a0_vs1")]
    for dv in range(NH):
        def evv(P, sp, sc, dv=dv):
            r.V.wait(sp)
            r.V.wait(s_vstg)
            r.V.tensor_copy(VTMPF, P).inc(sc, 1).inc(k.vch, 1)
            r.T.wait(k.vch)
            for tt in range(8):
                sl = tt % 2
                r.T.wait(ptilec[sl])
                i = r.T.transpose(k.PD[:, sl * 512:sl * 512 + 128],
                                  VTMPF[:, tt * 128:(tt + 1) * 128], k.IDENT
                                  ).inc(ptile[sl], 1)
                if tt >= 6:
                    i.inc(s_vstg, 1)
                r.V.wait(ptile[sl])
                r.V.wait(vstg[sl])
                r.V.tensor_copy(VTT[sl], k.PD[:, sl * 512:sl * 512 + 128]
                                ).inc(ptilec[sl], 1).inc(k.vch, 1)
                r.Y.wait(k.vch)
                r.Y.dma_start(out=k.d_l0v_in.ap()[tt * 128:(tt + 1) * 128,
                                                  dv * 128:(dv + 1) * 128],
                              in_=VTT[sl]).inc(vstg[sl], 16)
        proj_fm(k.d_l0_wv, k.BROW, dv, dv % 2 == 0, evv)

    r.G.wait_val(s_kstage, s_kstage.v)
    r.G.wait(s_lmd)
    r.G.collective_compute("AllGather", ALU.bypass,
                           replica_groups=[[0, 1, 2, 3], [4, 5, 6, 7]],
                           ins=[k.d_l0k_in.ap()], outs=[k.d_l0k_out.ap()]
                           ).inc(s_ccK, 1)
    r.G.wait(vstg[0])
    r.G.wait(vstg[1])
    r.G.collective_compute("AllGather", ALU.bypass,
                           replica_groups=[[0, 1, 2, 3], [4, 5, 6, 7]],
                           ins=[k.d_l0v_in.ap()], outs=[k.d_l0v_out.ap()]
                           ).inc(s_ccV, 1)

    r.Y.wait(s_ccK)
    for rr in range(4):
        r.Y.dma_start(out=k.MAXB[0:1, rr * 16:(rr + 1) * 16],
                      in_=k.d_l0k_out.ap()[rr * 2049 + 2048: rr * 2049 + 2049, 0:16]
                      ).inc(s_maxk, 16)
    r.V.wait(s_maxk)
    r.V.wait(s_lmd)
    GAT = k.ROWSF[0:1, 0:64]
    r.V.tensor_copy(GAT, k.MAXB).inc(k.vch, 1)
    mk1, mk2 = k.ROWSF[0:1, 64:80], k.ROWSF[0:1, 80:96]
    r.V.wait(k.vch)
    r.V.tensor_tensor(mk1, k.ROWSF[0:1, 0:16], k.ROWSF[0:1, 16:32], ALU.max
                      ).inc(k.vch, 1)
    r.V.wait(k.vch)
    r.V.tensor_tensor(mk2, mk1, k.ROWSF[0:1, 32:48], ALU.max).inc(k.vch, 1)
    r.V.wait(k.vch)
    r.V.tensor_tensor(mk1, mk2, k.ROWSF[0:1, 48:64], ALU.max).inc(k.vch, 1)

    QSQ = k.SMB[:, 0:1024]
    for h in range(NH):
        r.S.wait(s_vstg)
        r.S.wait(k.s_pcc)
        r.S.activation(QSQ, k.ACT[:, h * 1024:(h + 1) * 1024], AF.Square
                       ).inc(k.sch, 1)
        r.T.wait(k.sch)
        r.T.wait(k.s_pcc)
        for n in range(2):
            r.T.matmul(k.PC[0:1, n * 512:(n + 1) * 512], k.ONESCB[:, 0:1],
                       QSQ[:, n * 512:(n + 1) * 512], start=True, stop=True)
        r.rlast("tensor").inc(k.s_pcp, 1)
        r.V.wait(k.s_pcp)
        r.V.wait(s_lmd)
        r.V.reduce_max(k.ROWSF[0:1, 160 + h:161 + h], k.PC[0:1, 0:1024], axis=AX.X
                       ).inc(k.s_pcc, 1).inc(k.vch, 1)
    r.S.wait(k.vch)
    r.S.activation(k.ROWSF[0:1, 96:112], k.ROWSF[0:1, 160:176], AF.Sqrt).inc(k.sch, 1)
    r.S.activation(k.ROWSF[0:1, 112:128], mk1, AF.Sqrt).inc(k.sch, 1)
    r.V.wait(k.sch)
    r.V.tensor_tensor(k.ROWSF[0:1, 128:144], k.ROWSF[0:1, 96:112],
                      k.ROWSF[0:1, 112:128], ALU.mult).inc(k.vch, 1)
    r.V.wait(k.vch)
    r.V.tensor_scalar(k.ROWSF[0:1, 144:160], k.ROWSF[0:1, 128:144],
                      -INV_SQRT_HD, None, ALU.mult).inc(k.vch, 1)
    NEGMB = k.SMT[:, 102:118]
    r.T.wait(k.vch)
    r.T.wait(k.s_pcc)
    r.T.matmul(k.PC[:, 0:16], k.ONES_F, k.ROWSF[0:1, 144:160], start=True, stop=True
               ).inc(k.s_pcp, 1)
    r.V.wait(k.s_pcp)
    r.V.tensor_copy(NEGMB, k.PC[:, 0:16]).inc(k.s_pcc, 1).inc(k.vch, 1)

    r.Y.wait(s_ccK)
    r.Y.wait(s_ccV)
    PBUF = [k.SMB[:, 0:1024], k.SMB[:, 1024:2048]]
    pv_done = [None, None]
    nm_done_v = k.vch.v
    for h in range(NH):
        ksrc = []
        for rr in range(4):
            ksrc.append((None, k.d_l0k_out.ap()[rr * 2049 + h * 128:
                                                rr * 2049 + (h + 1) * 128, :]))
        first, span = k.ws.ring[k.ws.i % 3]
        buK = None
        sl0 = k.ws.slots[first]
        for s_ in k.ws.slots[first:first + span]:
            if s_.release_at is not None:
                r.Y.wait_val(s_.release_at[0], s_.release_at[1])
        ldy = sl0.loaded["sync"]
        for rr in range(4):
            r.Y.dma_start(out=k.WBUF[:, first * 2048 + rr * 1024:
                                     first * 2048 + (rr + 1) * 1024],
                          in_=ksrc[rr][1]).inc(ldy, 16)
        buK = BufUse(k.WBUF[:, first * 2048:first * 2048 + 4096], ldy,
                     ldy.v, sl0.freed, k.ws.slots[first:first + span], k.ws)
        k.ws.i += 1
        buV = k.ws.load(
            k.d_l0v_out.ap()[:, h * 128:(h + 1) * 128].rearrange(
                "(kt p) j -> p kt j", p=128), 4096, queue="sync")
        for qb in range(2):
            Q = k.ACT[:, h * 1024 + qb * 512: h * 1024 + qb * 512 + 512]
            first_pv = True
            for kg in range(16):
                sl = kg % 2
                Ps, sps, spc = (k.PA, k.s_pap, k.s_pac) if sl == 0 else \
                               (k.PB, k.s_pbp, k.s_pbc)
                if kg == 0 and qb == 0:
                    buK.acquire(r.T)
                    buV.acquire(r.T)
                r.T.wait(spc)
                for kt2 in range(2):
                    kt = kg * 2 + kt2
                    r.T.matmul(Ps[:, kt2 * 512:(kt2 + 1) * 512],
                               buK.ap[:, kt * 128:(kt + 1) * 128], Q,
                               start=True, stop=True)
                r.rlast("tensor").inc(sps, 1)
                r.S.wait(sps)
                if h == 0 and qb == 0 and kg == 0:
                    r.S.wait_val(k.vch, nm_done_v)
                if pv_done[sl] is not None:
                    r.S.wait_val(s_pvd, pv_done[sl])
                r.S.activation(PBUF[sl], Ps, AF.Exp, bias=NEGMB[:, h:h + 1],
                               scale=INV_SQRT_HD).inc(spc, 1).inc(s_exp, 1)
                r.T.wait(s_exp)
                if first_pv:
                    r.T.wait(k.s_pcc)
                    first_pv = False
                for kt2 in range(2):
                    kt = kg * 2 + kt2
                    r.T.matmul(k.PC[:, 0:512], buV.ap[:, kt * 128:(kt + 1) * 128],
                               PBUF[sl][:, kt2 * 512:(kt2 + 1) * 512],
                               start=(kg == 0 and kt2 == 0),
                               stop=(kg == 15 and kt2 == 1))
                    r.T.matmul(k.PC[0:1, 512:1024], k.ONESCB[:, 0:1],
                               PBUF[sl][:, kt2 * 512:(kt2 + 1) * 512],
                               start=(kg == 0 and kt2 == 0),
                               stop=(kg == 15 and kt2 == 1))
                r.rlast("tensor").inc(s_pvd, 1)
                pv_done[sl] = s_pvd.v
            r.rlast("tensor").inc(k.s_pcp, 1)
            if qb == 1:
                i = r.rlast("tensor")
                buK.release(i)
                buV.release(i)
            r.V.wait(k.s_pcp)
            r.V.reciprocal(k.ZROW[0:1, 0:512], k.PC[0:1, 512:1024]).inc(s_zr, 1)
            r.T.wait(s_zr)
            r.T.matmul(k.PC[:, 512:1024], k.ONES_F, k.ZROW[0:1, 0:512],
                       start=True, stop=True).inc(s_zb, 1)
            r.V.wait(s_zb)
            r.V.tensor_copy(ZB, k.PC[:, 512:1024]).inc(k.vch, 1)
            r.V.wait(k.vch)
            r.V.tensor_tensor(k.LNX[:, h * 1024 + qb * 512:
                                    h * 1024 + qb * 512 + 512],
                              k.PC[:, 0:512], ZB, ALU.mult
                              ).inc(k.s_pcc, 1).inc(s_evac, 1)

    r.G.wait_val(s_proj, s_proj.v)
    r.G.dma_start(out=k.BROW, in_=k.d_l0_bo).inc(s_br, 16)
    for m in range(NH):
        def evo(P, sp, sc, m=m):
            r.V.wait(sp)
            xsl = k.X[:, m * 1024:(m + 1) * 1024]
            r.V.tensor_tensor(xsl, xsl, P, ALU.add).inc(sc, 1)
        if m == 0:
            r.T.wait(s_evac)
        proj_wo(k, k.d_l0_wo, k.BROW, m, m % 2 == 0, evo, s_br, s_proj)


def proj_wo(k, dunits, brow, h, ping, out_cb, s_br, s_proj):
    r = k.r
    bu = k.ws.load(dunits[h], 2048)
    P, sp, sc = (k.PA, k.s_pap, k.s_pac) if ping else (k.PB, k.s_pbp, k.s_pbc)
    bu.acquire(r.T)
    r.T.wait(sc)
    r.T.wait(s_br)
    for c in range(NC):
        for n in range(2):
            r.T.matmul(P[:, n * 512:(n + 1) * 512],
                       bu.ap[:, c * 128:(c + 1) * 128],
                       k.LNX[:, c * 1024 + n * 512: c * 1024 + n * 512 + 512],
                       start=(c == 0), stop=False)
    for n in range(2):
        r.T.matmul(P[:, n * 512:(n + 1) * 512],
                   brow[0:1, h * 128:(h + 1) * 128],
                   k.ONES_BF, start=False, stop=True)
    i = k.r.rlast("tensor").inc(sp, 1).inc(s_proj, 1)
    bu.release(i)
    out_cb(P, sp, sc)



def ph_prep1(k):
    r = k.r
    if getattr(k, "_prep_done", False):
        return
    k._prep_done = True
    s_q = r.sem("p1_q")
    s_sgfree = r.sem("p1_sgfree")
    A0 = k.SMT[:, 224:225]
    TMPC = k.SMT[:, 227:228]
    ONEC = k.SMT[:, 226:227]
    THR = k.SMT[:, 229:230]
    AB = k.SMF
    SG = k.SMB[:, 0:2048]
    MK = k.SMB[:, 2048:4096]
    WQ = [k.ACT[:, 0:2048], k.ACT[:, 2048:4096]]
    s_wq = [r.sem("p1_wq0"), r.sem("p1_wq1")]

    def units():
        for u in range(DFF // 128):
            yield k.d_w1[u], ("w1", u)
        for m in range(NC):
            for q in range(4):
                yield k.d_w2[m][:, q * 2048:(q + 1) * 2048], ("w2", m, q)

    A1 = k.SMT[:, 225:226]
    THR2 = k.SMT[:, 230:231]
    r.G.wait_val(r.bar, r.bar.v)
    s_onec = r.sem("p1_onec")
    r.Y.wait_val(r.bar, r.bar.v)
    r.Y.dma_start(out=ONEC, in_=k.d_onescol).inc(s_onec, 16)
    r.V.memset(A0, 0.0).inc(k.vch, 1)
    r.V.memset(A1, 0.0).inc(k.vch, 1)
    for i, (src, tag) in enumerate(units()):
        bu = k.ws32.load(src, 2048, queue="gpsimd" if i % 2 == 0 else "sync")
        bu.acquire(r.V)
        r.V.tensor_reduce(TMPC, bu.ap, AX.X, ALU.add, apply_absolute_value=True
                          ).inc(k.vch, 1)
        bu.release(r.rlast("vector"))
        r.V.wait(k.vch)
        acc = A0 if tag[0] == "w1" else A1
        r.V.tensor_tensor(acc, acc, TMPC, ALU.add).inc(k.vch, 1)
        r.V.wait(k.vch)
    for acc, thr in ((A0, THR), (A1, THR2)):
        r.T.wait(k.vch)
        r.T.wait(s_onec)
        r.T.wait(k.s_pcc)
        r.T.matmul(k.PC[0:1, 0:1], acc, ONEC, start=True, stop=True).inc(k.s_pcp, 1)
        r.V.wait(k.s_pcp)
        r.V.tensor_scalar(k.SMT[0:1, 228:229], k.PC[0:1, 0:1], 0.7 / float(DFF * D),
                          None, ALU.mult).inc(k.s_pcc, 1).inc(k.vch, 1)
        r.T.wait(k.vch)
        r.T.wait(k.s_pcc)
        r.T.matmul(k.PC[:, 0:1], k.ONES_F, k.SMT[0:1, 228:229], start=True, stop=True
                   ).inc(k.s_pcp, 1)
        r.V.wait(k.s_pcp)
        r.V.tensor_copy(thr, k.PC[:, 0:1]).inc(k.s_pcc, 1).inc(k.vch, 1)

    NTHR = k.SMT[:, 231:232]
    NTHR2 = k.SMT[:, 232:233]
    r.V.wait(k.vch)
    r.V.tensor_scalar(NTHR, THR, -1.0, None, ALU.mult).inc(k.vch, 1)
    r.V.tensor_scalar(NTHR2, THR2, -1.0, None, ALU.mult).inc(k.vch, 1)
    r.V.wait(k.vch)
    for i, (src, tag) in enumerate(units()):
        bu = k.ws32.load(src, 2048, queue="gpsimd")
        sl = i % 2
        w1 = tag[0] == "w1"
        sc2 = 1.0 if w1 else 0.5
        bu.acquire(r.V)
        r.V.tensor_scalar(SG, bu.ap, THR if w1 else THR2, sc2,
                          ALU.is_ge, ALU.mult).inc(k.vch, 1)
        i2 = r.V.tensor_scalar(MK, bu.ap, NTHR if w1 else NTHR2, sc2,
                               ALU.is_le, ALU.mult)
        i2.inc(k.vch, 1)
        bu.release(i2)
        r.V.wait(k.vch)
        r.V.wait(s_wq[sl])
        r.V.tensor_tensor(WQ[sl], SG, MK, ALU.subtract).inc(k.vch, 1)
        r.Y.wait(k.vch)
        if w1:
            dst = k.d_w1q.ap()[tag[1]]
        else:
            dst = k.d_w2q.ap()[tag[1]][:, tag[2] * 2048:(tag[2] + 1) * 2048]
        r.Y.dma_start(out=dst, in_=WQ[sl]).inc(s_wq[sl], 16)
    r.V.wait(s_wq[0])
    r.V.wait(s_wq[1])



def ph_ffn1(k):
    r = k.r
    if not getattr(k, "_b2h_done", False):
        k._b2h_done = True
        r.V.wait_val(r.bar, r.bar.v)
        r.V.tensor_scalar(k.SMT[:, 320:336], k.SMT[:, 200:216], 0.5, None,
                          ALU.mult).inc(k.vch, 1)
        k._b2h_v = k.vch.v
    B2H = k.SMT[:, 320:336]
    s_h = r.sem("f1_h")
    for tt2 in range(2):
        toff = tt2 * 512
        for fc in range(DFF // 128):
            bu = k.ws.load(k.d_w1q.ap()[fc], 2048, queue="sync")
            P, sp, sc = (k.PA, k.s_pap, k.s_pac) if fc % 2 == 0 else \
                         (k.PB, k.s_pbp, k.s_pbc)
            bu.acquire(r.T)
            r.T.wait(sc)
            if fc == 0:
                r.T.wait(s_h)
            for c in range(NC):
                r.T.matmul(P[:, 0:512], bu.ap[:, c * 128:(c + 1) * 128],
                           k.LNX[:, c * 1024 + toff: c * 1024 + toff + 512],
                           start=(c == 0), stop=(c == NC - 1))
            i = r.rlast("tensor").inc(sp, 1)
            bu.release(i)
            r.S.wait(sp)
            r.S.activation(k.ACT[:, fc * 512:(fc + 1) * 512], P[:, 0:512], AF.Gelu,
                           bias=k.SMT[:, 136 + fc:137 + fc]).inc(sc, 1).inc(s_h, 1)
        r.T.wait(s_h)
        for m in range(NC):
            P, sp, sc = (k.PC[:, 0:512], k.s_pcp, k.s_pcc) if m % 2 == 0 else \
                         (k.PD[:, 0:512], k.s_pdp, k.s_pdc)
            r.T.wait(sc)
            for q in range(4):
                bu = k.ws.load(k.d_w2q.ap()[m][:, q * 2048:(q + 1) * 2048], 2048,
                               queue="sync")
                bu.acquire(r.T)
                for c in range(NC):
                    r.T.matmul(P, bu.ap[:, c * 128:(c + 1) * 128],
                               k.ACT[:, (q * NC + c) * 512:(q * NC + c + 1) * 512],
                               start=(q == 0 and c == 0), stop=(q == 3 and c == NC - 1))
                bu.release(r.rlast("tensor"))
            r.rlast("tensor").inc(sp, 1)
            r.V.wait(sp)
            r.V.wait_val(k.vch, k._b2h_v)
            xsl = k.X[:, m * 1024 + toff: m * 1024 + toff + 512]
            r.V.scalar_tensor_tensor(xsl, P, B2H[:, m:m + 1], xsl,
                                     ALU.add, ALU.add).inc(sc, 1).inc(s_h, 1)



def ph_attn1(k):
    r = k.r
    s_br = r.sem("a1_brow")
    s_proj = r.sem("a1_proj")
    s_ar = r.sem("a1_ar")
    s_arb = r.sem("a1_arb")
    s_cc = r.sem("a1_cc")
    s_kv = r.sem("a1_kv")
    s_dec = r.sem("a1_dec")
    KF = k.ACT32[:, 0:1024]
    VF = k.ACT32[:, 1024:2048]
    KT = k.SMB[:, 2048:3072]
    VT = k.SMB[:, 3072:4096]
    KSUM = k.SMT[:, 0:16]

    r.G.wait_val(r.bar, r.bar.v)
    r.G.dma_start(out=k.BROW, in_=k.d_l1_bv).inc(s_br, 16)
    r.G.dma_start(out=k.BROW2, in_=k.d_l1_bk).inc(s_br, 16)
    r.Y.wait_val(r.bar, r.bar.v)
    r.Y.dma_start(out=k.ROWSF, in_=k.d_decay_row).inc(s_dec, 16)

    def proj(units, brow, h, P, sp, sc):
        bu = k.ws.load(units[h], 2048)
        bu.acquire(r.T)
        r.T.wait(sc)
        r.T.wait(s_br)
        for c in range(NC):
            for n in range(2):
                r.T.matmul(P[:, n * 512:(n + 1) * 512],
                           bu.ap[:, c * 128:(c + 1) * 128],
                           k.LNX[:, c * 1024 + n * 512: c * 1024 + n * 512 + 512],
                           start=(c == 0), stop=False)
        for n in range(2):
            r.T.matmul(P[:, n * 512:(n + 1) * 512],
                       brow[0:1, h * 128:(h + 1) * 128],
                       k.ONES_BF, start=False, stop=True)
        i = r.rlast("tensor").inc(sp, 1).inc(s_proj, 1)
        bu.release(i)

    ptile = [r.sem("a1_pt0"), r.sem("a1_pt1")]
    ptilec = [r.sem("a1_pt0c"), r.sem("a1_pt1c")]
    for h in range(NH):
        proj(k.d_l1_wk, k.BROW2, h, k.PA, k.s_pap, k.s_pac)
        r.S.wait(k.s_pap)
        r.S.wait(s_kv)
        r.S.activation(KF, k.PA, AF.Relu).inc(k.s_pac, 1).inc(k.sch, 1)
        proj(k.d_l1_wv, k.BROW, h, k.PB, k.s_pbp, k.s_pbc)
        r.S.wait(k.s_pbp)
        r.S.wait(s_kv)
        r.S.activation(VF, k.PB, AF.Copy).inc(k.s_pbc, 1).inc(k.sch, 1)
        r.T.wait(k.sch)
        for tt in range(8):
            sl = tt % 2
            r.T.wait(ptilec[sl])
            r.T.transpose(k.PD[:, sl * 512:sl * 512 + 128],
                          KF[:, tt * 128:(tt + 1) * 128], k.IDENT).inc(ptile[sl], 1)
            r.V.wait(ptile[sl])
            r.V.wait(s_dec)
            r.V.tensor_scalar(KT[:, tt * 128:(tt + 1) * 128],
                              k.PD[:, sl * 512:sl * 512 + 128],
                              k.SMT[:, 216 + tt:217 + tt], None, ALU.mult
                              ).inc(ptilec[sl], 1).inc(k.vch, 1)
        for tt in range(8):
            sl = tt % 2
            r.T.wait(ptilec[sl])
            i = r.T.transpose(k.PD[:, sl * 512:sl * 512 + 128],
                              VF[:, tt * 128:(tt + 1) * 128], k.IDENT
                              ).inc(ptile[sl], 1)
            if tt >= 6:
                i.inc(s_kv, 1)
            r.V.wait(ptile[sl])
            r.V.tensor_copy(VT[:, tt * 128:(tt + 1) * 128],
                            k.PD[:, sl * 512:sl * 512 + 128]
                            ).inc(ptilec[sl], 1).inc(k.vch, 1)
        r.T.wait(k.s_pcc)
        r.T.wait(k.vch)
        for tt in range(8):
            r.T.matmul(k.PC[:, 0:128], KT[:, tt * 128:(tt + 1) * 128],
                       VT[:, tt * 128:(tt + 1) * 128],
                       start=(tt == 0), stop=(tt == 7))
        for tt in range(8):
            r.T.matmul(k.PC[:, 128:129], KT[:, tt * 128:(tt + 1) * 128],
                       k.ONESCB[:, 0:1], start=(tt == 0), stop=(tt == 7))
        r.rlast("tensor").inc(k.s_pcp, 1)
        r.V.wait(k.s_pcp)
        r.V.tensor_copy(k.KVS[:, h * 128:(h + 1) * 128], k.PC[:, 0:128]
                        ).inc(k.vch, 1)
        r.V.tensor_copy(KSUM[:, h:h + 1], k.PC[:, 128:129]
                        ).inc(k.s_pcc, 1).inc(k.vch, 1)

    r.Y.wait(k.vch)
    r.Y.dma_start(out=k.d_l1cc_in.ap()[:, 0:2048], in_=k.KVS).inc(s_cc, 16)
    r.Y.dma_start(out=k.d_l1cc_in.ap()[:, 2048:2064], in_=KSUM).inc(s_cc, 16)
    r.G.wait_val(s_cc, s_cc.v)
    r.G.collective_compute("AllReduce", ALU.add,
                           replica_groups=[[0, 1, 2, 3], [4, 5, 6, 7]],
                           ins=[k.d_l1cc_in.ap()], outs=[k.d_l1cc_out.ap()]
                           ).inc(s_ar, 1)

    r.G.wait_val(s_proj, s_proj.v)
    r.G.dma_start(out=k.BROW2, in_=k.d_l1_bq).inc(s_br, 16)
    DECB = k.SMB[:, 0:1024]
    QTMP = k.SMB[:, 1024:2048]
    r.T.wait(s_dec)
    r.T.wait(k.s_pdc)
    r.T.wait(ptilec[0])
    r.T.wait(ptilec[1])
    for n in range(2):
        r.T.matmul(k.PD[:, n * 512:(n + 1) * 512], k.ONES_F,
                   k.ROWSF[0:1, n * 512:(n + 1) * 512], start=True, stop=True)
    r.rlast("tensor").inc(k.s_pdp, 1)
    r.V.wait(k.s_pdp)
    r.V.wait(s_kv)
    r.V.tensor_copy(DECB, k.PD).inc(k.s_pdc, 1).inc(k.vch, 1)
    for h in range(NH):
        P, sp, sc = (k.PA, k.s_pap, k.s_pac) if h % 2 == 0 else \
                    (k.PB, k.s_pbp, k.s_pbc)
        proj(k.d_l1_wq, k.BROW2, h, P, sp, sc)
        r.S.wait(sp)
        r.S.wait(k.vch)
        r.S.activation(QTMP, P, AF.Relu).inc(sc, 1).inc(k.sch, 1)
        r.V.wait(k.sch)
        r.V.tensor_tensor(k.ACT[:, h * 1024:(h + 1) * 1024], QTMP, DECB,
                          ALU.mult).inc(k.vch, 1)

    KVB = k.SMB[:, 2048:4096]
    KSB = k.ONESCB[:, 2:18]
    r.G.wait_val(s_ar, s_ar.v)
    r.G.dma_start(out=KVB, in_=k.d_l1cc_out.ap()[:, 0:2048]).inc(s_arb, 16)
    r.G.dma_start(out=KSB, in_=k.d_l1cc_out.ap()[:, 2048:2064]).inc(s_arb, 16)

    s_z = r.sem("a1_z")
    ZBB = k.SMB[:, 0:1024]
    for h in range(NH):
        P, sp, sc = (k.PA, k.s_pap, k.s_pac) if h % 2 == 0 else \
                    (k.PB, k.s_pbp, k.s_pbc)
        r.T.wait(sc)
        r.T.wait(s_arb)
        Q0 = k.ACT[:, h * 1024: h * 1024 + 512]
        Q1 = k.ACT[:, h * 1024 + 512: h * 1024 + 1024]
        r.T.wait(k.vch)
        for n, Qn in enumerate((Q0, Q1)):
            r.T.matmul(P[:, n * 512:(n + 1) * 512], KVB[:, h * 128:(h + 1) * 128],
                       Qn, start=True, stop=True)
        r.rlast("tensor").inc(sp, 1)
        r.T.wait(k.s_pcc)
        for n, Qn in enumerate((Q0, Q1)):
            r.T.matmul(k.PC[0:1, n * 512:(n + 1) * 512], KSB[:, h:h + 1],
                       Qn, start=True, stop=True)
        r.rlast("tensor").inc(k.s_pcp, 1)
        r.V.wait(k.s_pcp)
        r.V.tensor_scalar(k.ROWSF, k.PC[0:1, 0:1024], 1e-6, None, ALU.add
                          ).inc(k.s_pcc, 1).inc(k.vch, 1)
        r.V.wait(k.vch)
        r.V.reciprocal(k.ZROW, k.ROWSF).inc(s_z, 1)
        r.T.wait(s_z)
        r.T.wait(k.s_pdc)
        for n in range(2):
            r.T.matmul(k.PD[:, n * 512:(n + 1) * 512], k.ONES_F,
                       k.ZROW[0:1, n * 512:(n + 1) * 512], start=True, stop=True)
        r.rlast("tensor").inc(k.s_pdp, 1)
        r.V.wait(k.s_pdp)
        r.V.tensor_copy(ZBB, k.PD).inc(k.s_pdc, 1).inc(k.vch, 1)
        r.V.wait(k.vch)
        r.V.wait(sp)
        r.V.tensor_tensor(k.ACT[:, 16384 + h * 1024: 16384 + (h + 1) * 1024],
                          P, ZBB, ALU.mult).inc(sc, 1).inc(k.vch, 1)

    s_bo = r.sem("a1_bo")
    r.G.wait_val(s_proj, s_proj.v)
    r.G.wait(k.s_pdp)
    r.G.dma_start(out=k.BROW2, in_=k.d_l1_bo).inc(s_bo, 16)
    for m in range(NH):
        bu = k.ws.load(k.d_l1_wo[m], 2048)
        P, sp, sc = (k.PA, k.s_pap, k.s_pac) if m % 2 == 0 else \
                    (k.PB, k.s_pbp, k.s_pbc)
        bu.acquire(r.T)
        r.T.wait(sc)
        r.T.wait(s_bo)
        if m == 0:
            r.T.wait(k.vch)
        for c in range(NC):
            for n in range(2):
                r.T.matmul(P[:, n * 512:(n + 1) * 512],
                           bu.ap[:, c * 128:(c + 1) * 128],
                           k.ACT[:, 16384 + c * 1024 + n * 512:
                                 16384 + c * 1024 + n * 512 + 512],
                           start=(c == 0), stop=False)
        for n in range(2):
            r.T.matmul(P[:, n * 512:(n + 1) * 512],
                       k.BROW2[0:1, m * 128:(m + 1) * 128],
                       k.ONES_BF, start=False, stop=True)
        i = r.rlast("tensor").inc(sp, 1)
        bu.release(i)
        r.V.wait(sp)
        xsl = k.X[:, m * 1024:(m + 1) * 1024]
        r.V.tensor_tensor(xsl, xsl, P, ALU.add).inc(sc, 1)



_PROG_CACHE = {}


def _get_program(phases, dbg=None):
    key = (tuple(phases), dbg)
    if key not in _PROG_CACHE:
        k = build_program(tuple(phases), dbg=dbg)
        _PROG_CACHE[key] = k
    return _PROG_CACHE[key]


def kernel(**inputs):
    pre = _prep(inputs)
    x = pre["x_cores"]
    for g in GROUPS:
        k = _get_program(g)
        in_maps = []
        for c in range(8):
            m = {"x": x[c]}
            for nm in k.innames:
                if nm in ("decay_pp", "decay_row"):
                    m[nm] = pre[nm][c]
                else:
                    m[nm] = pre[nm]
            in_maps.append(m)
        res = run_bass_kernel_spmd(k.nc, in_maps, core_ids=list(range(8)))
        x = [res.results[c]["y"] for c in range(8)]
    out = np.empty((2, S, D), np.float32)
    for c in range(8):
        out[c // 4, (c % 4) * T:(c % 4 + 1) * T, :] = x[c].T
    return out
